# version 14
# baseline (speedup 1.0000x reference)
import math
import numpy as np
import ml_dtypes
import concourse.bass as bass
import concourse.mybir as mybir
from concourse.bass_utils import run_bass_kernel_spmd

F32 = mybir.dt.float32
BF16 = mybir.dt.bfloat16
I32 = mybir.dt.int32
AF = mybir.ActivationFunctionType
ALU = mybir.AluOpType
AX = mybir.AxisListType.X
BF = ml_dtypes.bfloat16

D = 1024
NG = 8
SL = 512
NCH = 4
CW = SL + 1
NH = 16
NHP = 8
SC = 1.0 / math.sqrt(192.0)
SQ192 = math.sqrt(192.0)
NB = [2, 2, 2, 2, 2, 2, 3, 3, 4, 5, 7, 9, 12, 16, 16, 16]
NBP = [max(NB[2 * i], NB[2 * i + 1]) for i in range(8)]
SLOT_CAP = [4, 8, 12, 16]
ENG_NAMES = ["sync", "gpsimd", "tensor", "scalar", "vector"]


def slope8(h):
    return 8.0 * 2.0 ** (-(h + 1) / 2.0)


def chunks_of(j):
    return [j, 7 - j, 8 + j, 15 - j]


def owner_of(g):
    j = min(g, 7 - g) if g < 8 else min(g - 8, 15 - g)
    return j, chunks_of(j).index(g)


def group_struct():
    gs = []
    for s in range(NCH):
        for hp in range(NHP):
            gs.append((s, hp, min(SLOT_CAP[s], NBP[hp])))
    return gs


GROUPS = group_struct()
NJOBS = sum(n for _, _, n in GROUPS)


def make_jobs(j):
    jobs = []
    ch = chunks_of(j)
    for s, hp, njobs in GROUPS:
        pos = ch[s]
        for jb in range(njobs):
            kch = max(pos - jb, 0)
            r, s2 = owner_of(kch)
            jobs.append(dict(s=s, hp=hp, jb=jb, kch=kch, pad=(jb > pos),
                             rank=r, kslot=s2))
    return jobs


def split3(x):
    a = x.astype(BF)
    r = x - a.astype(np.float32)
    b = r.astype(BF)
    c = (r - b.astype(np.float32)).astype(BF)
    return a, b, c


_SHARED = None


def build_shared_consts():
    global _SHARED
    if _SHARED is not None:
        return _SHARED
    c = {}
    c["ident_f"] = np.eye(128, dtype=np.float32)
    c["ident_b"] = np.eye(128, dtype=BF)
    P = np.zeros((128, 128), dtype=np.float32)
    for m in range(128):
        blk, r = divmod(m, 32)
        P[[1, 0, 3, 2][blk] * 32 + r, m] = 1.0
    c["pswap"] = P.astype(BF)
    mk = np.zeros((128, 4 * SL), dtype=np.float32)
    for kb in range(4):
        for p in range(128):
            kl = kb * 128 + p
            mk[p, kb * SL:(kb + 1) * SL][np.arange(SL) < kl] = -4e9
    c["maskadd"] = mk
    bet = np.zeros((NH, 3, SL), dtype=BF)
    t = np.arange(SL, dtype=np.float32)
    for h in range(NH):
        bet[h, 0], bet[h, 1], bet[h, 2] = split3(slope8(h) * t * SQ192)
    c["beta"] = bet
    inv = 1.0 / (10000.0 ** (np.arange(0, 64, 2, dtype=np.float32) / 64))
    fr = np.outer(np.arange(16 * SL, dtype=np.float32), inv)
    sgn = np.concatenate([-np.ones(32), np.ones(32), -np.ones(32), np.ones(32)])
    c["cos_full"] = np.tile(np.cos(fr).T, (4, 1)).astype(BF)
    c["sin_full"] = (np.tile(np.sin(fr).T, (4, 1)) * sgn[:, None]).astype(BF)
    _SHARED = c
    return c


def build_core_inputs(core, inp, shared):
    b, j = divmod(core, 4)
    ch = chunks_of(j)
    d = {}
    tok = np.asarray(inp["L0_attn_tokens"])
    d["tok"] = np.ascontiguousarray(
        np.stack([tok[b, g * SL:(g + 1) * SL, :] for g in ch], axis=0))
    halo = np.zeros((NCH, D), dtype=np.float32)
    for s, g in enumerate(ch):
        if g > 0:
            halo[s] = tok[b, g * SL - 1, :]
    d["halo"] = halo

    def dmaj(v, ng):
        return np.ascontiguousarray(np.asarray(v).reshape(ng, 128).T)
    d["b_in"] = dmaj(inp["L0_attn_b_in"], 2)
    d["ldec"] = dmaj(inp["L0_attn_log_decay"], 2)
    d["normw"] = dmaj(inp["L0_attn_norm_w"], 8)
    d["bsum"] = np.ascontiguousarray(
        (np.asarray(inp["L0_attn_b_out"]).reshape(8, 128)
         + np.asarray(inp["L0_attn_b_skip"]).reshape(8, 128)).T)
    d["mur"] = dmaj(inp["L0_ffn_mu_r"], 8)
    d["muk"] = dmaj(inp["L0_ffn_mu_k"], 8)
    d["bout1"] = dmaj(inp["L1_attn_b_out"], 8)
    bq = np.asarray(inp["L1_attn_b_qkv"])
    qb = np.zeros((128, 8), np.float32)
    kb_ = np.zeros((128, 8), np.float32)
    vb = np.zeros((8, 128), np.float32)
    for hp in range(8):
        qb[0:64, hp] = bq[2 * hp, 0:64]
        qb[64:128, hp] = bq[2 * hp + 1, 0:64]
        kb_[0:64, hp] = bq[2 * hp, 64:128]
        kb_[64:128, hp] = bq[2 * hp + 1, 64:128]
        vb[hp, 0:64] = bq[2 * hp, 128:192]
        vb[hp, 64:128] = bq[2 * hp + 1, 128:192]
    d["bq_q"], d["bq_k"] = qb, kb_
    d["bq_v"] = np.ascontiguousarray(vb.reshape(1, 8 * 128))
    for nm, key in [("W_in", "L0_attn_W_in"), ("W_out", "L0_attn_W_out"),
                    ("W_skip", "L0_attn_W_skip"), ("W_r", "L0_ffn_W_r"),
                    ("W_k", "L0_ffn_W_k"), ("W_v", "L0_ffn_W_v"),
                    ("wg", "L1_ffn_w_gate"), ("wu", "L1_ffn_w_up"),
                    ("wd", "L1_ffn_w_down")]:
        d[nm] = np.asarray(inp[key])
    d["wqkv"] = np.ascontiguousarray(np.asarray(inp["L1_attn_w_qkv"]).reshape(D, NH * 192))
    d["wout1"] = np.ascontiguousarray(np.asarray(inp["L1_attn_w_out"]).reshape(NH * 64, D))
    for k in ("ident_f", "ident_b", "pswap", "maskadd", "beta"):
        d[k] = shared[k]
    cs = np.zeros((NCH, 128, SL), BF)
    sn = np.zeros((NCH, 128, SL), BF)
    for s, g in enumerate(ch):
        cs[s] = shared["cos_full"][:, g * SL:(g + 1) * SL]
        sn[s] = shared["sin_full"][:, g * SL:(g + 1) * SL]
    d["ropec"], d["ropes"] = cs, sn
    jobs = make_jobs(j)
    jof = np.zeros((128, NJOBS * 4), np.int32)
    tb2 = np.zeros((128, NJOBS * 2), np.float32)
    prow = np.arange(128, dtype=np.int32)
    for idx, jb in enumerate(jobs):
        for h in range(2):
            kr = ((jb["rank"] * NHP + jb["hp"]) * NCH + jb["kslot"]) * 128 + 64 * h
            vr = ((jb["rank"] * NH + 2 * jb["hp"] + h) * NCH + jb["kslot"]) * 128
            jof[:, idx * 4 + h] = kr + prow
            jof[:, idx * 4 + 2 + h] = vr + prow
            tb2[:, idx * 2 + h] = (-1e9 if jb["pad"]
                                   else slope8(2 * jb["hp"] + h) * (jb["kch"] * SL)) - 16.0
    d["jof"], d["tb2"] = jof, tb2
    qaug = np.zeros((NHP, NCH, 2, 3, SL), BF)
    for hp in range(NHP):
        for s in range(NCH):
            q = (ch[s] * SL + np.arange(SL)).astype(np.float32)
            for h in range(2):
                a, bb, cc = split3(-slope8(2 * hp + h) * q * SQ192)
                qaug[hp, s, h, 0], qaug[hp, s, h, 1], qaug[hp, s, h, 2] = a, bb, cc
    d["qaug"] = qaug
    sp = np.zeros((1, 8), np.int32)
    for g in range(2):
        for s in range(NCH):
            sp[0, g * NCH + s] = g * 17 + ch[s]
    d["spick"] = sp
    return d


class Prog:
    def __init__(self, nc, ctx):
        self.nc = nc
        self.ctx = ctx
        self.ops = {n: [] for n in ENG_NAMES}
        self.h = {}
        self.cnt = {}
        self.owner = {}
        self.lastwait = {n: {} for n in ENG_NAMES}
        self.dma_sems = {"sd", "gd"}

    def sem(self, name, owner=None):
        if name not in self.h:
            self.h[name] = self.ctx.enter_context(self.nc.semaphore(name))
            self.cnt[name] = 0
            self.owner[name] = owner
        return self.h[name]

    def emit(self, eng, fn, deps=(), inc=None, incn=1):
        for dep in deps:
            if dep is None:
                continue
            s, v = dep
            if self.owner.get(s) == eng and s not in self.dma_sems:
                continue
            if s in self.dma_sems:
                v = self.cnt[s]
            if self.lastwait[eng].get(s, -1) >= v:
                continue
            self.lastwait[eng][s] = v
            self.ops[eng].append(("w", (s, v)))
        if inc:
            self.cnt[inc] += incn
            self.ops[eng].append(("o", (fn, inc, incn)))
            return (inc, self.cnt[inc])
        self.ops[eng].append(("o", (fn, None, 0)))
        return None

    def barrier(self):
        self.sem("ph", owner=None)
        for n in ENG_NAMES:
            self.cnt["ph"] += 1
            self.ops[n].append(("o", (lambda e: e.nop(), "ph", 1)))
        v = self.cnt["ph"]
        for n in ENG_NAMES:
            self.lastwait[n]["ph"] = v
            self.ops[n].append(("w", ("ph", v)))

    def replay(self, block):
        for name in ENG_NAMES:
            mine = self.ops[name]

            def sec(eng, mine=mine, name=name):
                for oi, (kind, payload) in enumerate(mine):
                    if kind == "w":
                        s, v = payload
                        eng.wait_ge(self.h[s], v)
                    else:
                        fn, inc, incn = payload
                        try:
                            inst = fn(eng)
                        except Exception as ex:
                            raise RuntimeError(f"replay fail {name} op {oi}: {ex}") from ex
                        if inc is not None:
                            inst.then_inc(self.h[inc], incn)
            getattr(block, name)(sec)


def build_program(upto=99, dbg=()):
    from contextlib import ExitStack
    nc = bass.Bass("TRN2", target_bir_lowering=False, detect_race_conditions=False)
    ctx = ExitStack()
    e = ctx.enter_context

    def din(name, shape, dt):
        return nc.dram_tensor(name, list(shape), dt, kind="ExternalInput")

    tok = din("tok", (NCH, SL, D), F32)
    halo = din("halo", (NCH, D), F32)
    b_in = din("b_in", (128, 2), F32)
    ldec = din("ldec", (128, 2), F32)
    normw = din("normw", (128, 8), F32)
    bsum = din("bsum", (128, 8), F32)
    mur = din("mur", (128, 8), F32)
    muk = din("muk", (128, 8), F32)
    bout1 = din("bout1", (128, 8), F32)
    bq_q = din("bq_q", (128, 8), F32)
    bq_k = din("bq_k", (128, 8), F32)
    bq_v = din("bq_v", (1, 8 * 128), F32)
    W_in = din("W_in", (256, D), F32)
    W_out = din("W_out", (D, 256), F32)
    W_skip = din("W_skip", (D, D), F32)
    W_r = din("W_r", (D, D), F32)
    W_k = din("W_k", (D, D), F32)
    W_v = din("W_v", (D, D), F32)
    wqkv = din("wqkv", (D, NH * 192), F32)
    wout1 = din("wout1", (NH * 64, D), F32)
    wg = din("wg", (3072, D), I32)
    wu = din("wu", (3072, D), I32)
    wd = din("wd", (D, 3072), I32)
    ident_f = din("ident_f", (128, 128), F32)
    ident_b = din("ident_b", (128, 128), BF16)
    pswap = din("pswap", (128, 128), BF16)
    maskadd = din("maskadd", (128, 4 * SL), F32)
    beta = din("beta", (NH, 3, SL), BF16)
    ropec = din("ropec", (NCH, 128, SL), BF16)
    ropes = din("ropes", (NCH, 128, SL), BF16)
    jof = din("jof", (128, NJOBS * 4), I32)
    tb2 = din("tb2", (128, NJOBS * 2), F32)
    qaug = din("qaug", (NHP, NCH, 2, 3, SL), BF16)
    spick = din("spick", (1, 8), I32)

    out = nc.dram_tensor("out", [NCH, SL, D], F32, kind="ExternalOutput")
    dbg_t = {}
    for nm, shape, dt in dbg:
        dbg_t[nm] = nc.dram_tensor(nm, list(shape), dt, kind="ExternalOutput")

    wc_l0 = nc.dram_tensor("wc_l0", [4, 128, 8 * 8 * 128], BF16)
    wc_wd = nc.dram_tensor("wc_wd", [2, 128, 4 * 24 * 128], BF16)
    wc_wo = nc.dram_tensor("wc_wo", [2, 128, 4 * 8 * 128], BF16)
    wc_qkv = nc.dram_tensor("wc_qkv", [NHP, 128, 8 * 384], BF16)
    qcache = nc.dram_tensor("qcache", [NHP, NCH, 128, SL], BF16)
    acache = nc.dram_tensor("acache", [NHP, NCH, 128, SL], BF16)
    hcache = nc.dram_tensor("hcache", [NCH, 128, 24 * SL], BF16)
    sum_in = nc.dram_tensor("sum_in", [8, 128], F32)
    sum_out = nc.dram_tensor("sum_out", [32, 128], F32)
    kv_kin = nc.dram_tensor("kv_kin", [NHP * NCH * 128, SL], BF16)
    kv_vin = nc.dram_tensor("kv_vin", [NH * NCH * 128, 260], BF16)
    kv_kall = nc.dram_tensor("kv_kall", [4 * NHP * NCH * 128, SL], BF16)
    kv_vall = nc.dram_tensor("kv_vall", [4 * NH * NCH * 128, 260], BF16)
    bounce = nc.dram_tensor("bounce", [128, 34], F32)

    x_ar = e(nc.sbuf_tensor("x_ar", [128, NG * NCH * CW], F32))
    xn_ar = e(nc.sbuf_tensor("xn_ar", [128, NG * NCH * CW], BF16))
    arA = e(nc.sbuf_tensor("arA", [128, NG * CW], BF16))
    arB = e(nc.sbuf_tensor("arB", [128, NG * CW], BF16))
    arC = e(nc.sbuf_tensor("arC", [128, 6144], BF16))
    st_ar = e(nc.sbuf_tensor("st_ar", [128, 2 * NCH * SL], F32))
    row_ar = e(nc.sbuf_tensor("row_ar", [128, 2 * CW], F32))
    scr_f = e(nc.sbuf_tensor("scr_f", [128, 2 * CW], F32))
    stg = e(nc.sbuf_tensor("stg", [128, 3072], F32))
    wbufB = e(nc.sbuf_tensor("wbufB", [128, 3072], BF16))
    vec_sb = e(nc.sbuf_tensor("vec_sb", [128, 66], F32))
    vrow_b = e(nc.sbuf_tensor("vrow_b", [1, 8 * 128], BF16))
    ident_fs = e(nc.sbuf_tensor("ident_fs", [128, 128], F32))
    ident_bs = e(nc.sbuf_tensor("ident_bs", [128, 128], BF16))
    pswap_s = e(nc.sbuf_tensor("pswap_s", [128, 128], BF16))
    sh8 = e(nc.sbuf_tensor("sh8", [128, 2048], F32))
    iotab = e(nc.sbuf_tensor("iotab", [128, SL], F32))
    onesb = e(nc.sbuf_tensor("onesb", [128, 1], BF16))
    ones_f = e(nc.sbuf_tensor("ones_f", [65, 128], F32))
    ones_bb = e(nc.sbuf_tensor("ones_bb", [1, 128], BF16))
    sum_sb = e(nc.sbuf_tensor("sum_sb", [128, 72], F32))
    spick_s = e(nc.sbuf_tensor("spick_s", [1, 8], I32))
    jot = e(nc.sbuf_tensor("jot", [128, NJOBS * 4], I32))
    tbb = e(nc.sbuf_tensor("tbb", [128, NJOBS * 2], F32))
    sin_sb = e(nc.sbuf_tensor("sin_sb", [128, 8], F32))
    w_in_sb = e(nc.sbuf_tensor("w_in_sb", [128, 8 * 256], BF16))
    w_out_sb = e(nc.sbuf_tensor("w_out_sb", [128, 2 * 1024], BF16))
    kbuf = [e(nc.sbuf_tensor(f"kbuf{i}", [128, SL], BF16)) for i in range(2)]
    vbuf = [e(nc.sbuf_tensor(f"vbuf{i}", [128, 4 * 65], BF16)) for i in range(2)]
    qbuf = e(nc.sbuf_tensor("qbuf", [128, SL], BF16))
    qbuf2 = e(nc.sbuf_tensor("qbuf2", [128, SL], BF16))
    pbuf = [e(nc.sbuf_tensor(f"pbuf{i}", [128, 4 * SL], BF16)) for i in range(2)]
    epsc = e(nc.sbuf_tensor("epsc", [128, 2], F32))

    p_st = e(nc.psum_tensor("p_st", [128, 4 * SL], F32))
    p_a0 = e(nc.psum_tensor("p_a0", [128, SL], F32))
    p_a1 = e(nc.psum_tensor("p_a1", [128, SL], F32))
    p_m0 = e(nc.psum_tensor("p_m0", [128, SL], F32))
    p_m1 = e(nc.psum_tensor("p_m1", [128, SL], F32))

    P = Prog(nc, ctx)
    for s_, o_ in [("sd", "sync"), ("gd", "gpsimd"), ("pe", "tensor"),
                   ("se", "scalar"), ("ve", "vector"), ("gc", "gpsimd"),
                   ("cc", None)]:
        P.sem(s_, owner=o_)

    rS = nc.gpsimd.alloc_register("rS")

    def xsl(g, c, c0, w):
        return x_ar[:, (g * NCH + c) * CW + c0:(g * NCH + c) * CW + c0 + w]

    def xnsl(g, c, c0, w):
        return xn_ar[:, (g * NCH + c) * CW + c0:(g * NCH + c) * CW + c0 + w]

    def dma(q, o, i, deps=(), slow=False, dbl=False, sem=None):
        kw = dict(allow_slow_non_contiguous=True) if slow else {}
        s = sem or ("sd" if q == "sync" else "gd")
        if s not in P.h:
            P.sem(s, owner=q)
            P.dma_sems.add(s)
        m = P.emit(q, lambda e_, o=o, i=i, kw=kw: e_.dma_start(out=o, in_=i, **kw),
                   deps=deps, inc=s, incn=16)
        if dbl:
            m = P.emit(q, lambda e_, o=o, i=i, kw=kw: e_.dma_start(out=o, in_=i, **kw),
                       inc=s, incn=16)
        return m

    def mm(pp, lhsT, rhs, start, stop, deps=(), inc=False):
        return P.emit("tensor", lambda e_, pp=pp, l=lhsT, r=rhs, s1=start, s2=stop:
                      e_.matmul(pp, l, r, start=s1, stop=s2), deps=deps,
                      inc=("pe" if inc else None))

    def act(out_, in_, f, deps=(), bias=None, scale=None, inc=True, settle=False):
        kw = {}
        if bias is not None:
            kw["bias"] = bias
        if scale is not None:
            kw["scale"] = scale
        m = P.emit("scalar", lambda e_, o=out_, i=in_, f=f, kw=kw:
                   e_.activation(o, i, f, **kw), deps=deps,
                   inc=("se" if (inc or settle) else None))
        return m

    def vop(fn, deps=(), inc=True, settle=False):
        m = P.emit("vector", fn, deps=deps,
                   inc=("ve" if (inc or settle) else None))
        return m

    def trans(out_, in_, ident, deps=()):
        return P.emit("tensor", lambda e_, o=out_, i=in_, d=ident:
                      e_.transpose(o, i, d), deps=deps, inc="pe")

    def pesettle():
        return P.emit("tensor", lambda e_: e_.nop(), inc="pe")

    m_vec = None
    for src, dst in [(b_in, vec_sb[:, 0:2]), (ldec, vec_sb[:, 2:4]),
                     (normw, vec_sb[:, 4:12]), (bsum, vec_sb[:, 12:20]),
                     (mur, vec_sb[:, 20:28]), (muk, vec_sb[:, 28:36]),
                     (bq_q, vec_sb[:, 36:44]), (bq_k, vec_sb[:, 44:52]),
                     (bout1, vec_sb[:, 52:60]),
                     (ident_f, ident_fs[:, :]), (ident_b, ident_bs[:, :]),
                     (pswap, pswap_s[:, :]), (spick, spick_s[:, :])]:
        m_vec = dma("sync", dst, src.ap())
    for c in range(NCH):
        for g in range(NG):
            dma("sync", xsl(g, c, 0, 1),
                halo.ap()[c:c + 1, g * 128:(g + 1) * 128].rearrange("a p -> p a"),
                slow=True)
    m_mem = None
    for t_, v_ in [(onesb[:, :], 1.0), (ones_f[:, :], 1.0), (ones_bb[:, :], 1.0),
                   (qbuf[64:128, :], 0.0), (qbuf2[64:128, :], 0.0),
                   (qbuf[64:67, :], 1.0), (qbuf2[64:67, :], 1.0),
                   (kbuf[0][64:96, :], 1.0), (kbuf[1][64:96, :], 1.0),
                   (kbuf[0][96:128, :], 0.0), (kbuf[1][96:128, :], 0.0),
                   (sum_sb[:, :], 0.0), (epsc[:, 0:1], 1e-5), (epsc[:, 1:2], 1e-4)]:
        m_mem = P.emit("gpsimd", lambda e_, t=t_, v=v_: e_.memset(t, v), inc="gc")
    m_iota = P.emit("gpsimd", lambda e_: e_.iota(
        iotab[:, :], pattern=[[1, SL]], base=1, channel_multiplier=0,
        allow_small_or_imprecise_dtypes=True), inc="gc")
    dma("gpsimd", bass.AP(bounce, 0, [[34, 128], [1, 1]]), sum_sb[:, 0:1], deps=[m_mem], slow=True)
    dma("gpsimd", bass.AP(bounce, 17, [[34, 128], [1, 1]]), sum_sb[:, 0:1], slow=True)
    m_dec = act(vec_sb[:, 60:62], vec_sb[:, 2:4], AF.Sigmoid, deps=[m_vec],
                settle=True)
    m_L = act(vec_sb[:, 62:64], vec_sb[:, 60:62], AF.Ln)
    m_l = dma("sync", scr_f[0:1, 0:1024], bq_v.ap())
    m_bqv = act(vrow_b[:, :], scr_f[0:1, 0:1024], AF.Copy, deps=[m_l], settle=True)

    if upto <= 0:
        ctx_close_out(P, nc, ctx, out, x_ar, xsl, dbg_t, dma, vop, act, locals())
        return nc

    P.barrier()
    ing_ev = {}
    for it in range(NCH * 4):
        c, tb = divmod(it, 4)
        half = it % 2
        m_l = dma(["sync", "gpsimd"][half], stg[:, half * 1024:half * 1024 + 1024],
                  tok.ap()[c, tb * 128:(tb + 1) * 128, :],
                  deps=[ing_ev.get(it - 2)], sem=f"il{half}")
        m_tr = None
        for g in range(NG):
            m_tr = trans(p_st[:, half * 1024 + g * 128:half * 1024 + (g + 1) * 128],
                         stg[:, half * 1024 + g * 128:half * 1024 + (g + 1) * 128],
                         ident_fs[:, :], deps=[m_l, ing_ev.get(it - 2)])
        m_tr = pesettle()
        evs = None
        for g in range(NG):
            evs = vop(lambda e_, g=g, c=c, tb=tb, half=half: e_.tensor_copy(
                xsl(g, c, 1 + tb * 128, 128),
                p_st[:, half * 1024 + g * 128:half * 1024 + (g + 1) * 128]),
                deps=[m_tr], inc=(g == NG - 1))
        ing_ev[it] = evs
    if upto <= 1:
        ctx_close_out(P, nc, ctx, out, x_ar, xsl, dbg_t, dma, vop, act, locals())
        return nc

    P.barrier()

    prep_chain = [None]
    prep_chain_cv = [None]
    prep_chain_out = [(None, None)]

    PREP_SLOTS = [(stg, 0), (stg, 1024), (stg, 2048),
                  (sh8, 0), (sh8, 1024), (scr_f, 0)]
    PREP_QS = ["sync", "gpsimd"]

    def prep_tiles(n4, src_fn, dst_fn, out_fn=None, int_scale=None, ntr=4,
                   load_q=None, load_sem=None):
        prev_cv = {k: prep_chain_cv[0] for k in range(-6, 0)}
        prev_ev = {-1: prep_chain[0], -2: prep_chain[0]}
        prev_out = {-1: prep_chain_out[0][1] or prep_chain[0],
                    -2: prep_chain_out[0][0] or prep_chain[0]}
        for i4 in range(n4):
            w_ = 128 * ntr
            half = i4 % 2
            st_t, st_c = PREP_SLOTS[i4 % 6]
            stg_v = (st_t[:, :].bitcast(I32)[:, st_c:st_c + w_]
                     if int_scale else st_t[:, st_c:st_c + w_])
            m_ld = dma(PREP_QS[i4 % 2], stg_v, src_fn(i4),
                       deps=[prev_cv.get(i4 - 6)], sem=f"pl{i4 % 6}")
            m_cv = act(arA[:, half * 2052:half * 2052 + w_], stg_v, AF.Copy,
                       deps=[m_ld, prev_ev.get(i4 - 2)], scale=int_scale,
                       settle=True)
            prev_cv[i4] = m_cv
            def ptile(q, i4=i4):
                if i4 % 2 == 0:
                    return p_st[:, :].bitcast(BF16)[:, q * 1024:q * 1024 + 128]
                pb = p_m0 if q < 2 else p_m1
                return pb[:, :].bitcast(BF16)[:, (q % 2) * 128:(q % 2) * 128 + 128]
            m_tr = None
            for q in range(ntr):
                m_tr = trans(
                    ptile(q),
                    arA[:, half * 2052 + q * 128:half * 2052 + (q + 1) * 128],
                    ident_bs[:, :], deps=[m_cv, prev_ev.get(i4 - 2)])
            m_tr = pesettle()
            m_ev = None
            for q in range(ntr):
                last = (q == ntr - 1)
                m_ev = vop(lambda e_, q=q, i4=i4: e_.tensor_copy(
                    dst_fn(i4, q), ptile(q, i4)),
                    deps=[m_tr, prev_out.get(i4 - 2)] if q == 0 else [m_tr],
                    inc=last, settle=last)
            prev_ev[i4] = m_ev
            prev_out[i4] = out_fn(i4, m_ev) if out_fn else m_ev
        prep_chain[0] = prev_ev[n4 - 1]
        prep_chain_cv[0] = prev_cv[n4 - 1]
        if out_fn:
            prep_chain_out[0] = (prev_out.get(n4 - 2), prev_out.get(n4 - 1))
        return prev_out

    l0_srcs = [W_r, W_k, W_v, W_skip]
    l0_outs = prep_tiles(
        64,
        lambda i4: l0_srcs[i4 // 16].ap()[((i4 // 2) % 8) * 128:((i4 // 2) % 8 + 1) * 128,
                                          (i4 % 2) * 512:(i4 % 2 + 1) * 512],
        lambda i4, q: arB[:, (i4 % 2) * 2052 + q * 128:(i4 % 2) * 2052 + (q + 1) * 128],
        out_fn=lambda i4, mev: dma(
            "gpsimd",
            wc_l0.ap()[i4 // 16, :, ((i4 // 2) % 8) * 1024 + (i4 % 2) * 512:
                       ((i4 // 2) % 8) * 1024 + (i4 % 2) * 512 + 512],
            arB[:, (i4 % 2) * 2052:(i4 % 2) * 2052 + 512], deps=[mev],
            sem=f"pc{i4 % 2}"))
    prep_tiles(
        4,
        lambda i4: W_in.ap()[(i4 // 2) * 128:(i4 // 2 + 1) * 128,
                             (i4 % 2) * 512:(i4 % 2 + 1) * 512],
        lambda i4, q: w_in_sb[:, ((i4 % 2) * 4 + q) * 256 + (i4 // 2) * 128:
                              ((i4 % 2) * 4 + q) * 256 + (i4 // 2 + 1) * 128])
    prep_tiles(
        8,
        lambda i4: W_out.ap()[i4 * 128:(i4 + 1) * 128, 0:256],
        lambda i4, q: w_out_sb[:, q * 1024 + i4 * 128:q * 1024 + (i4 + 1) * 128],
        ntr=2)
    wd_outs = prep_tiles(
        48,
        lambda i4: wd.ap()[(i4 // 6) * 128:(i4 // 6 + 1) * 128,
                           (i4 % 6) * 512:(i4 % 6 + 1) * 512],
        lambda i4, q: arB[:, (i4 % 2) * 2052 + q * 128:(i4 % 2) * 2052 + (q + 1) * 128],
        out_fn=lambda i4, mev: dma(
            "scalar",
            wc_wd.ap()[(i4 // 6) // 4, :, ((i4 // 6) % 4) * 3072 + (i4 % 6) * 512:
                       ((i4 // 6) % 4) * 3072 + (i4 % 6) * 512 + 512],
            arB[:, (i4 % 2) * 2052:(i4 % 2) * 2052 + 512], deps=[mev],
            sem=f"pc{2 + i4 % 2}"),
        int_scale=1.0 / 127.0, load_q="gpsimd", load_sem="pl2")
    prevo = {}
    prevo_cv = {}
    for m in range(8):
        half = m % 2
        m_ld = None
        for a in range(8):
            m_ld = dma("sync", stg[:, half * 1024 + a * 128:half * 1024 + (a + 1) * 128],
                       wout1.ap()[a * 128:(a + 1) * 128, m * 128:(m + 1) * 128],
                       deps=([prevo.get(m - 2), prep_chain_cv[0]] if a == 0 else ()),
                       sem="pl")
        m_cv = act(arA[:, half * 2052:half * 2052 + 1024],
                   stg[:, half * 1024:half * 1024 + 1024], AF.Copy,
                   deps=[m_ld, prep_chain[0] if m < 2 else None], settle=True)
        prevo_cv[m] = m_cv
        prevo[m] = dma("gpsimd", wc_wo.ap()[m // 4, :, (m % 4) * 1024:(m % 4) * 1024 + 1024],
                       arA[:, half * 2052:half * 2052 + 1024], deps=[m_cv],
                       sem="pw")
    prevq = {}
    prevsh = {}
    for dc in range(8):
        m_ld2 = dma("gpsimd", stg[:, 1536:3072],
                    wqkv.ap()[dc * 128:(dc + 1) * 128, 1536:3072],
                    sem="pl4", deps=[prevsh.get(dc - 1),
                          (prevo_cv[7] if dc == 0 else None),
                          (prevo_cv[6] if dc == 0 else None)])
        m_ld = dma("scalar", stg[:, 0:1536], wqkv.ap()[dc * 128:(dc + 1) * 128, 0:1536],
                   sem="pl3", deps=[prevsh.get(dc - 1),
                         (prevo_cv[7] if dc == 0 else None),
                         (prevo_cv[6] if dc == 0 else None)])
        m_sh = None
        for kind in range(3):
            m_sh = vop(lambda e_, kind=kind: e_.tensor_copy(
                bass.AP(arB, kind * 128, [[4104, 128], [384, 8], [64, 2], [1, 64]]),
                bass.AP(stg, kind * 64, [[3072, 128], [384, 8], [192, 2], [1, 64]])),
                deps=([m_ld, m_ld2, prevq.get(dc - 1),
                       (prevo[7] if dc == 0 else None),
                       (prevo[6] if dc == 0 else None),
                       (wd_outs.get(46) if dc == 0 else None),
                       (wd_outs.get(47) if dc == 0 else None)]
                      if kind == 0 else [m_ld, m_ld2]),
                inc=(kind == 2), settle=(kind == 2))
        prevsh[dc] = m_sh
        m_o = None
        for hp in range(8):
            m_o = dma("gpsimd", wc_qkv.ap()[hp, :, dc * 384:(dc + 1) * 384],
                      arB[:, hp * 384:(hp + 1) * 384],
                      deps=[m_sh] if hp == 0 else (), sem="pw")
        prevq[dc] = m_o
    m_prepq = prevq[7]
    if upto <= 2:
        ctx_close_out(P, nc, ctx, out, x_ar, xsl, dbg_t, dma, vop, act, locals())
        return nc

    P.barrier()

    def emit_layernorm():
        outs = []
        pfree = None
        for c in range(NCH):
            m_cv = None
            for g in range(NG):
                m_cv = act(arA[:, g * CW:(g + 1) * CW], xsl(g, c, 0, CW), AF.Copy,
                           inc=(g == NG - 1))
            m_sq = None
            for g in range(NG):
                m_sq = act(arB[:, g * CW:(g + 1) * CW], arA[:, g * CW:(g + 1) * CW],
                           AF.Square, inc=(g == NG - 1))
            m_s1 = None
            for w0, w1 in [(0, 512), (512, 513)]:
                for g in range(NG):
                    m_s1 = mm(p_st[0:1, w0:w1], onesb[:, :],
                              arA[:, g * CW + w0:g * CW + w1],
                              g == 0, g == NG - 1, deps=[m_cv, pfree],
                              inc=(g == NG - 1 and w0 == 512))
            m_s2 = None
            for w0, w1 in [(0, 512), (512, 513)]:
                for g in range(NG):
                    m_s2 = mm(p_st[0:1, 1024 + w0:1024 + w1], onesb[:, :],
                              arB[:, g * CW + w0:g * CW + w1],
                              g == 0, g == NG - 1, deps=[m_sq, pfree],
                              inc=(g == NG - 1 and w0 == 512))
            m_mean = vop(lambda e_: e_.tensor_scalar(
                row_ar[0:1, 0:CW], p_st[0:1, 0:CW], 1.0 / 1024.0, None, ALU.mult),
                deps=[m_s1])
            m_e2 = vop(lambda e_: e_.tensor_scalar(
                row_ar[64:65, 0:CW], p_st[0:1, 1024:1024 + CW], 1.0 / 1024.0, None,
                ALU.mult), deps=[m_s2])
            vop(lambda e_: e_.tensor_tensor(
                row_ar[64:65, CW:2 * CW], row_ar[0:1, 0:CW], row_ar[0:1, 0:CW],
                ALU.mult), inc=False)
            m_var = vop(lambda e_: e_.tensor_tensor(
                scr_f[0:1, 0:CW], row_ar[64:65, 0:CW], row_ar[64:65, CW:2 * CW],
                ALU.subtract))
            m_ln = act(scr_f[0:1, CW:2 * CW], scr_f[0:1, 0:CW], AF.Ln,
                       deps=[m_var], bias=epsc[0:1, 0:1])
            m_rstd = act(row_ar[0:1, CW:2 * CW], scr_f[0:1, CW:2 * CW], AF.Exp,
                         scale=-0.5, settle=True)
            m_bm = None
            for w0, w1 in [(0, 512), (512, 513)]:
                m_bm = mm(p_st[:, w0:w1], ones_f[0:1, :], row_ar[0:1, w0:w1], True, True,
                          deps=[m_mean, m_e2], inc=(w0 == 512))
            m_br = None
            for w0, w1 in [(0, 512), (512, 513)]:
                m_br = mm(p_st[:, 1024 + w0:1024 + w1], ones_f[0:1, :],
                          row_ar[0:1, CW + w0:CW + w1], True, True,
                          deps=[m_rstd], inc=(w0 == 512))
            m_xn = None
            for g in range(NG):
                vop(lambda e_, g=g, c=c: e_.tensor_tensor(
                    scr_f[:, 0:CW], xsl(g, c, 0, CW), p_st[:, 0:CW], ALU.subtract),
                    deps=[m_bm, m_br], inc=False)
                m_xn = vop(lambda e_, g=g, c=c: e_.tensor_tensor(
                    xnsl(g, c, 0, CW), scr_f[:, 0:CW], p_st[:, 1024:1024 + CW],
                    ALU.mult), inc=(g == NG - 1))
            pfree = m_xn
            outs.append(m_xn)
        return outs

    xn_ms = emit_layernorm()
    if upto <= 3:
        ctx_close_out(P, nc, ctx, out, x_ar, xsl, dbg_t, dma, vop, act, locals())
        return nc

    P.barrier()
    m_dt = []
    for n2 in range(2):
        m_dt.append(act(sh8[:, 1024 + n2 * SL:1024 + (n2 + 1) * SL], iotab[:, :],
                        AF.Exp, deps=[m_iota, m_L], scale=vec_sb[:, 62 + n2:63 + n2],
                        settle=True))
    act(sum_sb[:, 64:66], vec_sb[:, 62:64], AF.Copy, deps=[m_L], scale=512.0)
    m_d512 = act(sum_sb[:, 64:66], sum_sb[:, 64:66], AF.Exp, settle=True)

    m_scan = {}
    scr_guard = [None]
    fr = dict(a0=None, a1=None, m0=None, m1=None)
    wload = {}
    wfree = {}
    nload = [0]

    def wload_next(src_ap, deps=()):
        i = nload[0]
        nload[0] += 1
        sl_ = i % 3
        extra = []
        if i < 3:
            extra = [("pc0", P.cnt.get("pc0", 0)), ("pc1", P.cnt.get("pc1", 0))]
        wload[i] = dma(["sync", "gpsimd", "sync"][sl_],
                       wbufB[:, sl_ * 1024:(sl_ + 1) * 1024], src_ap,
                       sem=f"ws{sl_}", deps=list(deps) + [wfree.get(i - 3)] + extra)
        return i

    def wtile(i, k):
        sl_ = i % 3
        return wbufB[:, sl_ * 1024 + k * 128:sl_ * 1024 + (k + 1) * 128]

    for c in range(NCH):
        m_sq = None
        for g in range(NG):
            m_sq = act(arB[:, g * SL:(g + 1) * SL], xnsl(g, c, 1, SL), AF.Square,
                       deps=[xn_ms[c]] if g == 0 else (), inc=(g == NG - 1))
        m_ms = None
        for g in range(NG):
            m_ms = mm(p_a0[0:1, 0:SL], onesb[:, :], arB[:, g * SL:(g + 1) * SL],
                      g == 0, g == NG - 1, deps=[m_sq, fr["a0"]], inc=(g == NG - 1))
        m_msr = vop(lambda e_: e_.tensor_scalar(
            row_ar[64:65, 0:SL], p_a0[0:1, 0:SL], 1.0 / 1024.0, None, ALU.mult),
            deps=[m_ms])
        fr["a0"] = m_msr
        act(row_ar[64:65, CW:CW + SL], row_ar[64:65, 0:SL], AF.Ln,
            deps=[m_msr], bias=epsc[64:65, 1:2])
        m_rs2 = act(row_ar[64:65, 0:SL], row_ar[64:65, CW:CW + SL], AF.Exp,
                    scale=-0.5, settle=True)
        m_b2 = mm(p_m1[:, 0:SL], ones_f[64:65, :], row_ar[64:65, 0:SL], True, True,
                  deps=[m_rs2, fr["m1"]], inc=True)
        m_xh = None
        for g in range(NG):
            m_xh = vop(lambda e_, g=g, c=c: e_.scalar_tensor_tensor(
                arA[:, g * SL:(g + 1) * SL], xnsl(g, c, 1, SL), vec_sb[:, 4 + g:5 + g],
                p_m1[:, 0:SL], ALU.mult, ALU.mult), deps=[m_b2],
                inc=(g == NG - 1), settle=(g == NG - 1))
        fr["m1"] = m_xh
        m_u = []
        for n2 in range(2):
            pp = p_a0 if n2 == 0 else p_a1
            mmx = None
            for g in range(NG):
                mmx = mm(pp[:, 0:SL], w_in_sb[:, g * 256 + n2 * 128:g * 256 + (n2 + 1) * 128],
                         arA[:, g * SL:(g + 1) * SL], g == 0, g == NG - 1,
                         deps=[m_xh, fr["a0"] if n2 == 0 else fr["a1"],
                               m_scan.get((c - 1, n2))], inc=(g == NG - 1))
            m_u.append(mmx)
        m_ue = []
        for n2 in range(2):
            pp = p_a0 if n2 == 0 else p_a1
            m_ue.append(vop(lambda e_, n2=n2, pp=pp: e_.tensor_scalar(
                scr_f[:, n2 * CW:n2 * CW + SL], pp[:, 0:SL],
                vec_sb[:, 0 + n2:1 + n2], None, ALU.add),
                deps=[m_u[n2], scr_guard[0]]))
            fr["a0" if n2 == 0 else "a1"] = m_ue[-1]
        if c == 0 and "dbg_u" in dbg_t:
            dma("sync", dbg_t["dbg_u"].ap(), scr_f[:, 0:2 * CW],
                deps=[m_ue[0], m_ue[1]])
        for n2 in range(2):
            dcol = vec_sb[:, 60 + n2:61 + n2]
            d0 = bass.AP(dcol.tensor, dcol.offset,
                         [[list(dcol.ap)[0][0], 128], [0, SL]])
            m_scan[(c, n2)] = vop(lambda e_, n2=n2, c=c, d0=d0: e_.tensor_tensor_scan(
                st_ar[:, (n2 * NCH + c) * SL:(n2 * NCH + c + 1) * SL], d0,
                scr_f[:, n2 * CW:n2 * CW + SL], 0.0, ALU.mult, ALU.add),
                deps=[m_ue[n2], m_dec])
        if c == NCH - 1:
            m_smc = None
            for n2 in range(2):
                for cc_ in range(NCH):
                    m_smc = vop(lambda e_, n2=n2, cc_=cc_: e_.tensor_copy(
                        sum_sb[:, cc_ * 2 + n2:cc_ * 2 + n2 + 1],
                        st_ar[:, (n2 * NCH + cc_ + 1) * SL - 1:(n2 * NCH + cc_ + 1) * SL]),
                        deps=[m_scan[(cc_, n2)]], inc=(n2 == 1 and cc_ == NCH - 1),
                        settle=(n2 == 1 and cc_ == NCH - 1))
            m_str = trans(p_m1[0:8, 0:128], sum_sb[:, 0:8], ident_fs[:, :],
                          deps=[m_smc, fr["m1"]])
            m_str = pesettle()
            m_sev = vop(lambda e_: e_.tensor_copy(sh8[0:8, 0:128], p_m1[0:8, 0:128]),
                        deps=[m_str], settle=True)
            fr["m1"] = m_sev
            m_sdma = dma("gpsimd", sum_in.ap(), sh8[0:8, 0:128], deps=[m_sev])
            m_sdma = dma("gpsimd", sum_in.ap(), sh8[0:8, 0:128], deps=[m_sdma])
            m_cc1 = P.emit("gpsimd", lambda e_: e_.collective_compute(
                "AllGather", ALU.bypass, replica_groups=[[0, 1, 2, 3], [4, 5, 6, 7]],
                ins=[sum_in.ap().opt()], outs=[sum_out.ap().opt()]),
                deps=[m_sdma], inc="cc")
            m_cc1 = P.emit("gpsimd", lambda e_: e_.collective_compute(
                "AllGather", ALU.bypass, replica_groups=[[0, 1, 2, 3], [4, 5, 6, 7]],
                ins=[sum_in.ap().opt()], outs=[sum_out.ap().opt()]),
                inc="cc")
        m_add = {}
        for m in range(8):
            i = wload_next(wc_l0.ap()[3, :, m * 1024:(m + 1) * 1024])
            pp = p_m0 if m % 2 == 0 else p_a0
            mmx = None
            for k in range(8):
                mmx = mm(pp[:, 0:SL], wtile(i, k), arA[:, k * SL:(k + 1) * SL],
                         k == 0, k == 7,
                         deps=[wload[i], m_xh, fr["m0"] if m % 2 == 0 else fr["a0"],
                               m_scan[(c, 0)] if m % 2 == 1 else None],
                         inc=(k == 7))
            wfree[i] = mmx
            m_add[m] = vop(lambda e_, m=m, c=c, pp=pp: e_.scalar_tensor_tensor(
                xsl(m, c, 1, SL), pp[:, 0:SL], vec_sb[:, 12 + m:13 + m],
                xsl(m, c, 1, SL), ALU.add, ALU.add), deps=[mmx])
            fr["m0" if m % 2 == 0 else "a0"] = m_add[m]
        for g in range(NG):
            vop(lambda e_, g=g, c=c: e_.tensor_tensor(
                xsl(g, c, 1, SL), xsl(g, c, 1, SL), xnsl(g, c, 1, SL), ALU.add),
                deps=[m_add[g]], inc=(g == NG - 1))
        m_mix = None
        for g in range(NG):
            vop(lambda e_, g=g, c=c: e_.tensor_tensor(
                arC[:, 0:SL], xnsl(g, c, 1, SL), xnsl(g, c, 0, SL), ALU.subtract),
                deps=[m_add[7], m_u[0], m_u[1]] if g == 0 else (), inc=False)
            vop(lambda e_, g=g, c=c: e_.scalar_tensor_tensor(
                arA[:, g * SL:(g + 1) * SL], arC[:, 0:SL], vec_sb[:, 20 + g:21 + g],
                xnsl(g, c, 0, SL), ALU.mult, ALU.add), inc=False)
            m_mix = vop(lambda e_, g=g, c=c: e_.scalar_tensor_tensor(
                arB[:, g * SL:(g + 1) * SL], arC[:, 0:SL], vec_sb[:, 28 + g:29 + g],
                xnsl(g, c, 0, SL), ALU.mult, ALU.add), inc=(g == NG - 1),
                settle=(g == NG - 1))
        m_r = {}
        for m in range(8):
            i = wload_next(wc_l0.ap()[0, :, m * 1024:(m + 1) * 1024])
            pp = p_m0 if m % 2 == 0 else p_a0
            mmx = None
            for k in range(8):
                mmx = mm(pp[:, 0:SL], wtile(i, k), arA[:, k * SL:(k + 1) * SL],
                         k == 0, k == 7,
                         deps=[wload[i], m_mix, fr["m0"] if m % 2 == 0 else fr["a0"]],
                         inc=(k == 7))
            wfree[i] = mmx
            m_r[m] = act(arC[:, 512 + m * SL:512 + (m + 1) * SL], pp[:, 0:SL],
                         AF.Sigmoid, deps=[mmx], settle=True)
            fr["m0" if m % 2 == 0 else "a0"] = m_r[m]
        m_k2 = {}
        for m in range(8):
            i = wload_next(wc_l0.ap()[1, :, m * 1024:(m + 1) * 1024])
            pp = p_m1 if m % 2 == 0 else p_a1
            mmx = None
            for k in range(8):
                mmx = mm(pp[:, 0:SL], wtile(i, k), arB[:, k * SL:(k + 1) * SL],
                         k == 0, k == 7,
                         deps=[wload[i], m_mix, m_r[7] if m <= 1 else None,
                               fr["m1"] if m % 2 == 0 else fr["a1"]],
                         inc=(k == 7))
            wfree[i] = mmx
            m_rl = vop(lambda e_, m=m, pp=pp: e_.tensor_scalar(
                arC[:, 4608 + (m % 2) * SL:4608 + (m % 2 + 1) * SL], pp[:, 0:SL],
                0.0, None, ALU.max), deps=[mmx, m_k2.get(m - 2)], settle=True)
            fr["m1" if m % 2 == 0 else "a1"] = m_rl
            m_k2[m] = act(arA[:, m * SL:(m + 1) * SL],
                          arC[:, 4608 + (m % 2) * SL:4608 + (m % 2 + 1) * SL],
                          AF.Square, deps=[m_rl], settle=True)
        for m in range(8):
            i = wload_next(wc_l0.ap()[2, :, m * 1024:(m + 1) * 1024])
            pp = p_m0 if m % 2 == 0 else p_a0
            mmx = None
            for k in range(8):
                mmx = mm(pp[:, 0:SL], wtile(i, k), arA[:, k * SL:(k + 1) * SL],
                         k == 0, k == 7,
                         deps=[wload[i], m_k2[7], fr["m0"] if m % 2 == 0 else fr["a0"]],
                         inc=(k == 7))
            wfree[i] = mmx
            m_t = vop(lambda e_, m=m, pp=pp: e_.tensor_tensor(
                scr_f[:, (m % 2) * CW:(m % 2) * CW + SL], pp[:, 0:SL],
                arC[:, 512 + m * SL:512 + (m + 1) * SL], ALU.mult), deps=[mmx])
            fr["m0" if m % 2 == 0 else "a0"] = m_t
            m_xvadd = vop(lambda e_, m=m, c=c: e_.tensor_tensor(
                xsl(m, c, 1, SL), xsl(m, c, 1, SL),
                scr_f[:, (m % 2) * CW:(m % 2) * CW + SL], ALU.add))
        scr_guard[0] = m_xvadd
    if upto <= 4:
        ctx_close_out(P, nc, ctx, out, x_ar, xsl, dbg_t, dma, vop, act, locals())
        return nc

    P.barrier()
    m_so = dma("sync", scr_f[0:32, 0:128], sum_out.ap(),
               deps=[m_cc1, scr_guard[0], m_sdma])
    m_str2 = trans(p_m0[:, 0:32], scr_f[0:32, 0:128], ident_fs[0:32, 0:32],
                   deps=[m_so, fr["m0"]])
    m_str2 = pesettle()
    m_sev2 = vop(lambda e_: e_.tensor_copy(sum_sb[:, 32:64], p_m0[:, 0:32]),
                 deps=[m_str2])
    fr["m0"] = m_sev2
    m_ro = None
    for g2 in range(2):
        for r in range(4):
            for s2 in range(NCH):
                gch = chunks_of(r)[s2]
                src = 32 + r * 8 + s2 * 2 + g2
                m_ro = vop(lambda e_, g2=g2, gch=gch, src=src: e_.tensor_copy(
                    sum_sb[:, (0 if g2 == 0 else 16) + gch:(0 if g2 == 0 else 16) + gch + 1],
                    sum_sb[:, src:src + 1]), deps=[m_sev2],
                    inc=(g2 == 1 and r == 3 and s2 == NCH - 1))
    m_isc = []
    for g2 in range(2):
        dcol = sum_sb[:, 64 + g2:65 + g2]
        d0 = bass.AP(dcol.tensor, dcol.offset, [[list(dcol.ap)[0][0], 128], [0, 16]])
        m_isc.append(vop(lambda e_, g2=g2, d0=d0: e_.tensor_tensor_scan(
            sum_sb[:, 32 + g2 * 16:48 + g2 * 16], d0,
            sum_sb[:, g2 * 16:16 + g2 * 16], 0.0, ALU.mult, ALU.add),
            deps=[m_ro, m_d512], settle=True))
    m_bo = [dma("gpsimd", bounce.ap()[:, g2 * 17 + 1:g2 * 17 + 17],
                sum_sb[:, 32 + g2 * 16:48 + g2 * 16], deps=[m_isc[g2]])
            for g2 in range(2)]
    m_bo = [dma("gpsimd", bounce.ap()[:, g2 * 17 + 1:g2 * 17 + 17],
                sum_sb[:, 32 + g2 * 16:48 + g2 * 16], deps=[m_bo[g2]])
            for g2 in range(2)]
    m_pick = {}
    for g2 in range(2):
        for c in range(NCH):
            idx = g2 * NCH + c
            P.emit("gpsimd", lambda e_, idx=idx: e_.reg_load(
                rS, spick_s[0:1, idx:idx + 1]), deps=[m_bo[g2], m_vec])
            m_pick[idx] = dma("gpsimd", sin_sb[:, idx:idx + 1],
                              bass.AP(bounce, rS, [[34, 128], [1, 1]]), slow=True)
    m_cor = {}
    for g2 in range(2):
        for c in range(NCH):
            idx = g2 * NCH + c
            m_cor[(c, g2)] = vop(lambda e_, g2=g2, c=c, idx=idx: e_.scalar_tensor_tensor(
                st_ar[:, (g2 * NCH + c) * SL:(g2 * NCH + c + 1) * SL],
                sh8[:, 1024 + g2 * SL:1024 + (g2 + 1) * SL],
                sin_sb[:, idx:idx + 1],
                st_ar[:, (g2 * NCH + c) * SL:(g2 * NCH + c + 1) * SL],
                ALU.mult, ALU.add), deps=[m_pick[idx], m_dt[g2]])
    for c in range(NCH):
        m_cast = None
        for g2 in range(2):
            m_cast = vop(lambda e_, g2=g2, c=c: e_.tensor_copy(
                arA[:, g2 * SL:(g2 + 1) * SL],
                st_ar[:, (g2 * NCH + c) * SL:(g2 * NCH + c + 1) * SL]),
                deps=[m_cor[(c, 0)], m_cor[(c, 1)]], inc=(g2 == 1),
                settle=(g2 == 1))
        for m in range(8):
            pp = p_m0 if m % 2 == 0 else p_a0
            mmx = None
            for n2 in range(2):
                mmx = mm(pp[:, 0:SL],
                         w_out_sb[:, n2 * 1024 + m * 128:n2 * 1024 + (m + 1) * 128],
                         arA[:, n2 * SL:(n2 + 1) * SL], n2 == 0, n2 == 1,
                         deps=[m_cast, fr["m0"] if m % 2 == 0 else fr["a0"]],
                         inc=(n2 == 1))
            m_ya = vop(lambda e_, m=m, c=c, pp=pp: e_.tensor_tensor(
                xsl(m, c, 1, SL), xsl(m, c, 1, SL), pp[:, 0:SL], ALU.add),
                deps=[mmx])
            fr["m0" if m % 2 == 0 else "a0"] = m_ya
    if upto <= 5:
        ctx_close_out(P, nc, ctx, out, x_ar, xsl, dbg_t, dma, vop, act, locals())
        return nc

    P.barrier()
    xn1_ms = emit_layernorm()

    P.barrier()
    fr = dict(a0=None, a1=None, m0=None, m1=None, ropeB=None)
    m_wq = {}
    m_kv_w = []
    m_ones = P.emit("gpsimd", lambda e_: e_.memset(arA[:, 2048:2568], 1.0),
                    inc="gc")
    fr["vdma0"] = fr["vdma1"] = m_ones
    for hp in range(NHP):
        sl_ = hp % 2
        m_wq[hp] = dma("sync", arC[:, sl_ * 3072:(sl_ + 1) * 3072],
                       wc_qkv.ap()[hp, :, :], deps=[m_wq.get(hp - 2), m_prepq],
                       sem="wq")
        for c in range(NCH):
            m_rope = dma("sync", arB[:, 2048:2560], ropec.ap()[c, :, :],
                         deps=[fr["ropeB"]], sem="wr")
            m_rope2 = dma("sync", arB[:, 2560:3072], ropes.ap()[c, :, :],
                          sem="wr")
            m_q = None
            m_k = None
            for g in range(NG):
                m_q = mm(p_a0[:, 0:SL],
                         arC[:, sl_ * 3072 + g * 384:sl_ * 3072 + g * 384 + 128],
                         xnsl(g, c, 1, SL), g == 0, g == NG - 1,
                         deps=[m_wq[hp], xn1_ms[c], fr["a0"]], inc=(g == NG - 1))
            for g in range(NG):
                m_k = mm(p_a1[:, 0:SL],
                         arC[:, sl_ * 3072 + g * 384 + 128:sl_ * 3072 + g * 384 + 256],
                         xnsl(g, c, 1, SL), g == 0, g == NG - 1,
                         deps=[m_wq[hp], xn1_ms[c], fr["a1"]], inc=(g == NG - 1))
            for qk in range(2):
                pp = p_a0 if qk == 0 else p_a1
                m_src = m_q if qk == 0 else m_k
                bias_c = vec_sb[:, (36 if qk == 0 else 44) + hp:
                                (37 if qk == 0 else 45) + hp]
                m_ev = act(arA[:, qk * 1024:qk * 1024 + SL], pp[:, 0:SL], AF.Identity,
                           deps=[m_src, fr.get("qdma" if qk == 0 else "kdma")],
                           bias=bias_c, settle=True)
                fr["a0" if qk == 0 else "a1"] = m_ev
                m_pm = mm(p_m0[:, 0:SL], pswap_s[:, :], arA[:, qk * 1024:qk * 1024 + SL],
                          True, True, deps=[m_ev, fr["m0"]], inc=True)
                m_t1 = vop(lambda e_, qk=qk: e_.tensor_tensor(
                    arA[:, qk * 1024 + 512:qk * 1024 + 1024],
                    arA[:, qk * 1024:qk * 1024 + 512], arB[:, 2048:2560], ALU.mult),
                    deps=[m_ev, m_rope, m_rope2])
                m_t2 = vop(lambda e_, qk=qk: e_.tensor_tensor(
                    arA[:, qk * 1024:qk * 1024 + 512], p_m0[:, 0:SL],
                    arB[:, 2560:3072], ALU.mult), deps=[m_pm])
                fr["m0"] = m_t2
                m_rr = vop(lambda e_, qk=qk: e_.tensor_tensor(
                    arA[:, qk * 1024:qk * 1024 + 512],
                    arA[:, qk * 1024:qk * 1024 + 512],
                    arA[:, qk * 1024 + 512:qk * 1024 + 1024], ALU.add),
                    deps=[m_t1, m_t2], settle=True)
                fr["ropeB"] = m_rr
                if qk == 0:
                    fr["qdma"] = dma("sync", qcache.ap()[hp, c, :, :],
                                     arA[:, 0:512], deps=[m_rr], sem="qw")
                    m_kv_w.append(fr["qdma"])
                else:
                    fr["kdma"] = dma(
                        "sync",
                        kv_kin.ap()[(hp * NCH + c) * 128:(hp * NCH + c + 1) * 128, :],
                        arA[:, 1024:1536], deps=[m_rr], sem="kw")
                    m_kv_w.append(fr["kdma"])
            m_ve2 = {}
            for tb in range(4):
                m_v = None
                for g in range(NG):
                    m_v = mm(p_m1[:, 0:128],
                             xnsl(g, c, 1 + tb * 128, 128),
                             arC[:, sl_ * 3072 + g * 384 + 256:sl_ * 3072 + g * 384 + 384],
                             g == 0, False,
                             deps=[m_wq[hp], xn1_ms[c], fr["m1"]], inc=False)
                m_v = mm(p_m1[:, 0:128], ones_bb[:, :],
                         vrow_b[:, hp * 128:(hp + 1) * 128], False, True,
                         deps=[m_bqv], inc=True)
                for h in range(2):
                    m_ve2[(tb, h)] = act(
                        arA[:, 2048 + h * 260 + tb * 65:2048 + h * 260 + tb * 65 + 64],
                        p_m1[:, h * 64:(h + 1) * 64], AF.Copy,
                        deps=[m_v, fr.get(f"vdma{h}")], settle=True)
                fr["m1"] = m_ve2[(tb, 1)]
            for h in range(2):
                fr[f"vdma{h}"] = dma(
                    "gpsimd",
                    kv_vin.ap()[((2 * hp + h) * NCH + c) * 128:
                                ((2 * hp + h) * NCH + c + 1) * 128, :],
                    arA[:, 2048 + h * 260:2048 + (h + 1) * 260],
                    deps=[m_ve2[(3, h)]], sem="vw")
                m_kv_w.append(fr[f"vdma{h}"])
    m_cc2 = P.emit("gpsimd", lambda e_: e_.collective_compute(
        "AllGather", ALU.bypass, replica_groups=[[0, 1, 2, 3], [4, 5, 6, 7]],
        ins=[kv_kin.ap().opt()], outs=[kv_kall.ap().opt()]),
        deps=m_kv_w, inc="cc")
    m_cc3 = P.emit("gpsimd", lambda e_: e_.collective_compute(
        "AllGather", ALU.bypass, replica_groups=[[0, 1, 2, 3], [4, 5, 6, 7]],
        ins=[kv_vin.ap().opt()], outs=[kv_vall.ap().opt()]),
        inc="cc")
    if upto <= 6:
        ctx_close_out(P, nc, ctx, out, x_ar, xsl, dbg_t, dma, vop, act, locals())
        return nc

    P.barrier()
    fr = dict(a0=None, a1=None, m0=None, m1=None)
    prev_cv = {}
    prev_mh = {}
    prev_ev = {}
    for hg in range(24):
        ping = hg % 2
        stgv = (stg[:, :].bitcast(I32) if ping == 0 else arB[:, :].bitcast(I32))
        m_lg = dma("sync", stgv[:, 0:1024], wg.ap()[hg * 128:(hg + 1) * 128, :],
                   deps=[prev_cv.get(hg - 2)], sem="gl")
        m_lu = dma("gpsimd", stgv[:, 1024:2048], wu.ap()[hg * 128:(hg + 1) * 128, :],
                   deps=[prev_cv.get(hg - 2)], sem="glv")
        m_cv = act(arA[:, ping * 2052:ping * 2052 + 2048], stgv[:, 0:2048],
                   AF.Copy, deps=[m_lg, m_lu, prev_mh.get(hg - 2)],
                   scale=1.0 / 127.0, settle=True)
        prev_cv[hg] = m_cv
        for rnd in range(2):
            pptI = p_m0 if rnd == 0 else p_m1
            m_tr = None
            for qq in range(8):
                q = rnd * 8 + qq
                m_tr = trans(
                    pptI[:, :].bitcast(BF16)[:, qq * 128:(qq + 1) * 128],
                    arA[:, ping * 2052 + q * 128:ping * 2052 + (q + 1) * 128],
                    ident_bs[:, :],
                    deps=[m_cv, prev_ev.get((hg - 1, rnd))])
            m_tr = pesettle()
            prev_ev[(hg, rnd)] = None
            for qq in range(8):
                q = rnd * 8 + qq
                last = (qq == 7)
                prev_ev[(hg, rnd)] = vop(lambda e_, q=q, qq=qq, ping=ping,
                                         pptI=pptI: e_.tensor_copy(
                    arC[:, ping * 2048 + q * 128:ping * 2048 + (q + 1) * 128],
                    pptI[:, :].bitcast(BF16)[:, qq * 128:(qq + 1) * 128]),
                    deps=[m_tr], inc=last, settle=last)
        m_w = prev_ev[(hg, 1)]
        for c in range(NCH):
            pg = p_a0[:, 0:SL] if c % 2 == 0 else p_st[:, 0:SL]
            pu = p_a1[:, 0:SL] if c % 2 == 0 else p_st[:, SL:2 * SL]
            gk, uk = f"g{c % 2}", f"u{c % 2}"
            m_g = None
            m_uu = None
            for g in range(NG):
                m_g = mm(pg, arC[:, ping * 2048 + g * 128:ping * 2048 + (g + 1) * 128],
                         xnsl(g, c, 1, SL), g == 0, g == NG - 1,
                         deps=[m_w, fr.get(gk)], inc=(g == NG - 1))
            for g in range(NG):
                m_uu = mm(pu,
                          arC[:, ping * 2048 + 1024 + g * 128:ping * 2048 + 1024 + (g + 1) * 128],
                          xnsl(g, c, 1, SL), g == 0, g == NG - 1,
                          deps=[m_w, fr.get(uk)], inc=(g == NG - 1))
            m_si = act(arA[:, ping * 2052 + (c % 2) * 512:ping * 2052 + (c % 2) * 512 + 512],
                       pg, AF.Silu, deps=[m_g, fr.get(f"mh{c % 2}")], settle=True)
            fr[gk] = m_si
            hsl = wbufB[:, c * 512:(c + 1) * 512]
            m_h = vop(lambda e_, hsl=hsl, ping=ping, c=c, pu=pu: e_.tensor_tensor(
                hsl, pu,
                arA[:, ping * 2052 + (c % 2) * 512:ping * 2052 + (c % 2) * 512 + 512],
                ALU.mult), deps=[m_si, m_uu, fr.get(f"hdma{c}")], settle=True)
            fr[uk] = m_h
            fr[f"mh{c % 2}"] = m_h
            m_use = dma("scalar", hcache.ap()[c, :, hg * SL:(hg + 1) * SL],
                        hsl, deps=[m_h], sem=f"hw{c}")
            fr[f"hdma{c}"] = m_use
        prev_mh[hg] = m_h
    if upto <= 7:
        ctx_close_out(P, nc, ctx, out, x_ar, xsl, dbg_t, dma, vop, act, locals())
        return nc

    P.barrier()
    m_mask = dma("sync", sh8[:, 0:2048], maskadd.ap())
    st_b = st_ar[:, :].bitcast(BF16)
    stg_b = stg[:, :].bitcast(BF16)
    kw_dep = [("pc2", 0), ("pc3", 0), ("pw", 0), ("gl", 0),
              ("hw0", 0), ("hw1", 0), ("hw2", 0), ("hw3", 0)]
    m_kw = dma("scalar", xn_ar[:, 0:12288], wc_wd.ap()[0, :, :], deps=kw_dep,
               sem="wd")
    dma("scalar", st_b[:, 0:8192], wc_wd.ap()[1, :, 0:8192], sem="wd")
    dma("scalar", stg_b[:, 0:4096], wc_wd.ap()[1, :, 8192:12288], sem="wd")
    dma("scalar", wbufB[:, 0:3072], wc_wo.ap()[0, :, 0:3072], sem="wd")
    dma("scalar", xn_ar[:, 12288:13312], wc_wo.ap()[0, :, 3072:4096], sem="wd")
    dma("scalar", stg_b[:, 4096:6144], wc_wo.ap()[1, :, 0:2048], sem="wd")
    m_kw = dma("scalar", xn_ar[:, 13312:15360], wc_wo.ap()[1, :, 2048:4096],
               sem="wd")
    m_jt = dma("sync", jot[:, :], jof.ap(), sem="jt")
    m_jt = dma("sync", tbb[:, :], tb2.ap(), sem="jt")
    qbufs = [qbuf, qbuf2]
    seq = []
    jb0 = 0
    for (s, hp, njobs) in GROUPS:
        for h in range(2):
            seq.append((s, hp, h, njobs, jb0))
        jb0 += njobs
    kb_use = [None, None]
    vb_use = [None, None]
    pb_use = [None, None]
    arB_use = None
    rrow_use = None
    exp_prev = [None, None]
    lastqk_item = {}
    acc_free = {0: None, 1: None}
    qload = {}

    def emit_qload(i):
        si, hpi, hi, _, _ = seq[i]
        qb_ = qbufs[i % 2]
        dma("sync", qb_[0:64, :], qcache.ap()[hpi, si, 64 * hi:64 * hi + 64, :],
            deps=[lastqk_item.get(i - 2)], sem=f"qb{i % 2}")
        return dma("sync", qb_[67:70, :], qaug.ap()[hpi, si, hi, :, :],
                   sem=f"qb{i % 2}")

    qload[0] = emit_qload(0)
    qload[1] = emit_qload(1)
    for i, (s, hp, h, njobs, jb0) in enumerate(seq):
        hh = 2 * hp + h
        pacc = p_a0 if h == 0 else p_a1
        qb_ = qbufs[i % 2]
        m_qload = qload[i]
        m_b = [dma("sync", kbuf[ii][64:67, :], beta.ap()[hh, :, :],
                   deps=[kb_use[ii]], sem="bb") for ii in range(2)]
        m_pv = None
        pend = None

        def emit_pv(pend):
            bs_, m_exph_, m_vg_, jb_ = pend
            m_pv = None
            for half in range(2):
                for kb in ((0, 1) if half == 0 else (2, 3)):
                    m_pv = mm(pacc[0:65, 0:SL],
                              vbuf[bs_][:, kb * 65:(kb + 1) * 65],
                              pbuf[bs_][:, kb * SL:(kb + 1) * SL],
                              (jb_ == 0 and kb == 0), (jb_ == njobs - 1 and kb == 3),
                              deps=[m_exph_[half], m_vg_,
                                    acc_free[h] if jb_ == 0 else None],
                              inc=(kb == 3))
            vb_use[bs_] = m_pv
            pb_use[bs_] = m_pv
            return m_pv

        for jb in range(njobs):
            jidx = jb0 + jb
            bs = jb % 2
            m_k = P.emit("gpsimd", lambda e_, bs=bs, col=4 * jidx + h:
                         e_.indirect_dma_start(
                kbuf[bs][0:64, :], None, kv_kall.ap()[0:64, :],
                bass.IndirectOffsetOnAxis(ap=jot[0:64, col:col + 1], axis=0)),
                deps=[m_jt, kb_use[bs], ("cc", P.cnt["cc"])],
                inc="gd", incn=16)
            m_vg = P.emit("gpsimd", lambda e_, bs=bs, col=4 * jidx + 2 + h:
                          e_.indirect_dma_start(
                vbuf[bs][:, :], None, kv_vall.ap()[0:128, :],
                bass.IndirectOffsetOnAxis(ap=jot[:, col:col + 1], axis=0)),
                deps=[m_jt, vb_use[bs], ("cc", P.cnt["cc"])],
                inc="gd", incn=16)
            m_qkh = [None, None]
            for half in range(2):
                m_qk = None
                for kb in ((0, 1) if half == 0 else (2, 3)):
                    m_qk = mm(p_st[:, kb * SL:(kb + 1) * SL],
                              kbuf[bs][0:128, kb * 128:(kb + 1) * 128], qb_[0:128, :],
                              True, True,
                              deps=[m_k, m_qload, m_b[bs], exp_prev[half]],
                              inc=(kb in (1, 3)))
                m_qkh[half] = m_qk
            kb_use[bs] = m_qkh[1]
            lastqk_item[i] = m_qkh[1]
            if pend is not None:
                m_pv = emit_pv(pend)
            m_mah = [None, None]
            if jb == 0:
                for half in range(2):
                    m_mah[half] = vop(lambda e_, half=half: e_.tensor_tensor(
                        p_st[:, half * 1024:(half + 1) * 1024],
                        p_st[:, half * 1024:(half + 1) * 1024],
                        sh8[:, half * 1024:(half + 1) * 1024], ALU.add),
                        deps=[m_qkh[half], m_mask], settle=True)
            m_exph = [None, None]
            for half in range(2):
                m_exph[half] = act(pbuf[bs][:, half * 1024:(half + 1) * 1024],
                                   p_st[:, half * 1024:(half + 1) * 1024], AF.Exp,
                                   deps=[m_qkh[half], m_mah[half], pb_use[bs], m_jt],
                                   bias=tbb[:, jidx * 2 + h:jidx * 2 + h + 1],
                                   scale=SC, settle=True)
                exp_prev[half] = m_exph[half]
            pend = (bs, m_exph, m_vg, jb)
        m_pv = emit_pv(pend)
        if i + 2 < len(seq):
            qload[i + 2] = emit_qload(i + 2)
        m_sx = vop(lambda e_, pacc=pacc: e_.tensor_copy(
            scr_f[0:1, 514:514 + SL], pacc[64:65, 0:SL]), deps=[m_pv, rrow_use])
        m_rb = mm(p_m0[0:64, 0:SL], ones_f[0:1, 0:64], scr_f[0:1, 514:514 + SL],
                  True, True, deps=[m_sx, fr["m0"]], inc=True)
        rrow_use = m_rb
        m_rcb = vop(lambda e_: e_.reciprocal(
            scr_f[0:64, 0:SL], p_m0[0:64, 0:SL]), deps=[m_rb], settle=True)
        fr["m0"] = m_rcb
        m_at = vop(lambda e_, pacc=pacc: e_.tensor_tensor(
            arB[0:64, 0:SL], pacc[0:64, 0:SL], scr_f[0:64, 0:SL], ALU.mult),
            deps=[m_rcb, arB_use], settle=True)
        acc_free[h] = m_at
        arB_use = dma("sync", acache.ap()[hp, s, 64 * h:64 * h + 64, :],
                      arB[0:64, 0:SL], deps=[m_at], sem="aw")
    if upto <= 8:
        ctx_close_out(P, nc, ctx, out, x_ar, xsl, dbg_t, dma, vop, act, locals())
        return nc

    P.barrier()
    fr = dict(m0=None, m1=None)
    scr_b = scr_f[:, :].bitcast(BF16)

    def wd_t(mi, k):
        idx = (mi % 4) * 24 + k
        if mi < 4:
            return xn_ar[:, idx * 128:(idx + 1) * 128]
        if idx < 64:
            return st_b[:, idx * 128:(idx + 1) * 128]
        return stg_b[:, (idx - 64) * 128:(idx - 63) * 128]

    def wo_t(mi, hp):
        if mi < 3:
            return wbufB[:, mi * 1024 + hp * 128:mi * 1024 + (hp + 1) * 128]
        if mi == 3:
            return xn_ar[:, 12288 + hp * 128:12288 + (hp + 1) * 128]
        if mi < 6:
            return stg_b[:, 4096 + (mi - 4) * 1024 + hp * 128:
                         4096 + (mi - 4) * 1024 + (hp + 1) * 128]
        return xn_ar[:, 13312 + (mi - 6) * 1024 + hp * 128:
                     13312 + (mi - 6) * 1024 + (hp + 1) * 128]

    def hct(k):
        if k < 12:
            return arC[:, k * 512:(k + 1) * 512]
        if k < 20:
            return pbuf[(k - 12) // 4][:, ((k - 12) % 4) * 512:((k - 12) % 4 + 1) * 512]
        return [qbuf[:, 0:512], kbuf[0][:, 0:512], kbuf[1][:, 0:512],
                scr_b[:, 0:512]][k - 20]

    def act_t(hp):
        return arA[:, hp * 512:(hp + 1) * 512]

    m_last_mm = None
    m_hr = m_h2 = None
    for c in range(NCH):
        ex = ([("hw0", 0), ("hw1", 0), ("hw2", 0), ("hw3", 0), ("aw", 0)]
              if c == 0 else [])
        for hp in range(NHP):
            m_h2 = dma("gpsimd", act_t(hp), acache.ap()[hp, c, :, :],
                       deps=(ex + [m_last_mm]) if hp == 0 else (), sem="h2")
        for k in range(24):
            q, s = (("sync", "hr") if (k < 12 or k >= 20) else ("gpsimd", "h2"))
            m = dma(q, hct(k), hcache.ap()[c, :, k * SL:(k + 1) * SL],
                    deps=(ex + [m_last_mm]) if k in (0, 12) else (), sem=s)
            if s == "hr":
                m_hr = m
            else:
                m_h2 = m
        for mi in range(8):
            pp = p_m0 if mi % 2 == 0 else p_m1
            frk = "m0" if mi % 2 == 0 else "m1"
            mmx = None
            for hp in range(NHP):
                mmx = mm(pp[:, 0:SL], wo_t(mi, hp), act_t(hp),
                         hp == 0, False,
                         deps=[m_h2, m_kw, fr[frk]], inc=False)
            for k in range(24):
                mmx = mm(pp[:, 0:SL], wd_t(mi, k), hct(k), False, k == 23,
                         deps=([m_hr, m_h2] if k == 0 else ()), inc=(k == 23))
            m_x2 = vop(lambda e_, mi=mi, c=c, pp=pp: e_.scalar_tensor_tensor(
                xsl(mi, c, 1, SL), pp[:, 0:SL],
                vec_sb[:, 52 + mi:53 + mi],
                xsl(mi, c, 1, SL), ALU.add, ALU.add),
                deps=[mmx])
            fr[frk] = m_x2
            m_last_mm = mmx
    P.barrier()
    m_out = []
    oev = {}
    for it in range(NCH * 4):
        c, tb = divmod(it, 4)
        half = it % 2
        m_tr3 = None
        for g in range(NG):
            m_tr3 = trans(p_st[:, half * 1024 + g * 128:half * 1024 + (g + 1) * 128],
                          xsl(g, c, 1 + tb * 128, 128), ident_fs[:, :],
                          deps=[oev.get(it - 2)])
        m_tr3 = pesettle()
        m_cp = None
        for g in range(NG):
            m_cp = vop(lambda e_, g=g, half=half: e_.tensor_copy(
                stg[:, half * 1024 + g * 128:half * 1024 + (g + 1) * 128],
                p_st[:, half * 1024 + g * 128:half * 1024 + (g + 1) * 128]),
                deps=[m_tr3, m_out[it - 2] if it >= 2 else None], inc=(g == NG - 1),
                settle=(g == NG - 1))
        oev[it] = m_cp
        m_out.append(dma(["sync", "gpsimd"][half], out.ap()[c, tb * 128:(tb + 1) * 128, :],
                         stg[:, half * 1024:half * 1024 + 1024], deps=[m_cp],
                         sem=f"ow{half}"))
    P.emit("sync", lambda e_: e_.nop(), deps=[m_out[-1], m_out[-2]])
    P.emit("gpsimd", lambda e_: e_.nop(), deps=[("cc", P.cnt["cc"])])
    if dbg_t:
        ctx_close_out(P, nc, ctx, out, x_ar, xsl, dbg_t, dma, vop, act, locals())
        return nc
    P.barrier()

    with nc.Block() as block:
        P.replay(block)
    ctx.close()
    return nc


def ctx_close_out(P, nc, ctx, out, x_ar, xsl, dbg_t, dma, vop, act, env):
    P.barrier()
    srcs = {"dbg_x": env["x_ar"], "dbg_xn": env["xn_ar"], "dbg_st": env["st_ar"],
            "dbg_row": env["row_ar"], "dbg_sum": env["sum_sb"],
            "dbg_arA": env["arA"], "dbg_arB": env["arB"], "dbg_arC": env["arC"],
            "dbg_scr": env["scr_f"], "dbg_sin": env["sin_sb"],
            "dbg_wi": env["w_in_sb"], "dbg_wo": env["w_out_sb"]}
    dram_srcs = {"dbg_ac": "acache", "dbg_h": "hcache", "dbg_q": "qcache",
                 "dbg_kall": "kv_kall", "dbg_vall": "kv_vall",
                 "dbg_kin": "kv_kin", "dbg_sout": "sum_out",
                 "dbg_wc0": "wc_l0", "dbg_wcd": "wc_wd",
                 "dbg_wco": "wc_wo", "dbg_wcq": "wc_qkv"}
    m_last = None
    if P.cnt.get("cc", 0):
        P.emit("sync", lambda e_: e_.nop(), deps=[("cc", P.cnt["cc"])])
    for nm, t in dbg_t.items():
        if nm == "dbg_u":
            continue
        if nm in dram_srcs:
            s_t = env[dram_srcs[nm]]
            m_last = dma("sync", t.ap().flatten(), s_t.ap().flatten())
        else:
            src = srcs[nm]
            w = t.ap().shape[1]
            m_last = dma("sync", t.ap(), src[:, 0:w])
    if m_last is not None:
        P.emit("sync", lambda e_: e_.nop(), deps=[m_last])
    if P.cnt.get("cc", 0):
        P.emit("gpsimd", lambda e_: e_.nop(), deps=[("cc", P.cnt["cc"])])
    P.barrier()
    with nc.Block() as block:
        P.replay(block)
    ctx.close()


def kernel(**inputs):
    shared = build_shared_consts()
    nc = build_program(upto=99)
    in_maps = [build_core_inputs(c, inputs, shared) for c in range(8)]
    res = run_bass_kernel_spmd(nc, in_maps, core_ids=list(range(8)))
    outp = np.zeros((2, 16 * SL, D), dtype=np.float32)
    for core in range(8):
        b, j = divmod(core, 4)
        for s, g in enumerate(chunks_of(j)):
            outp[b, g * SL:(g + 1) * SL, :] = res.results[core]["out"][s]
    return outp


# version 15
# speedup vs baseline: 1.1442x; 1.1442x over previous
import math
import numpy as np
import ml_dtypes
import concourse.bass as bass
import concourse.mybir as mybir
from concourse.bass_utils import run_bass_kernel_spmd

F32 = mybir.dt.float32
BF16 = mybir.dt.bfloat16
I32 = mybir.dt.int32
AF = mybir.ActivationFunctionType
ALU = mybir.AluOpType
AX = mybir.AxisListType.X
BF = ml_dtypes.bfloat16

D = 1024
NG = 8
SL = 512
NCH = 4
CW = SL + 1
NH = 16
NHP = 8
SC = 1.0 / math.sqrt(192.0)
SQ192 = math.sqrt(192.0)
NB = [2, 2, 2, 2, 2, 2, 3, 3, 4, 5, 7, 9, 12, 16, 16, 16]
NBP = [max(NB[2 * i], NB[2 * i + 1]) for i in range(8)]
SLOT_CAP = [4, 8, 12, 16]
ENG_NAMES = ["sync", "gpsimd", "tensor", "scalar", "vector"]


def slope8(h):
    return 8.0 * 2.0 ** (-(h + 1) / 2.0)


def chunks_of(j):
    return [j, 7 - j, 8 + j, 15 - j]


def owner_of(g):
    j = min(g, 7 - g) if g < 8 else min(g - 8, 15 - g)
    return j, chunks_of(j).index(g)


def group_struct():
    gs = []
    for s in range(NCH):
        for hp in range(NHP):
            gs.append((s, hp, min(SLOT_CAP[s], NBP[hp])))
    return gs


GROUPS = group_struct()
NJOBS = sum(n for _, _, n in GROUPS)


def make_jobs(j):
    jobs = []
    ch = chunks_of(j)
    for s, hp, njobs in GROUPS:
        pos = ch[s]
        for jb in range(njobs):
            kch = max(pos - jb, 0)
            r, s2 = owner_of(kch)
            jobs.append(dict(s=s, hp=hp, jb=jb, kch=kch, pad=(jb > pos),
                             rank=r, kslot=s2))
    return jobs


def split3(x):
    a = x.astype(BF)
    r = x - a.astype(np.float32)
    b = r.astype(BF)
    c = (r - b.astype(np.float32)).astype(BF)
    return a, b, c


_SHARED = None


def build_shared_consts():
    global _SHARED
    if _SHARED is not None:
        return _SHARED
    c = {}
    c["ident_f"] = np.eye(128, dtype=np.float32)
    c["ident_b"] = np.eye(128, dtype=BF)
    P = np.zeros((128, 128), dtype=np.float32)
    for m in range(128):
        blk, r = divmod(m, 32)
        P[[1, 0, 3, 2][blk] * 32 + r, m] = 1.0
    c["pswap"] = P.astype(BF)
    mk = np.zeros((128, 4 * SL), dtype=np.float32)
    for kb in range(4):
        for p in range(128):
            kl = kb * 128 + p
            mk[p, kb * SL:(kb + 1) * SL][np.arange(SL) < kl] = -4e9
    c["maskadd"] = mk
    bet = np.zeros((NH, 3, SL), dtype=BF)
    t = np.arange(SL, dtype=np.float32)
    for h in range(NH):
        bet[h, 0], bet[h, 1], bet[h, 2] = split3(slope8(h) * t * SQ192)
    c["beta"] = bet
    inv = 1.0 / (10000.0 ** (np.arange(0, 64, 2, dtype=np.float32) / 64))
    fr = np.outer(np.arange(16 * SL, dtype=np.float32), inv)
    sgn = np.concatenate([-np.ones(32), np.ones(32), -np.ones(32), np.ones(32)])
    c["cos_full"] = np.tile(np.cos(fr).T, (4, 1)).astype(BF)
    c["sin_full"] = (np.tile(np.sin(fr).T, (4, 1)) * sgn[:, None]).astype(BF)
    _SHARED = c
    return c


def build_core_inputs(core, inp, shared):
    b, j = divmod(core, 4)
    ch = chunks_of(j)
    d = {}
    tok = np.asarray(inp["L0_attn_tokens"])
    d["tok"] = np.ascontiguousarray(
        np.stack([tok[b, g * SL:(g + 1) * SL, :] for g in ch], axis=0))
    halo = np.zeros((NCH, D), dtype=np.float32)
    for s, g in enumerate(ch):
        if g > 0:
            halo[s] = tok[b, g * SL - 1, :]
    d["halo"] = halo

    def dmaj(v, ng):
        return np.ascontiguousarray(np.asarray(v).reshape(ng, 128).T)
    d["b_in"] = dmaj(inp["L0_attn_b_in"], 2)
    d["ldec"] = dmaj(inp["L0_attn_log_decay"], 2)
    d["normw"] = dmaj(inp["L0_attn_norm_w"], 8)
    d["bsum"] = np.ascontiguousarray(
        (np.asarray(inp["L0_attn_b_out"]).reshape(8, 128)
         + np.asarray(inp["L0_attn_b_skip"]).reshape(8, 128)).T)
    d["mur"] = dmaj(inp["L0_ffn_mu_r"], 8)
    d["muk"] = dmaj(inp["L0_ffn_mu_k"], 8)
    d["bout1"] = dmaj(inp["L1_attn_b_out"], 8)
    bq = np.asarray(inp["L1_attn_b_qkv"])
    qb = np.zeros((128, 8), np.float32)
    kb_ = np.zeros((128, 8), np.float32)
    vb = np.zeros((8, 128), np.float32)
    for hp in range(8):
        qb[0:64, hp] = bq[2 * hp, 0:64]
        qb[64:128, hp] = bq[2 * hp + 1, 0:64]
        kb_[0:64, hp] = bq[2 * hp, 64:128]
        kb_[64:128, hp] = bq[2 * hp + 1, 64:128]
        vb[hp, 0:64] = bq[2 * hp, 128:192]
        vb[hp, 64:128] = bq[2 * hp + 1, 128:192]
    d["bq_q"], d["bq_k"] = qb, kb_
    d["bq_v"] = np.ascontiguousarray(vb.reshape(1, 8 * 128))
    for nm, key in [("W_in", "L0_attn_W_in"), ("W_out", "L0_attn_W_out"),
                    ("W_skip", "L0_attn_W_skip"), ("W_r", "L0_ffn_W_r"),
                    ("W_k", "L0_ffn_W_k"), ("W_v", "L0_ffn_W_v"),
                    ("wg", "L1_ffn_w_gate"), ("wu", "L1_ffn_w_up"),
                    ("wd", "L1_ffn_w_down")]:
        d[nm] = np.asarray(inp[key])
    d["wqkv"] = np.ascontiguousarray(np.asarray(inp["L1_attn_w_qkv"]).reshape(D, NH * 192))
    d["wout1"] = np.ascontiguousarray(np.asarray(inp["L1_attn_w_out"]).reshape(NH * 64, D))
    for k in ("ident_f", "ident_b", "pswap", "maskadd", "beta"):
        d[k] = shared[k]
    cs = np.zeros((NCH, 128, SL), BF)
    sn = np.zeros((NCH, 128, SL), BF)
    for s, g in enumerate(ch):
        cs[s] = shared["cos_full"][:, g * SL:(g + 1) * SL]
        sn[s] = shared["sin_full"][:, g * SL:(g + 1) * SL]
    d["ropec"], d["ropes"] = cs, sn
    jobs = make_jobs(j)
    jof = np.zeros((128, NJOBS * 4), np.int32)
    tb2 = np.zeros((128, NJOBS * 2), np.float32)
    prow = np.arange(128, dtype=np.int32)
    for idx, jb in enumerate(jobs):
        for h in range(2):
            kr = ((jb["rank"] * NHP + jb["hp"]) * NCH + jb["kslot"]) * 128 + 64 * h
            vr = ((jb["rank"] * NH + 2 * jb["hp"] + h) * NCH + jb["kslot"]) * 128
            jof[:, idx * 4 + h] = kr + prow
            jof[:, idx * 4 + 2 + h] = vr + prow
            tb2[:, idx * 2 + h] = (-1e9 if jb["pad"]
                                   else slope8(2 * jb["hp"] + h) * (jb["kch"] * SL)) - 16.0
    d["jof"], d["tb2"] = jof, tb2
    qaug = np.zeros((NHP, NCH, 2, 3, SL), BF)
    for hp in range(NHP):
        for s in range(NCH):
            q = (ch[s] * SL + np.arange(SL)).astype(np.float32)
            for h in range(2):
                a, bb, cc = split3(-slope8(2 * hp + h) * q * SQ192)
                qaug[hp, s, h, 0], qaug[hp, s, h, 1], qaug[hp, s, h, 2] = a, bb, cc
    d["qaug"] = qaug
    sp = np.zeros((1, 8), np.int32)
    for g in range(2):
        for s in range(NCH):
            sp[0, g * NCH + s] = g * 17 + ch[s]
    d["spick"] = sp
    return d


class Prog:
    def __init__(self, nc, ctx):
        self.nc = nc
        self.ctx = ctx
        self.ops = {n: [] for n in ENG_NAMES}
        self.h = {}
        self.cnt = {}
        self.owner = {}
        self.lastwait = {n: {} for n in ENG_NAMES}
        self.dma_sems = {"sd", "gd"}

    def sem(self, name, owner=None):
        if name not in self.h:
            self.h[name] = self.ctx.enter_context(self.nc.semaphore(name))
            self.cnt[name] = 0
            self.owner[name] = owner
        return self.h[name]

    def emit(self, eng, fn, deps=(), inc=None, incn=1):
        for dep in deps:
            if dep is None:
                continue
            s, v = dep
            if self.owner.get(s) == eng and s not in self.dma_sems:
                continue
            if s in self.dma_sems:
                v = self.cnt[s]
            if self.lastwait[eng].get(s, -1) >= v:
                continue
            self.lastwait[eng][s] = v
            self.ops[eng].append(("w", (s, v)))
        if inc:
            self.cnt[inc] += incn
            self.ops[eng].append(("o", (fn, inc, incn)))
            return (inc, self.cnt[inc])
        self.ops[eng].append(("o", (fn, None, 0)))
        return None

    def barrier(self):
        self.sem("ph", owner=None)
        for n in ENG_NAMES:
            self.cnt["ph"] += 1
            self.ops[n].append(("o", (lambda e: e.nop(), "ph", 1)))
        v = self.cnt["ph"]
        for n in ENG_NAMES:
            self.lastwait[n]["ph"] = v
            self.ops[n].append(("w", ("ph", v)))

    def replay(self, block):
        for name in ENG_NAMES:
            mine = self.ops[name]

            def sec(eng, mine=mine, name=name):
                for oi, (kind, payload) in enumerate(mine):
                    if kind == "w":
                        s, v = payload
                        eng.wait_ge(self.h[s], v)
                    else:
                        fn, inc, incn = payload
                        try:
                            inst = fn(eng)
                        except Exception as ex:
                            raise RuntimeError(f"replay fail {name} op {oi}: {ex}") from ex
                        if inc is not None:
                            inst.then_inc(self.h[inc], incn)
            getattr(block, name)(sec)


def build_program(upto=99, dbg=()):
    from contextlib import ExitStack
    nc = bass.Bass("TRN2", target_bir_lowering=False, detect_race_conditions=False)
    ctx = ExitStack()
    e = ctx.enter_context

    def din(name, shape, dt):
        return nc.dram_tensor(name, list(shape), dt, kind="ExternalInput")

    tok = din("tok", (NCH, SL, D), F32)
    halo = din("halo", (NCH, D), F32)
    b_in = din("b_in", (128, 2), F32)
    ldec = din("ldec", (128, 2), F32)
    normw = din("normw", (128, 8), F32)
    bsum = din("bsum", (128, 8), F32)
    mur = din("mur", (128, 8), F32)
    muk = din("muk", (128, 8), F32)
    bout1 = din("bout1", (128, 8), F32)
    bq_q = din("bq_q", (128, 8), F32)
    bq_k = din("bq_k", (128, 8), F32)
    bq_v = din("bq_v", (1, 8 * 128), F32)
    W_in = din("W_in", (256, D), F32)
    W_out = din("W_out", (D, 256), F32)
    W_skip = din("W_skip", (D, D), F32)
    W_r = din("W_r", (D, D), F32)
    W_k = din("W_k", (D, D), F32)
    W_v = din("W_v", (D, D), F32)
    wqkv = din("wqkv", (D, NH * 192), F32)
    wout1 = din("wout1", (NH * 64, D), F32)
    wg = din("wg", (3072, D), I32)
    wu = din("wu", (3072, D), I32)
    wd = din("wd", (D, 3072), I32)
    ident_f = din("ident_f", (128, 128), F32)
    ident_b = din("ident_b", (128, 128), BF16)
    pswap = din("pswap", (128, 128), BF16)
    maskadd = din("maskadd", (128, 4 * SL), F32)
    beta = din("beta", (NH, 3, SL), BF16)
    ropec = din("ropec", (NCH, 128, SL), BF16)
    ropes = din("ropes", (NCH, 128, SL), BF16)
    jof = din("jof", (128, NJOBS * 4), I32)
    tb2 = din("tb2", (128, NJOBS * 2), F32)
    qaug = din("qaug", (NHP, NCH, 2, 3, SL), BF16)
    spick = din("spick", (1, 8), I32)

    out = nc.dram_tensor("out", [NCH, SL, D], F32, kind="ExternalOutput")
    dbg_t = {}
    for nm, shape, dt in dbg:
        dbg_t[nm] = nc.dram_tensor(nm, list(shape), dt, kind="ExternalOutput")

    wc_l0 = nc.dram_tensor("wc_l0", [4, 128, 8 * 8 * 128], BF16)
    wc_wd = nc.dram_tensor("wc_wd", [2, 128, 4 * 24 * 128], BF16)
    wc_wo = nc.dram_tensor("wc_wo", [2, 128, 4 * 8 * 128], BF16)
    wc_qkv = nc.dram_tensor("wc_qkv", [NHP, 128, 8 * 384], BF16)
    qcache = nc.dram_tensor("qcache", [NHP, NCH, 128, SL], BF16)
    acache = nc.dram_tensor("acache", [NHP, NCH, 128, SL], BF16)
    hcache = nc.dram_tensor("hcache", [NCH, 128, 24 * SL], BF16)
    sum_in = nc.dram_tensor("sum_in", [8, 128], F32)
    sum_out = nc.dram_tensor("sum_out", [32, 128], F32)
    kv_kin = nc.dram_tensor("kv_kin", [NHP * NCH * 128, SL], BF16)
    kv_vin = nc.dram_tensor("kv_vin", [NH * NCH * 128, 260], BF16)
    kv_kall = nc.dram_tensor("kv_kall", [4 * NHP * NCH * 128, SL], BF16)
    kv_vall = nc.dram_tensor("kv_vall", [4 * NH * NCH * 128, 260], BF16)
    bounce = nc.dram_tensor("bounce", [128, 34], F32)

    x_ar = e(nc.sbuf_tensor("x_ar", [128, NG * NCH * CW], F32))
    xn_ar = e(nc.sbuf_tensor("xn_ar", [128, NG * NCH * CW], BF16))
    arA = e(nc.sbuf_tensor("arA", [128, NG * CW], BF16))
    arB = e(nc.sbuf_tensor("arB", [128, NG * CW], BF16))
    arC = e(nc.sbuf_tensor("arC", [128, 6144], BF16))
    st_ar = e(nc.sbuf_tensor("st_ar", [128, 2 * NCH * SL], F32))
    row_ar = e(nc.sbuf_tensor("row_ar", [128, 2 * CW], F32))
    scr_f = e(nc.sbuf_tensor("scr_f", [128, 2 * CW], F32))
    stg = e(nc.sbuf_tensor("stg", [128, 3072], F32))
    wbufB = e(nc.sbuf_tensor("wbufB", [128, 3072], BF16))
    vec_sb = e(nc.sbuf_tensor("vec_sb", [128, 66], F32))
    vrow_b = e(nc.sbuf_tensor("vrow_b", [1, 8 * 128], BF16))
    ident_fs = e(nc.sbuf_tensor("ident_fs", [128, 128], F32))
    ident_bs = e(nc.sbuf_tensor("ident_bs", [128, 128], BF16))
    pswap_s = e(nc.sbuf_tensor("pswap_s", [128, 128], BF16))
    sh8 = e(nc.sbuf_tensor("sh8", [128, 2048], F32))
    iotab = e(nc.sbuf_tensor("iotab", [128, SL], F32))
    onesb = e(nc.sbuf_tensor("onesb", [128, 1], BF16))
    ones_f = e(nc.sbuf_tensor("ones_f", [65, 128], F32))
    ones_bb = e(nc.sbuf_tensor("ones_bb", [1, 128], BF16))
    sum_sb = e(nc.sbuf_tensor("sum_sb", [128, 72], F32))
    spick_s = e(nc.sbuf_tensor("spick_s", [1, 8], I32))
    jot = e(nc.sbuf_tensor("jot", [128, NJOBS * 4], I32))
    tbb = e(nc.sbuf_tensor("tbb", [128, NJOBS * 2], F32))
    sin_sb = e(nc.sbuf_tensor("sin_sb", [128, 8], F32))
    w_in_sb = e(nc.sbuf_tensor("w_in_sb", [128, 8 * 256], BF16))
    w_out_sb = e(nc.sbuf_tensor("w_out_sb", [128, 2 * 1024], BF16))
    kbuf = [e(nc.sbuf_tensor(f"kbuf{i}", [128, SL], BF16)) for i in range(2)]
    vbuf = [e(nc.sbuf_tensor(f"vbuf{i}", [128, 4 * 65], BF16)) for i in range(2)]
    qbuf = e(nc.sbuf_tensor("qbuf", [128, SL], BF16))
    qbuf2 = e(nc.sbuf_tensor("qbuf2", [128, SL], BF16))
    pbuf = [e(nc.sbuf_tensor(f"pbuf{i}", [128, 4 * SL], BF16)) for i in range(2)]
    epsc = e(nc.sbuf_tensor("epsc", [128, 2], F32))

    p_st = e(nc.psum_tensor("p_st", [128, 4 * SL], F32))
    p_a0 = e(nc.psum_tensor("p_a0", [128, SL], F32))
    p_a1 = e(nc.psum_tensor("p_a1", [128, SL], F32))
    p_m0 = e(nc.psum_tensor("p_m0", [128, SL], F32))
    p_m1 = e(nc.psum_tensor("p_m1", [128, SL], F32))

    P = Prog(nc, ctx)
    for s_, o_ in [("sd", "sync"), ("gd", "gpsimd"), ("pe", "tensor"),
                   ("se", "scalar"), ("ve", "vector"), ("gc", "gpsimd"),
                   ("cc", None)]:
        P.sem(s_, owner=o_)

    rS = nc.gpsimd.alloc_register("rS")

    def xsl(g, c, c0, w):
        return x_ar[:, (g * NCH + c) * CW + c0:(g * NCH + c) * CW + c0 + w]

    def xnsl(g, c, c0, w):
        return xn_ar[:, (g * NCH + c) * CW + c0:(g * NCH + c) * CW + c0 + w]

    def dma(q, o, i, deps=(), slow=False, dbl=False, sem=None):
        kw = dict(allow_slow_non_contiguous=True) if slow else {}
        s = sem or ("sd" if q == "sync" else "gd")
        if s not in P.h:
            P.sem(s, owner=q)
            P.dma_sems.add(s)
        m = P.emit(q, lambda e_, o=o, i=i, kw=kw: e_.dma_start(out=o, in_=i, **kw),
                   deps=deps, inc=s, incn=16)
        if dbl:
            m = P.emit(q, lambda e_, o=o, i=i, kw=kw: e_.dma_start(out=o, in_=i, **kw),
                       inc=s, incn=16)
        return m

    def mm(pp, lhsT, rhs, start, stop, deps=(), inc=False):
        return P.emit("tensor", lambda e_, pp=pp, l=lhsT, r=rhs, s1=start, s2=stop:
                      e_.matmul(pp, l, r, start=s1, stop=s2), deps=deps,
                      inc=("pe" if inc else None))

    def act(out_, in_, f, deps=(), bias=None, scale=None, inc=True, settle=False):
        kw = {}
        if bias is not None:
            kw["bias"] = bias
        if scale is not None:
            kw["scale"] = scale
        m = P.emit("scalar", lambda e_, o=out_, i=in_, f=f, kw=kw:
                   e_.activation(o, i, f, **kw), deps=deps,
                   inc=("se" if (inc or settle) else None))
        return m

    def vop(fn, deps=(), inc=True, settle=False):
        m = P.emit("vector", fn, deps=deps,
                   inc=("ve" if (inc or settle) else None))
        return m

    def trans(out_, in_, ident, deps=()):
        return P.emit("tensor", lambda e_, o=out_, i=in_, d=ident:
                      e_.transpose(o, i, d), deps=deps, inc="pe")

    def pesettle():
        return P.emit("tensor", lambda e_: e_.nop(), inc="pe")

    m_vec = None
    for src, dst in [(b_in, vec_sb[:, 0:2]), (ldec, vec_sb[:, 2:4]),
                     (normw, vec_sb[:, 4:12]), (bsum, vec_sb[:, 12:20]),
                     (mur, vec_sb[:, 20:28]), (muk, vec_sb[:, 28:36]),
                     (bq_q, vec_sb[:, 36:44]), (bq_k, vec_sb[:, 44:52]),
                     (bout1, vec_sb[:, 52:60]),
                     (ident_f, ident_fs[:, :]), (ident_b, ident_bs[:, :]),
                     (pswap, pswap_s[:, :]), (spick, spick_s[:, :])]:
        m_vec = dma("sync", dst, src.ap())
    for c in range(NCH):
        for g in range(NG):
            dma("sync", xsl(g, c, 0, 1),
                halo.ap()[c:c + 1, g * 128:(g + 1) * 128].rearrange("a p -> p a"),
                slow=True)
    m_mem = None
    for t_, v_ in [(onesb[:, :], 1.0), (ones_f[:, :], 1.0), (ones_bb[:, :], 1.0),
                   (qbuf[64:128, :], 0.0), (qbuf2[64:128, :], 0.0),
                   (qbuf[64:67, :], 1.0), (qbuf2[64:67, :], 1.0),
                   (kbuf[0][64:96, :], 1.0), (kbuf[1][64:96, :], 1.0),
                   (kbuf[0][96:128, :], 0.0), (kbuf[1][96:128, :], 0.0),
                   (sum_sb[:, :], 0.0), (epsc[:, 0:1], 1e-5), (epsc[:, 1:2], 1e-4)]:
        m_mem = P.emit("gpsimd", lambda e_, t=t_, v=v_: e_.memset(t, v), inc="gc")
    m_iota = P.emit("gpsimd", lambda e_: e_.iota(
        iotab[:, :], pattern=[[1, SL]], base=1, channel_multiplier=0,
        allow_small_or_imprecise_dtypes=True), inc="gc")
    dma("gpsimd", bass.AP(bounce, 0, [[34, 128], [1, 1]]), sum_sb[:, 0:1], deps=[m_mem], slow=True)
    dma("gpsimd", bass.AP(bounce, 17, [[34, 128], [1, 1]]), sum_sb[:, 0:1], slow=True)
    m_dec = act(vec_sb[:, 60:62], vec_sb[:, 2:4], AF.Sigmoid, deps=[m_vec],
                settle=True)
    m_L = act(vec_sb[:, 62:64], vec_sb[:, 60:62], AF.Ln)
    m_l = dma("sync", scr_f[0:1, 0:1024], bq_v.ap())
    m_bqv = act(vrow_b[:, :], scr_f[0:1, 0:1024], AF.Copy, deps=[m_l], settle=True)

    if upto <= 0:
        ctx_close_out(P, nc, ctx, out, x_ar, xsl, dbg_t, dma, vop, act, locals())
        return nc

    P.barrier()
    ing_ev = {}
    for it in range(NCH * 4):
        c, tb = divmod(it, 4)
        half = it % 2
        m_l = dma(["sync", "gpsimd"][half], stg[:, half * 1024:half * 1024 + 1024],
                  tok.ap()[c, tb * 128:(tb + 1) * 128, :],
                  deps=[ing_ev.get(it - 2)], sem=f"il{half}")
        m_tr = None
        for g in range(NG):
            m_tr = trans(p_st[:, half * 1024 + g * 128:half * 1024 + (g + 1) * 128],
                         stg[:, half * 1024 + g * 128:half * 1024 + (g + 1) * 128],
                         ident_fs[:, :], deps=[m_l, ing_ev.get(it - 2)])
        m_tr = pesettle()
        evs = None
        for g in range(NG):
            evs = vop(lambda e_, g=g, c=c, tb=tb, half=half: e_.tensor_copy(
                xsl(g, c, 1 + tb * 128, 128),
                p_st[:, half * 1024 + g * 128:half * 1024 + (g + 1) * 128]),
                deps=[m_tr], inc=(g == NG - 1))
        ing_ev[it] = evs
    if upto <= 1:
        ctx_close_out(P, nc, ctx, out, x_ar, xsl, dbg_t, dma, vop, act, locals())
        return nc

    P.barrier()

    prep_chain = [None]
    prep_chain_cv = [None]
    prep_chain_out = [(None, None)]

    PREP_SLOTS = [(stg, 0), (stg, 1024), (stg, 2048),
                  (sh8, 0), (sh8, 1024), (scr_f, 0)]
    PREP_QS = ["sync", "gpsimd", "scalar"]

    def prep_tiles(n4, src_fn, dst_fn, out_fn=None, int_scale=None, ntr=4,
                   load_q=None, load_sem=None):
        prev_cv = {k: prep_chain_cv[0] for k in range(-6, 0)}
        prev_ev = {-1: prep_chain[0], -2: prep_chain[0]}
        prev_out = {-1: prep_chain_out[0][1] or prep_chain[0],
                    -2: prep_chain_out[0][0] or prep_chain[0]}
        w_ = 128 * ntr

        def slot_view(i4):
            st_t, st_c = PREP_SLOTS[i4 % 6]
            return (st_t[:, :].bitcast(I32)[:, st_c:st_c + w_]
                    if int_scale else st_t[:, st_c:st_c + w_])

        def emit_ld(i4):
            return dma(PREP_QS[i4 % 3], slot_view(i4), src_fn(i4),
                       deps=[prev_cv.get(i4 - 6)], sem=f"pl{i4 % 6}")

        lds = {}
        for i4 in range(min(6, n4)):
            lds[i4] = emit_ld(i4)
        for i4 in range(n4):
            half = i4 % 2
            stg_v = slot_view(i4)
            m_ld = lds[i4]
            m_cv = act(arA[:, half * 2052:half * 2052 + w_], stg_v, AF.Copy,
                       deps=[m_ld, prev_ev.get(i4 - 2)], scale=int_scale,
                       settle=True)
            prev_cv[i4] = m_cv
            def ptile(q, i4=i4):
                if i4 % 2 == 0:
                    return p_st[:, :].bitcast(BF16)[:, q * 1024:q * 1024 + 128]
                pb = p_m0 if q < 2 else p_m1
                return pb[:, :].bitcast(BF16)[:, (q % 2) * 128:(q % 2) * 128 + 128]
            m_tr = None
            for q in range(ntr):
                m_tr = trans(
                    ptile(q),
                    arA[:, half * 2052 + q * 128:half * 2052 + (q + 1) * 128],
                    ident_bs[:, :], deps=[m_cv, prev_ev.get(i4 - 2)])
            m_tr = pesettle()
            m_ev = None
            for q in range(ntr):
                last = (q == ntr - 1)
                m_ev = vop(lambda e_, q=q, i4=i4: e_.tensor_copy(
                    dst_fn(i4, q), ptile(q, i4)),
                    deps=[m_tr, prev_out.get(i4 - 2)] if q == 0 else [m_tr],
                    inc=last, settle=last)
            prev_ev[i4] = m_ev
            prev_out[i4] = out_fn(i4, m_ev) if out_fn else m_ev
            if i4 + 6 < n4:
                lds[i4 + 6] = emit_ld(i4 + 6)
        prep_chain[0] = prev_ev[n4 - 1]
        prep_chain_cv[0] = prev_cv[n4 - 1]
        if out_fn:
            prep_chain_out[0] = (prev_out.get(n4 - 2), prev_out.get(n4 - 1))
        return prev_out

    l0_srcs = [W_r, W_k, W_v, W_skip]
    l0_outs = prep_tiles(
        64,
        lambda i4: l0_srcs[i4 // 16].ap()[((i4 // 2) % 8) * 128:((i4 // 2) % 8 + 1) * 128,
                                          (i4 % 2) * 512:(i4 % 2 + 1) * 512],
        lambda i4, q: arB[:, (i4 % 2) * 2052 + q * 128:(i4 % 2) * 2052 + (q + 1) * 128],
        out_fn=lambda i4, mev: dma(
            "gpsimd",
            wc_l0.ap()[i4 // 16, :, ((i4 // 2) % 8) * 1024 + (i4 % 2) * 512:
                       ((i4 // 2) % 8) * 1024 + (i4 % 2) * 512 + 512],
            arB[:, (i4 % 2) * 2052:(i4 % 2) * 2052 + 512], deps=[mev],
            sem=f"pc{i4 % 2}"))
    prep_tiles(
        4,
        lambda i4: W_in.ap()[(i4 // 2) * 128:(i4 // 2 + 1) * 128,
                             (i4 % 2) * 512:(i4 % 2 + 1) * 512],
        lambda i4, q: w_in_sb[:, ((i4 % 2) * 4 + q) * 256 + (i4 // 2) * 128:
                              ((i4 % 2) * 4 + q) * 256 + (i4 // 2 + 1) * 128])
    prep_tiles(
        8,
        lambda i4: W_out.ap()[i4 * 128:(i4 + 1) * 128, 0:256],
        lambda i4, q: w_out_sb[:, q * 1024 + i4 * 128:q * 1024 + (i4 + 1) * 128],
        ntr=2)
    wd_outs = prep_tiles(
        48,
        lambda i4: wd.ap()[(i4 // 6) * 128:(i4 // 6 + 1) * 128,
                           (i4 % 6) * 512:(i4 % 6 + 1) * 512],
        lambda i4, q: arB[:, (i4 % 2) * 2052 + q * 128:(i4 % 2) * 2052 + (q + 1) * 128],
        out_fn=lambda i4, mev: dma(
            "scalar",
            wc_wd.ap()[(i4 // 6) // 4, :, ((i4 // 6) % 4) * 3072 + (i4 % 6) * 512:
                       ((i4 // 6) % 4) * 3072 + (i4 % 6) * 512 + 512],
            arB[:, (i4 % 2) * 2052:(i4 % 2) * 2052 + 512], deps=[mev],
            sem=f"pc{2 + i4 % 2}"),
        int_scale=1.0 / 127.0, load_q="gpsimd", load_sem="pl2")
    prevo = {}
    prevo_cv = {}
    for m in range(8):
        half = m % 2
        m_ld = None
        for a in range(8):
            m_ld = dma("sync", stg[:, half * 1024 + a * 128:half * 1024 + (a + 1) * 128],
                       wout1.ap()[a * 128:(a + 1) * 128, m * 128:(m + 1) * 128],
                       deps=([prevo.get(m - 2), prep_chain_cv[0]] if a == 0 else ()),
                       sem="pl")
        m_cv = act(arA[:, half * 2052:half * 2052 + 1024],
                   stg[:, half * 1024:half * 1024 + 1024], AF.Copy,
                   deps=[m_ld, prep_chain[0] if m < 2 else None], settle=True)
        prevo_cv[m] = m_cv
        prevo[m] = dma("gpsimd", wc_wo.ap()[m // 4, :, (m % 4) * 1024:(m % 4) * 1024 + 1024],
                       arA[:, half * 2052:half * 2052 + 1024], deps=[m_cv],
                       sem="pw")
    prevq = {}
    prevsh = {}
    for dc in range(8):
        m_ld2 = dma("gpsimd", stg[:, 1536:3072],
                    wqkv.ap()[dc * 128:(dc + 1) * 128, 1536:3072],
                    sem="pl4", deps=[prevsh.get(dc - 1),
                          (prevo_cv[7] if dc == 0 else None),
                          (prevo_cv[6] if dc == 0 else None)])
        m_ld = dma("scalar", stg[:, 0:1536], wqkv.ap()[dc * 128:(dc + 1) * 128, 0:1536],
                   sem="pl3", deps=[prevsh.get(dc - 1),
                         (prevo_cv[7] if dc == 0 else None),
                         (prevo_cv[6] if dc == 0 else None)])
        m_sh = None
        for kind in range(3):
            m_sh = vop(lambda e_, kind=kind: e_.tensor_copy(
                bass.AP(arB, kind * 128, [[4104, 128], [384, 8], [64, 2], [1, 64]]),
                bass.AP(stg, kind * 64, [[3072, 128], [384, 8], [192, 2], [1, 64]])),
                deps=([m_ld, m_ld2, prevq.get(dc - 1),
                       (prevo[7] if dc == 0 else None),
                       (prevo[6] if dc == 0 else None),
                       (wd_outs.get(46) if dc == 0 else None),
                       (wd_outs.get(47) if dc == 0 else None)]
                      if kind == 0 else [m_ld, m_ld2]),
                inc=(kind == 2), settle=(kind == 2))
        prevsh[dc] = m_sh
        m_o = None
        for hp in range(8):
            m_o = dma("gpsimd", wc_qkv.ap()[hp, :, dc * 384:(dc + 1) * 384],
                      arB[:, hp * 384:(hp + 1) * 384],
                      deps=[m_sh] if hp == 0 else (), sem="pw")
        prevq[dc] = m_o
    m_prepq = prevq[7]
    if upto <= 2:
        ctx_close_out(P, nc, ctx, out, x_ar, xsl, dbg_t, dma, vop, act, locals())
        return nc

    P.barrier()

    def emit_layernorm():
        outs = []
        pfree = None
        for c in range(NCH):
            m_cv = None
            for g in range(NG):
                m_cv = act(arA[:, g * CW:(g + 1) * CW], xsl(g, c, 0, CW), AF.Copy,
                           inc=(g == NG - 1))
            m_sq = None
            for g in range(NG):
                m_sq = act(arB[:, g * CW:(g + 1) * CW], arA[:, g * CW:(g + 1) * CW],
                           AF.Square, inc=(g == NG - 1))
            m_s1 = None
            for w0, w1 in [(0, 512), (512, 513)]:
                for g in range(NG):
                    m_s1 = mm(p_st[0:1, w0:w1], onesb[:, :],
                              arA[:, g * CW + w0:g * CW + w1],
                              g == 0, g == NG - 1, deps=[m_cv, pfree],
                              inc=(g == NG - 1 and w0 == 512))
            m_s2 = None
            for w0, w1 in [(0, 512), (512, 513)]:
                for g in range(NG):
                    m_s2 = mm(p_st[0:1, 1024 + w0:1024 + w1], onesb[:, :],
                              arB[:, g * CW + w0:g * CW + w1],
                              g == 0, g == NG - 1, deps=[m_sq, pfree],
                              inc=(g == NG - 1 and w0 == 512))
            m_mean = vop(lambda e_: e_.tensor_scalar(
                row_ar[0:1, 0:CW], p_st[0:1, 0:CW], 1.0 / 1024.0, None, ALU.mult),
                deps=[m_s1])
            m_e2 = vop(lambda e_: e_.tensor_scalar(
                row_ar[64:65, 0:CW], p_st[0:1, 1024:1024 + CW], 1.0 / 1024.0, None,
                ALU.mult), deps=[m_s2])
            vop(lambda e_: e_.tensor_tensor(
                row_ar[64:65, CW:2 * CW], row_ar[0:1, 0:CW], row_ar[0:1, 0:CW],
                ALU.mult), inc=False)
            m_var = vop(lambda e_: e_.tensor_tensor(
                scr_f[0:1, 0:CW], row_ar[64:65, 0:CW], row_ar[64:65, CW:2 * CW],
                ALU.subtract))
            m_ln = act(scr_f[0:1, CW:2 * CW], scr_f[0:1, 0:CW], AF.Ln,
                       deps=[m_var], bias=epsc[0:1, 0:1])
            m_rstd = act(row_ar[0:1, CW:2 * CW], scr_f[0:1, CW:2 * CW], AF.Exp,
                         scale=-0.5, settle=True)
            m_bm = None
            for w0, w1 in [(0, 512), (512, 513)]:
                m_bm = mm(p_st[:, w0:w1], ones_f[0:1, :], row_ar[0:1, w0:w1], True, True,
                          deps=[m_mean, m_e2], inc=(w0 == 512))
            m_br = None
            for w0, w1 in [(0, 512), (512, 513)]:
                m_br = mm(p_st[:, 1024 + w0:1024 + w1], ones_f[0:1, :],
                          row_ar[0:1, CW + w0:CW + w1], True, True,
                          deps=[m_rstd], inc=(w0 == 512))
            m_xn = None
            for g in range(NG):
                vop(lambda e_, g=g, c=c: e_.tensor_tensor(
                    scr_f[:, 0:CW], xsl(g, c, 0, CW), p_st[:, 0:CW], ALU.subtract),
                    deps=[m_bm, m_br], inc=False)
                m_xn = vop(lambda e_, g=g, c=c: e_.tensor_tensor(
                    xnsl(g, c, 0, CW), scr_f[:, 0:CW], p_st[:, 1024:1024 + CW],
                    ALU.mult), inc=(g == NG - 1))
            pfree = m_xn
            outs.append(m_xn)
        return outs

    xn_ms = emit_layernorm()
    if upto <= 3:
        ctx_close_out(P, nc, ctx, out, x_ar, xsl, dbg_t, dma, vop, act, locals())
        return nc

    P.barrier()
    m_dt = []
    for n2 in range(2):
        m_dt.append(act(sh8[:, 1024 + n2 * SL:1024 + (n2 + 1) * SL], iotab[:, :],
                        AF.Exp, deps=[m_iota, m_L], scale=vec_sb[:, 62 + n2:63 + n2],
                        settle=True))
    act(sum_sb[:, 64:66], vec_sb[:, 62:64], AF.Copy, deps=[m_L], scale=512.0)
    m_d512 = act(sum_sb[:, 64:66], sum_sb[:, 64:66], AF.Exp, settle=True)

    m_scan = {}
    scr_guard = [None]
    fr = dict(a0=None, a1=None, m0=None, m1=None)
    wload = {}
    wfree = {}
    nload = [0]

    def wload_next(src_ap, deps=()):
        i = nload[0]
        nload[0] += 1
        sl_ = i % 3
        extra = []
        if i < 3:
            extra = [("pc0", P.cnt.get("pc0", 0)), ("pc1", P.cnt.get("pc1", 0))]
        wload[i] = dma(["sync", "gpsimd", "sync"][sl_],
                       wbufB[:, sl_ * 1024:(sl_ + 1) * 1024], src_ap,
                       sem=f"ws{sl_}", deps=list(deps) + [wfree.get(i - 3)] + extra)
        return i

    def wtile(i, k):
        sl_ = i % 3
        return wbufB[:, sl_ * 1024 + k * 128:sl_ * 1024 + (k + 1) * 128]

    for c in range(NCH):
        m_sq = None
        for g in range(NG):
            m_sq = act(arB[:, g * SL:(g + 1) * SL], xnsl(g, c, 1, SL), AF.Square,
                       deps=[xn_ms[c]] if g == 0 else (), inc=(g == NG - 1))
        m_ms = None
        for g in range(NG):
            m_ms = mm(p_a0[0:1, 0:SL], onesb[:, :], arB[:, g * SL:(g + 1) * SL],
                      g == 0, g == NG - 1, deps=[m_sq, fr["a0"]], inc=(g == NG - 1))
        m_msr = vop(lambda e_: e_.tensor_scalar(
            row_ar[64:65, 0:SL], p_a0[0:1, 0:SL], 1.0 / 1024.0, None, ALU.mult),
            deps=[m_ms])
        fr["a0"] = m_msr
        act(row_ar[64:65, CW:CW + SL], row_ar[64:65, 0:SL], AF.Ln,
            deps=[m_msr], bias=epsc[64:65, 1:2])
        m_rs2 = act(row_ar[64:65, 0:SL], row_ar[64:65, CW:CW + SL], AF.Exp,
                    scale=-0.5, settle=True)
        m_b2 = mm(p_m1[:, 0:SL], ones_f[64:65, :], row_ar[64:65, 0:SL], True, True,
                  deps=[m_rs2, fr["m1"]], inc=True)
        m_xh = None
        for g in range(NG):
            m_xh = vop(lambda e_, g=g, c=c: e_.scalar_tensor_tensor(
                arA[:, g * SL:(g + 1) * SL], xnsl(g, c, 1, SL), vec_sb[:, 4 + g:5 + g],
                p_m1[:, 0:SL], ALU.mult, ALU.mult), deps=[m_b2],
                inc=(g == NG - 1), settle=(g == NG - 1))
        fr["m1"] = m_xh
        m_u = []
        for n2 in range(2):
            pp = p_a0 if n2 == 0 else p_a1
            mmx = None
            for g in range(NG):
                mmx = mm(pp[:, 0:SL], w_in_sb[:, g * 256 + n2 * 128:g * 256 + (n2 + 1) * 128],
                         arA[:, g * SL:(g + 1) * SL], g == 0, g == NG - 1,
                         deps=[m_xh, fr["a0"] if n2 == 0 else fr["a1"],
                               m_scan.get((c - 1, n2))], inc=(g == NG - 1))
            m_u.append(mmx)
        m_ue = []
        for n2 in range(2):
            pp = p_a0 if n2 == 0 else p_a1
            m_ue.append(vop(lambda e_, n2=n2, pp=pp: e_.tensor_scalar(
                scr_f[:, n2 * CW:n2 * CW + SL], pp[:, 0:SL],
                vec_sb[:, 0 + n2:1 + n2], None, ALU.add),
                deps=[m_u[n2], scr_guard[0]]))
            fr["a0" if n2 == 0 else "a1"] = m_ue[-1]
        if c == 0 and "dbg_u" in dbg_t:
            dma("sync", dbg_t["dbg_u"].ap(), scr_f[:, 0:2 * CW],
                deps=[m_ue[0], m_ue[1]])
        for n2 in range(2):
            dcol = vec_sb[:, 60 + n2:61 + n2]
            d0 = bass.AP(dcol.tensor, dcol.offset,
                         [[list(dcol.ap)[0][0], 128], [0, SL]])
            m_scan[(c, n2)] = vop(lambda e_, n2=n2, c=c, d0=d0: e_.tensor_tensor_scan(
                st_ar[:, (n2 * NCH + c) * SL:(n2 * NCH + c + 1) * SL], d0,
                scr_f[:, n2 * CW:n2 * CW + SL], 0.0, ALU.mult, ALU.add),
                deps=[m_ue[n2], m_dec])
        if c == NCH - 1:
            m_smc = None
            for n2 in range(2):
                for cc_ in range(NCH):
                    m_smc = vop(lambda e_, n2=n2, cc_=cc_: e_.tensor_copy(
                        sum_sb[:, cc_ * 2 + n2:cc_ * 2 + n2 + 1],
                        st_ar[:, (n2 * NCH + cc_ + 1) * SL - 1:(n2 * NCH + cc_ + 1) * SL]),
                        deps=[m_scan[(cc_, n2)]], inc=(n2 == 1 and cc_ == NCH - 1),
                        settle=(n2 == 1 and cc_ == NCH - 1))
            m_str = trans(p_m1[0:8, 0:128], sum_sb[:, 0:8], ident_fs[:, :],
                          deps=[m_smc, fr["m1"]])
            m_str = pesettle()
            m_sev = vop(lambda e_: e_.tensor_copy(sh8[0:8, 0:128], p_m1[0:8, 0:128]),
                        deps=[m_str], settle=True)
            fr["m1"] = m_sev
            m_sdma = dma("gpsimd", sum_in.ap(), sh8[0:8, 0:128], deps=[m_sev])
            m_sdma = dma("gpsimd", sum_in.ap(), sh8[0:8, 0:128], deps=[m_sdma])
            m_cc1 = P.emit("gpsimd", lambda e_: e_.collective_compute(
                "AllGather", ALU.bypass, replica_groups=[[0, 1, 2, 3], [4, 5, 6, 7]],
                ins=[sum_in.ap().opt()], outs=[sum_out.ap().opt()]),
                deps=[m_sdma], inc="cc")
            m_cc1 = P.emit("gpsimd", lambda e_: e_.collective_compute(
                "AllGather", ALU.bypass, replica_groups=[[0, 1, 2, 3], [4, 5, 6, 7]],
                ins=[sum_in.ap().opt()], outs=[sum_out.ap().opt()]),
                inc="cc")
        m_add = {}
        for m in range(8):
            i = wload_next(wc_l0.ap()[3, :, m * 1024:(m + 1) * 1024])
            pp = p_m0 if m % 2 == 0 else p_a0
            mmx = None
            for k in range(8):
                mmx = mm(pp[:, 0:SL], wtile(i, k), arA[:, k * SL:(k + 1) * SL],
                         k == 0, k == 7,
                         deps=[wload[i], m_xh, fr["m0"] if m % 2 == 0 else fr["a0"],
                               m_scan[(c, 0)] if m % 2 == 1 else None],
                         inc=(k == 7))
            wfree[i] = mmx
            m_add[m] = vop(lambda e_, m=m, c=c, pp=pp: e_.scalar_tensor_tensor(
                xsl(m, c, 1, SL), pp[:, 0:SL], vec_sb[:, 12 + m:13 + m],
                xsl(m, c, 1, SL), ALU.add, ALU.add), deps=[mmx])
            fr["m0" if m % 2 == 0 else "a0"] = m_add[m]
        for g in range(NG):
            vop(lambda e_, g=g, c=c: e_.tensor_tensor(
                xsl(g, c, 1, SL), xsl(g, c, 1, SL), xnsl(g, c, 1, SL), ALU.add),
                deps=[m_add[g]], inc=(g == NG - 1))
        m_mix = None
        for g in range(NG):
            vop(lambda e_, g=g, c=c: e_.tensor_tensor(
                arC[:, 0:SL], xnsl(g, c, 1, SL), xnsl(g, c, 0, SL), ALU.subtract),
                deps=[m_add[7], m_u[0], m_u[1]] if g == 0 else (), inc=False)
            vop(lambda e_, g=g, c=c: e_.scalar_tensor_tensor(
                arA[:, g * SL:(g + 1) * SL], arC[:, 0:SL], vec_sb[:, 20 + g:21 + g],
                xnsl(g, c, 0, SL), ALU.mult, ALU.add), inc=False)
            m_mix = vop(lambda e_, g=g, c=c: e_.scalar_tensor_tensor(
                arB[:, g * SL:(g + 1) * SL], arC[:, 0:SL], vec_sb[:, 28 + g:29 + g],
                xnsl(g, c, 0, SL), ALU.mult, ALU.add), inc=(g == NG - 1),
                settle=(g == NG - 1))
        m_r = {}
        for m in range(8):
            i = wload_next(wc_l0.ap()[0, :, m * 1024:(m + 1) * 1024])
            pp = p_m0 if m % 2 == 0 else p_a0
            mmx = None
            for k in range(8):
                mmx = mm(pp[:, 0:SL], wtile(i, k), arA[:, k * SL:(k + 1) * SL],
                         k == 0, k == 7,
                         deps=[wload[i], m_mix, fr["m0"] if m % 2 == 0 else fr["a0"]],
                         inc=(k == 7))
            wfree[i] = mmx
            m_r[m] = act(arC[:, 512 + m * SL:512 + (m + 1) * SL], pp[:, 0:SL],
                         AF.Sigmoid, deps=[mmx], settle=True)
            fr["m0" if m % 2 == 0 else "a0"] = m_r[m]
        m_k2 = {}
        for m in range(8):
            i = wload_next(wc_l0.ap()[1, :, m * 1024:(m + 1) * 1024])
            pp = p_m1 if m % 2 == 0 else p_a1
            mmx = None
            for k in range(8):
                mmx = mm(pp[:, 0:SL], wtile(i, k), arB[:, k * SL:(k + 1) * SL],
                         k == 0, k == 7,
                         deps=[wload[i], m_mix, m_r[7] if m <= 1 else None,
                               fr["m1"] if m % 2 == 0 else fr["a1"]],
                         inc=(k == 7))
            wfree[i] = mmx
            m_rl = vop(lambda e_, m=m, pp=pp: e_.tensor_scalar(
                arC[:, 4608 + (m % 2) * SL:4608 + (m % 2 + 1) * SL], pp[:, 0:SL],
                0.0, None, ALU.max), deps=[mmx, m_k2.get(m - 2)], settle=True)
            fr["m1" if m % 2 == 0 else "a1"] = m_rl
            m_k2[m] = act(arA[:, m * SL:(m + 1) * SL],
                          arC[:, 4608 + (m % 2) * SL:4608 + (m % 2 + 1) * SL],
                          AF.Square, deps=[m_rl], settle=True)
        for m in range(8):
            i = wload_next(wc_l0.ap()[2, :, m * 1024:(m + 1) * 1024])
            pp = p_m0 if m % 2 == 0 else p_a0
            mmx = None
            for k in range(8):
                mmx = mm(pp[:, 0:SL], wtile(i, k), arA[:, k * SL:(k + 1) * SL],
                         k == 0, k == 7,
                         deps=[wload[i], m_k2[7], fr["m0"] if m % 2 == 0 else fr["a0"]],
                         inc=(k == 7))
            wfree[i] = mmx
            m_t = vop(lambda e_, m=m, pp=pp: e_.tensor_tensor(
                scr_f[:, (m % 2) * CW:(m % 2) * CW + SL], pp[:, 0:SL],
                arC[:, 512 + m * SL:512 + (m + 1) * SL], ALU.mult), deps=[mmx])
            fr["m0" if m % 2 == 0 else "a0"] = m_t
            m_xvadd = vop(lambda e_, m=m, c=c: e_.tensor_tensor(
                xsl(m, c, 1, SL), xsl(m, c, 1, SL),
                scr_f[:, (m % 2) * CW:(m % 2) * CW + SL], ALU.add))
        scr_guard[0] = m_xvadd
    if upto <= 4:
        ctx_close_out(P, nc, ctx, out, x_ar, xsl, dbg_t, dma, vop, act, locals())
        return nc

    P.barrier()
    m_so = dma("sync", scr_f[0:32, 0:128], sum_out.ap(),
               deps=[m_cc1, scr_guard[0], m_sdma])
    m_str2 = trans(p_m0[:, 0:32], scr_f[0:32, 0:128], ident_fs[0:32, 0:32],
                   deps=[m_so, fr["m0"]])
    m_str2 = pesettle()
    m_sev2 = vop(lambda e_: e_.tensor_copy(sum_sb[:, 32:64], p_m0[:, 0:32]),
                 deps=[m_str2])
    fr["m0"] = m_sev2
    m_ro = None
    for g2 in range(2):
        for r in range(4):
            for s2 in range(NCH):
                gch = chunks_of(r)[s2]
                src = 32 + r * 8 + s2 * 2 + g2
                m_ro = vop(lambda e_, g2=g2, gch=gch, src=src: e_.tensor_copy(
                    sum_sb[:, (0 if g2 == 0 else 16) + gch:(0 if g2 == 0 else 16) + gch + 1],
                    sum_sb[:, src:src + 1]), deps=[m_sev2],
                    inc=(g2 == 1 and r == 3 and s2 == NCH - 1))
    m_isc = []
    for g2 in range(2):
        dcol = sum_sb[:, 64 + g2:65 + g2]
        d0 = bass.AP(dcol.tensor, dcol.offset, [[list(dcol.ap)[0][0], 128], [0, 16]])
        m_isc.append(vop(lambda e_, g2=g2, d0=d0: e_.tensor_tensor_scan(
            sum_sb[:, 32 + g2 * 16:48 + g2 * 16], d0,
            sum_sb[:, g2 * 16:16 + g2 * 16], 0.0, ALU.mult, ALU.add),
            deps=[m_ro, m_d512], settle=True))
    m_bo = [dma("gpsimd", bounce.ap()[:, g2 * 17 + 1:g2 * 17 + 17],
                sum_sb[:, 32 + g2 * 16:48 + g2 * 16], deps=[m_isc[g2]])
            for g2 in range(2)]
    m_bo = [dma("gpsimd", bounce.ap()[:, g2 * 17 + 1:g2 * 17 + 17],
                sum_sb[:, 32 + g2 * 16:48 + g2 * 16], deps=[m_bo[g2]])
            for g2 in range(2)]
    m_pick = {}
    for g2 in range(2):
        for c in range(NCH):
            idx = g2 * NCH + c
            P.emit("gpsimd", lambda e_, idx=idx: e_.reg_load(
                rS, spick_s[0:1, idx:idx + 1]), deps=[m_bo[g2], m_vec])
            m_pick[idx] = dma("gpsimd", sin_sb[:, idx:idx + 1],
                              bass.AP(bounce, rS, [[34, 128], [1, 1]]), slow=True)
    m_cor = {}
    for g2 in range(2):
        for c in range(NCH):
            idx = g2 * NCH + c
            m_cor[(c, g2)] = vop(lambda e_, g2=g2, c=c, idx=idx: e_.scalar_tensor_tensor(
                st_ar[:, (g2 * NCH + c) * SL:(g2 * NCH + c + 1) * SL],
                sh8[:, 1024 + g2 * SL:1024 + (g2 + 1) * SL],
                sin_sb[:, idx:idx + 1],
                st_ar[:, (g2 * NCH + c) * SL:(g2 * NCH + c + 1) * SL],
                ALU.mult, ALU.add), deps=[m_pick[idx], m_dt[g2]])
    for c in range(NCH):
        m_cast = None
        for g2 in range(2):
            m_cast = vop(lambda e_, g2=g2, c=c: e_.tensor_copy(
                arA[:, g2 * SL:(g2 + 1) * SL],
                st_ar[:, (g2 * NCH + c) * SL:(g2 * NCH + c + 1) * SL]),
                deps=[m_cor[(c, 0)], m_cor[(c, 1)]], inc=(g2 == 1),
                settle=(g2 == 1))
        for m in range(8):
            pp = p_m0 if m % 2 == 0 else p_a0
            mmx = None
            for n2 in range(2):
                mmx = mm(pp[:, 0:SL],
                         w_out_sb[:, n2 * 1024 + m * 128:n2 * 1024 + (m + 1) * 128],
                         arA[:, n2 * SL:(n2 + 1) * SL], n2 == 0, n2 == 1,
                         deps=[m_cast, fr["m0"] if m % 2 == 0 else fr["a0"]],
                         inc=(n2 == 1))
            m_ya = vop(lambda e_, m=m, c=c, pp=pp: e_.tensor_tensor(
                xsl(m, c, 1, SL), xsl(m, c, 1, SL), pp[:, 0:SL], ALU.add),
                deps=[mmx])
            fr["m0" if m % 2 == 0 else "a0"] = m_ya
    if upto <= 5:
        ctx_close_out(P, nc, ctx, out, x_ar, xsl, dbg_t, dma, vop, act, locals())
        return nc

    P.barrier()
    xn1_ms = emit_layernorm()

    P.barrier()
    fr = dict(a0=None, a1=None, m0=None, m1=None, ropeB=None)
    m_wq = {}
    m_kv_w = []
    m_ones = P.emit("gpsimd", lambda e_: e_.memset(arA[:, 2048:2568], 1.0),
                    inc="gc")
    fr["vdma0"] = fr["vdma1"] = m_ones
    for hp in range(NHP):
        sl_ = hp % 2
        m_wq[hp] = dma("sync", arC[:, sl_ * 3072:(sl_ + 1) * 3072],
                       wc_qkv.ap()[hp, :, :], deps=[m_wq.get(hp - 2), m_prepq],
                       sem="wq")
        for c in range(NCH):
            m_rope = dma("sync", arB[:, 2048:2560], ropec.ap()[c, :, :],
                         deps=[fr["ropeB"]], sem="wr")
            m_rope2 = dma("sync", arB[:, 2560:3072], ropes.ap()[c, :, :],
                          sem="wr")
            m_q = None
            m_k = None
            for g in range(NG):
                m_q = mm(p_a0[:, 0:SL],
                         arC[:, sl_ * 3072 + g * 384:sl_ * 3072 + g * 384 + 128],
                         xnsl(g, c, 1, SL), g == 0, g == NG - 1,
                         deps=[m_wq[hp], xn1_ms[c], fr["a0"]], inc=(g == NG - 1))
            for g in range(NG):
                m_k = mm(p_a1[:, 0:SL],
                         arC[:, sl_ * 3072 + g * 384 + 128:sl_ * 3072 + g * 384 + 256],
                         xnsl(g, c, 1, SL), g == 0, g == NG - 1,
                         deps=[m_wq[hp], xn1_ms[c], fr["a1"]], inc=(g == NG - 1))
            for qk in range(2):
                pp = p_a0 if qk == 0 else p_a1
                m_src = m_q if qk == 0 else m_k
                bias_c = vec_sb[:, (36 if qk == 0 else 44) + hp:
                                (37 if qk == 0 else 45) + hp]
                m_ev = act(arA[:, qk * 1024:qk * 1024 + SL], pp[:, 0:SL], AF.Identity,
                           deps=[m_src, fr.get("qdma" if qk == 0 else "kdma")],
                           bias=bias_c, settle=True)
                fr["a0" if qk == 0 else "a1"] = m_ev
                m_pm = mm(p_m0[:, 0:SL], pswap_s[:, :], arA[:, qk * 1024:qk * 1024 + SL],
                          True, True, deps=[m_ev, fr["m0"]], inc=True)
                m_t1 = vop(lambda e_, qk=qk: e_.tensor_tensor(
                    arA[:, qk * 1024 + 512:qk * 1024 + 1024],
                    arA[:, qk * 1024:qk * 1024 + 512], arB[:, 2048:2560], ALU.mult),
                    deps=[m_ev, m_rope, m_rope2])
                m_t2 = vop(lambda e_, qk=qk: e_.tensor_tensor(
                    arA[:, qk * 1024:qk * 1024 + 512], p_m0[:, 0:SL],
                    arB[:, 2560:3072], ALU.mult), deps=[m_pm])
                fr["m0"] = m_t2
                m_rr = vop(lambda e_, qk=qk: e_.tensor_tensor(
                    arA[:, qk * 1024:qk * 1024 + 512],
                    arA[:, qk * 1024:qk * 1024 + 512],
                    arA[:, qk * 1024 + 512:qk * 1024 + 1024], ALU.add),
                    deps=[m_t1, m_t2], settle=True)
                fr["ropeB"] = m_rr
                if qk == 0:
                    fr["qdma"] = dma("sync", qcache.ap()[hp, c, :, :],
                                     arA[:, 0:512], deps=[m_rr], sem="qw")
                    m_kv_w.append(fr["qdma"])
                else:
                    fr["kdma"] = dma(
                        "sync",
                        kv_kin.ap()[(hp * NCH + c) * 128:(hp * NCH + c + 1) * 128, :],
                        arA[:, 1024:1536], deps=[m_rr], sem="kw")
                    m_kv_w.append(fr["kdma"])
            m_ve2 = {}
            for tb in range(4):
                m_v = None
                for g in range(NG):
                    m_v = mm(p_m1[:, 0:128],
                             xnsl(g, c, 1 + tb * 128, 128),
                             arC[:, sl_ * 3072 + g * 384 + 256:sl_ * 3072 + g * 384 + 384],
                             g == 0, False,
                             deps=[m_wq[hp], xn1_ms[c], fr["m1"]], inc=False)
                m_v = mm(p_m1[:, 0:128], ones_bb[:, :],
                         vrow_b[:, hp * 128:(hp + 1) * 128], False, True,
                         deps=[m_bqv], inc=True)
                for h in range(2):
                    m_ve2[(tb, h)] = act(
                        arA[:, 2048 + h * 260 + tb * 65:2048 + h * 260 + tb * 65 + 64],
                        p_m1[:, h * 64:(h + 1) * 64], AF.Copy,
                        deps=[m_v, fr.get(f"vdma{h}")], settle=True)
                fr["m1"] = m_ve2[(tb, 1)]
            for h in range(2):
                fr[f"vdma{h}"] = dma(
                    "gpsimd",
                    kv_vin.ap()[((2 * hp + h) * NCH + c) * 128:
                                ((2 * hp + h) * NCH + c + 1) * 128, :],
                    arA[:, 2048 + h * 260:2048 + (h + 1) * 260],
                    deps=[m_ve2[(3, h)]], sem="vw")
                m_kv_w.append(fr[f"vdma{h}"])
    m_cc2 = P.emit("gpsimd", lambda e_: e_.collective_compute(
        "AllGather", ALU.bypass, replica_groups=[[0, 1, 2, 3], [4, 5, 6, 7]],
        ins=[kv_kin.ap().opt()], outs=[kv_kall.ap().opt()]),
        deps=m_kv_w, inc="cc")
    m_cc3 = P.emit("gpsimd", lambda e_: e_.collective_compute(
        "AllGather", ALU.bypass, replica_groups=[[0, 1, 2, 3], [4, 5, 6, 7]],
        ins=[kv_vin.ap().opt()], outs=[kv_vall.ap().opt()]),
        inc="cc")
    if upto <= 6:
        ctx_close_out(P, nc, ctx, out, x_ar, xsl, dbg_t, dma, vop, act, locals())
        return nc

    P.barrier()
    fr = dict(a0=None, a1=None, m0=None, m1=None)
    prev_cv = {}
    prev_mh = {}
    prev_ev = {}
    for hg in range(24):
        ping = hg % 2
        stgv = (stg[:, :].bitcast(I32) if ping == 0 else arB[:, :].bitcast(I32))
        m_lg = dma("sync", stgv[:, 0:1024], wg.ap()[hg * 128:(hg + 1) * 128, :],
                   deps=[prev_cv.get(hg - 2)], sem="gl")
        m_lu = dma("gpsimd", stgv[:, 1024:2048], wu.ap()[hg * 128:(hg + 1) * 128, :],
                   deps=[prev_cv.get(hg - 2)], sem="glv")
        m_cv = act(arA[:, ping * 2052:ping * 2052 + 2048], stgv[:, 0:2048],
                   AF.Copy, deps=[m_lg, m_lu, prev_mh.get(hg - 2)],
                   scale=1.0 / 127.0, settle=True)
        prev_cv[hg] = m_cv
        for rnd in range(2):
            pptI = p_m0 if rnd == 0 else p_m1
            m_tr = None
            for qq in range(8):
                q = rnd * 8 + qq
                m_tr = trans(
                    pptI[:, :].bitcast(BF16)[:, qq * 128:(qq + 1) * 128],
                    arA[:, ping * 2052 + q * 128:ping * 2052 + (q + 1) * 128],
                    ident_bs[:, :],
                    deps=[m_cv, prev_ev.get((hg - 1, rnd))])
            m_tr = pesettle()
            prev_ev[(hg, rnd)] = None
            for qq in range(8):
                q = rnd * 8 + qq
                last = (qq == 7)
                prev_ev[(hg, rnd)] = vop(lambda e_, q=q, qq=qq, ping=ping,
                                         pptI=pptI: e_.tensor_copy(
                    arC[:, ping * 2048 + q * 128:ping * 2048 + (q + 1) * 128],
                    pptI[:, :].bitcast(BF16)[:, qq * 128:(qq + 1) * 128]),
                    deps=[m_tr], inc=last, settle=last)
        m_w = prev_ev[(hg, 1)]
        for c in range(NCH):
            pg = p_a0[:, 0:SL] if c % 2 == 0 else p_st[:, 0:SL]
            pu = p_a1[:, 0:SL] if c % 2 == 0 else p_st[:, SL:2 * SL]
            gk, uk = f"g{c % 2}", f"u{c % 2}"
            m_g = None
            m_uu = None
            for g in range(NG):
                m_g = mm(pg, arC[:, ping * 2048 + g * 128:ping * 2048 + (g + 1) * 128],
                         xnsl(g, c, 1, SL), g == 0, g == NG - 1,
                         deps=[m_w, fr.get(gk)], inc=(g == NG - 1))
            for g in range(NG):
                m_uu = mm(pu,
                          arC[:, ping * 2048 + 1024 + g * 128:ping * 2048 + 1024 + (g + 1) * 128],
                          xnsl(g, c, 1, SL), g == 0, g == NG - 1,
                          deps=[m_w, fr.get(uk)], inc=(g == NG - 1))
            m_si = act(arA[:, ping * 2052 + (c % 2) * 512:ping * 2052 + (c % 2) * 512 + 512],
                       pg, AF.Silu, deps=[m_g, fr.get(f"mh{c % 2}")], settle=True)
            fr[gk] = m_si
            hsl = wbufB[:, c * 512:(c + 1) * 512]
            m_h = vop(lambda e_, hsl=hsl, ping=ping, c=c, pu=pu: e_.tensor_tensor(
                hsl, pu,
                arA[:, ping * 2052 + (c % 2) * 512:ping * 2052 + (c % 2) * 512 + 512],
                ALU.mult), deps=[m_si, m_uu, fr.get(f"hdma{c}")], settle=True)
            fr[uk] = m_h
            fr[f"mh{c % 2}"] = m_h
            m_use = dma("scalar", hcache.ap()[c, :, hg * SL:(hg + 1) * SL],
                        hsl, deps=[m_h], sem=f"hw{c}")
            fr[f"hdma{c}"] = m_use
        prev_mh[hg] = m_h
    if upto <= 7:
        ctx_close_out(P, nc, ctx, out, x_ar, xsl, dbg_t, dma, vop, act, locals())
        return nc

    P.barrier()
    m_mask = dma("sync", sh8[:, 0:2048], maskadd.ap())
    st_b = st_ar[:, :].bitcast(BF16)
    stg_b = stg[:, :].bitcast(BF16)
    kw_dep = [("pc2", 0), ("pc3", 0), ("pw", 0), ("gl", 0),
              ("hw0", 0), ("hw1", 0), ("hw2", 0), ("hw3", 0)]
    m_kw = dma("scalar", xn_ar[:, 0:12288], wc_wd.ap()[0, :, :], deps=kw_dep,
               sem="wd")
    dma("scalar", st_b[:, 0:8192], wc_wd.ap()[1, :, 0:8192], sem="wd")
    dma("scalar", stg_b[:, 0:4096], wc_wd.ap()[1, :, 8192:12288], sem="wd")
    dma("scalar", wbufB[:, 0:3072], wc_wo.ap()[0, :, 0:3072], sem="wd")
    dma("scalar", xn_ar[:, 12288:13312], wc_wo.ap()[0, :, 3072:4096], sem="wd")
    dma("scalar", stg_b[:, 4096:6144], wc_wo.ap()[1, :, 0:2048], sem="wd")
    m_kw = dma("scalar", xn_ar[:, 13312:15360], wc_wo.ap()[1, :, 2048:4096],
               sem="wd")
    m_jt = dma("sync", jot[:, :], jof.ap(), sem="jt")
    m_jt = dma("sync", tbb[:, :], tb2.ap(), sem="jt")
    qbufs = [qbuf, qbuf2]
    seq = []
    jb0 = 0
    for (s, hp, njobs) in GROUPS:
        for h in range(2):
            seq.append((s, hp, h, njobs, jb0))
        jb0 += njobs
    kb_use = [None, None]
    vb_use = [None, None]
    pb_use = [None, None]
    arB_use = None
    rrow_use = None
    exp_prev = [None, None]
    lastqk_item = {}
    acc_free = {0: None, 1: None}
    qload = {}

    def emit_qload(i):
        si, hpi, hi, _, _ = seq[i]
        qb_ = qbufs[i % 2]
        dma("sync", qb_[0:64, :], qcache.ap()[hpi, si, 64 * hi:64 * hi + 64, :],
            deps=[lastqk_item.get(i - 2)], sem=f"qb{i % 2}")
        return dma("sync", qb_[67:70, :], qaug.ap()[hpi, si, hi, :, :],
                   sem=f"qb{i % 2}")

    qload[0] = emit_qload(0)
    qload[1] = emit_qload(1)
    for i, (s, hp, h, njobs, jb0) in enumerate(seq):
        hh = 2 * hp + h
        pacc = p_a0 if h == 0 else p_a1
        qb_ = qbufs[i % 2]
        m_qload = qload[i]
        m_b = [dma("sync", kbuf[ii][64:67, :], beta.ap()[hh, :, :],
                   deps=[kb_use[ii]], sem="bb") for ii in range(2)]
        m_pv = None
        for jb in range(njobs):
            jidx = jb0 + jb
            bs = jb % 2
            m_k = P.emit("gpsimd", lambda e_, bs=bs, col=4 * jidx + h:
                         e_.indirect_dma_start(
                kbuf[bs][0:64, :], None, kv_kall.ap()[0:64, :],
                bass.IndirectOffsetOnAxis(ap=jot[0:64, col:col + 1], axis=0)),
                deps=[m_jt, kb_use[bs], ("cc", P.cnt["cc"])],
                inc="gd", incn=16)
            m_vg = P.emit("gpsimd", lambda e_, bs=bs, col=4 * jidx + 2 + h:
                          e_.indirect_dma_start(
                vbuf[bs][:, :], None, kv_vall.ap()[0:128, :],
                bass.IndirectOffsetOnAxis(ap=jot[:, col:col + 1], axis=0)),
                deps=[m_jt, vb_use[bs], ("cc", P.cnt["cc"])],
                inc="gd", incn=16)
            m_qkh = [None, None]
            for half in range(2):
                m_qk = None
                for kb in ((0, 1) if half == 0 else (2, 3)):
                    m_qk = mm(p_st[:, kb * SL:(kb + 1) * SL],
                              kbuf[bs][0:128, kb * 128:(kb + 1) * 128], qb_[0:128, :],
                              True, True,
                              deps=[m_k, m_qload, m_b[bs], exp_prev[half]],
                              inc=(kb in (1, 3)))
                m_qkh[half] = m_qk
            kb_use[bs] = m_qkh[1]
            lastqk_item[i] = m_qkh[1]
            m_mah = [None, None]
            if jb == 0:
                for half in range(2):
                    m_mah[half] = vop(lambda e_, half=half: e_.tensor_tensor(
                        p_st[:, half * 1024:(half + 1) * 1024],
                        p_st[:, half * 1024:(half + 1) * 1024],
                        sh8[:, half * 1024:(half + 1) * 1024], ALU.add),
                        deps=[m_qkh[half], m_mask], settle=True)
            m_exph = [None, None]
            for half in range(2):
                m_exph[half] = act(pbuf[bs][:, half * 1024:(half + 1) * 1024],
                                   p_st[:, half * 1024:(half + 1) * 1024], AF.Exp,
                                   deps=[m_qkh[half], m_mah[half], pb_use[bs], m_jt],
                                   bias=tbb[:, jidx * 2 + h:jidx * 2 + h + 1],
                                   scale=SC, settle=True)
                exp_prev[half] = m_exph[half]
            for half in range(2):
                for kb in ((0, 1) if half == 0 else (2, 3)):
                    m_pv = mm(pacc[0:65, 0:SL], vbuf[bs][:, kb * 65:(kb + 1) * 65],
                              pbuf[bs][:, kb * SL:(kb + 1) * SL],
                              (jb == 0 and kb == 0), (jb == njobs - 1 and kb == 3),
                              deps=[m_exph[half], m_vg,
                                    acc_free[h] if jb == 0 else None],
                              inc=(kb == 3))
            vb_use[bs] = m_pv
            pb_use[bs] = m_pv
        if i + 2 < len(seq):
            qload[i + 2] = emit_qload(i + 2)
        m_sx = vop(lambda e_, pacc=pacc: e_.tensor_copy(
            scr_f[0:1, 514:514 + SL], pacc[64:65, 0:SL]), deps=[m_pv, rrow_use])
        m_rb = mm(p_m0[0:64, 0:SL], ones_f[0:1, 0:64], scr_f[0:1, 514:514 + SL],
                  True, True, deps=[m_sx, fr["m0"]], inc=True)
        rrow_use = m_rb
        m_rcb = vop(lambda e_: e_.reciprocal(
            scr_f[0:64, 0:SL], p_m0[0:64, 0:SL]), deps=[m_rb], settle=True)
        fr["m0"] = m_rcb
        m_at = vop(lambda e_, pacc=pacc: e_.tensor_tensor(
            arB[0:64, 0:SL], pacc[0:64, 0:SL], scr_f[0:64, 0:SL], ALU.mult),
            deps=[m_rcb, arB_use], settle=True)
        acc_free[h] = m_at
        arB_use = dma("sync", acache.ap()[hp, s, 64 * h:64 * h + 64, :],
                      arB[0:64, 0:SL], deps=[m_at], sem="aw")
    if upto <= 8:
        ctx_close_out(P, nc, ctx, out, x_ar, xsl, dbg_t, dma, vop, act, locals())
        return nc

    P.barrier()
    fr = dict(m0=None, m1=None)
    scr_b = scr_f[:, :].bitcast(BF16)

    def wd_t(mi, k):
        idx = (mi % 4) * 24 + k
        if mi < 4:
            return xn_ar[:, idx * 128:(idx + 1) * 128]
        if idx < 64:
            return st_b[:, idx * 128:(idx + 1) * 128]
        return stg_b[:, (idx - 64) * 128:(idx - 63) * 128]

    def wo_t(mi, hp):
        if mi < 3:
            return wbufB[:, mi * 1024 + hp * 128:mi * 1024 + (hp + 1) * 128]
        if mi == 3:
            return xn_ar[:, 12288 + hp * 128:12288 + (hp + 1) * 128]
        if mi < 6:
            return stg_b[:, 4096 + (mi - 4) * 1024 + hp * 128:
                         4096 + (mi - 4) * 1024 + (hp + 1) * 128]
        return xn_ar[:, 13312 + (mi - 6) * 1024 + hp * 128:
                     13312 + (mi - 6) * 1024 + (hp + 1) * 128]

    def hct(k):
        if k < 12:
            return arC[:, k * 512:(k + 1) * 512]
        if k < 20:
            return pbuf[(k - 12) // 4][:, ((k - 12) % 4) * 512:((k - 12) % 4 + 1) * 512]
        return [qbuf[:, 0:512], kbuf[0][:, 0:512], kbuf[1][:, 0:512],
                scr_b[:, 0:512]][k - 20]

    def act_t(hp):
        return arA[:, hp * 512:(hp + 1) * 512]

    m_last_mm = None
    m_hr = m_h2 = None
    for c in range(NCH):
        ex = ([("hw0", 0), ("hw1", 0), ("hw2", 0), ("hw3", 0), ("aw", 0)]
              if c == 0 else [])
        for hp in range(NHP):
            m_h2 = dma("gpsimd", act_t(hp), acache.ap()[hp, c, :, :],
                       deps=(ex + [m_last_mm]) if hp == 0 else (), sem="h2")
        for k in range(24):
            q, s = (("sync", "hr") if (k < 12 or k >= 20) else ("gpsimd", "h2"))
            m = dma(q, hct(k), hcache.ap()[c, :, k * SL:(k + 1) * SL],
                    deps=(ex + [m_last_mm]) if k in (0, 12) else (), sem=s)
            if s == "hr":
                m_hr = m
            else:
                m_h2 = m
        for mi in range(8):
            pp = p_m0 if mi % 2 == 0 else p_m1
            frk = "m0" if mi % 2 == 0 else "m1"
            mmx = None
            for hp in range(NHP):
                mmx = mm(pp[:, 0:SL], wo_t(mi, hp), act_t(hp),
                         hp == 0, False,
                         deps=[m_h2, m_kw, fr[frk]], inc=False)
            for k in range(24):
                mmx = mm(pp[:, 0:SL], wd_t(mi, k), hct(k), False, k == 23,
                         deps=([m_hr, m_h2] if k == 0 else ()), inc=(k == 23))
            m_x2 = vop(lambda e_, mi=mi, c=c, pp=pp: e_.scalar_tensor_tensor(
                xsl(mi, c, 1, SL), pp[:, 0:SL],
                vec_sb[:, 52 + mi:53 + mi],
                xsl(mi, c, 1, SL), ALU.add, ALU.add),
                deps=[mmx])
            fr[frk] = m_x2
            m_last_mm = mmx
    P.barrier()
    m_out = []
    oev = {}
    for it in range(NCH * 4):
        c, tb = divmod(it, 4)
        half = it % 2
        m_tr3 = None
        for g in range(NG):
            m_tr3 = trans(p_st[:, half * 1024 + g * 128:half * 1024 + (g + 1) * 128],
                          xsl(g, c, 1 + tb * 128, 128), ident_fs[:, :],
                          deps=[oev.get(it - 2)])
        m_tr3 = pesettle()
        m_cp = None
        for g in range(NG):
            m_cp = vop(lambda e_, g=g, half=half: e_.tensor_copy(
                stg[:, half * 1024 + g * 128:half * 1024 + (g + 1) * 128],
                p_st[:, half * 1024 + g * 128:half * 1024 + (g + 1) * 128]),
                deps=[m_tr3, m_out[it - 2] if it >= 2 else None], inc=(g == NG - 1),
                settle=(g == NG - 1))
        oev[it] = m_cp
        m_out.append(dma(["sync", "gpsimd"][half], out.ap()[c, tb * 128:(tb + 1) * 128, :],
                         stg[:, half * 1024:half * 1024 + 1024], deps=[m_cp],
                         sem=f"ow{half}"))
    P.emit("sync", lambda e_: e_.nop(), deps=[m_out[-1], m_out[-2]])
    P.emit("gpsimd", lambda e_: e_.nop(), deps=[("cc", P.cnt["cc"])])
    if dbg_t:
        ctx_close_out(P, nc, ctx, out, x_ar, xsl, dbg_t, dma, vop, act, locals())
        return nc
    P.barrier()

    with nc.Block() as block:
        P.replay(block)
    ctx.close()
    return nc


def ctx_close_out(P, nc, ctx, out, x_ar, xsl, dbg_t, dma, vop, act, env):
    P.barrier()
    srcs = {"dbg_x": env["x_ar"], "dbg_xn": env["xn_ar"], "dbg_st": env["st_ar"],
            "dbg_row": env["row_ar"], "dbg_sum": env["sum_sb"],
            "dbg_arA": env["arA"], "dbg_arB": env["arB"], "dbg_arC": env["arC"],
            "dbg_scr": env["scr_f"], "dbg_sin": env["sin_sb"],
            "dbg_wi": env["w_in_sb"], "dbg_wo": env["w_out_sb"]}
    dram_srcs = {"dbg_ac": "acache", "dbg_h": "hcache", "dbg_q": "qcache",
                 "dbg_kall": "kv_kall", "dbg_vall": "kv_vall",
                 "dbg_kin": "kv_kin", "dbg_sout": "sum_out",
                 "dbg_wc0": "wc_l0", "dbg_wcd": "wc_wd",
                 "dbg_wco": "wc_wo", "dbg_wcq": "wc_qkv"}
    m_last = None
    if P.cnt.get("cc", 0):
        P.emit("sync", lambda e_: e_.nop(), deps=[("cc", P.cnt["cc"])])
    for nm, t in dbg_t.items():
        if nm == "dbg_u":
            continue
        if nm in dram_srcs:
            s_t = env[dram_srcs[nm]]
            m_last = dma("sync", t.ap().flatten(), s_t.ap().flatten())
        else:
            src = srcs[nm]
            w = t.ap().shape[1]
            m_last = dma("sync", t.ap(), src[:, 0:w])
    if m_last is not None:
        P.emit("sync", lambda e_: e_.nop(), deps=[m_last])
    if P.cnt.get("cc", 0):
        P.emit("gpsimd", lambda e_: e_.nop(), deps=[("cc", P.cnt["cc"])])
    P.barrier()
    with nc.Block() as block:
        P.replay(block)
    ctx.close()


def kernel(**inputs):
    shared = build_shared_consts()
    nc = build_program(upto=99)
    in_maps = [build_core_inputs(c, inputs, shared) for c in range(8)]
    res = run_bass_kernel_spmd(nc, in_maps, core_ids=list(range(8)))
    outp = np.zeros((2, 16 * SL, D), dtype=np.float32)
    for core in range(8):
        b, j = divmod(core, 4)
        for s, g in enumerate(chunks_of(j)):
            outp[b, g * SL:(g + 1) * SL, :] = res.results[core]["out"][s]
    return outp


# version 17
# speedup vs baseline: 1.1472x; 1.0026x over previous
import math
import numpy as np
import ml_dtypes
import concourse.bass as bass
import concourse.mybir as mybir
from concourse.bass_utils import run_bass_kernel_spmd

F32 = mybir.dt.float32
BF16 = mybir.dt.bfloat16
I32 = mybir.dt.int32
AF = mybir.ActivationFunctionType
ALU = mybir.AluOpType
AX = mybir.AxisListType.X
BF = ml_dtypes.bfloat16

D = 1024
NG = 8
SL = 512
NCH = 4
CW = SL + 1
NH = 16
NHP = 8
SC = 1.0 / math.sqrt(192.0)
SQ192 = math.sqrt(192.0)
NB = [2, 2, 2, 2, 2, 2, 3, 3, 4, 5, 7, 9, 12, 16, 16, 16]
NBP = [max(NB[2 * i], NB[2 * i + 1]) for i in range(8)]
SLOT_CAP = [4, 8, 12, 16]
ENG_NAMES = ["sync", "gpsimd", "tensor", "scalar", "vector"]


def slope8(h):
    return 8.0 * 2.0 ** (-(h + 1) / 2.0)


def chunks_of(j):
    return [j, 7 - j, 8 + j, 15 - j]


def owner_of(g):
    j = min(g, 7 - g) if g < 8 else min(g - 8, 15 - g)
    return j, chunks_of(j).index(g)


def group_struct():
    gs = []
    for s in range(NCH):
        for hp in range(NHP):
            gs.append((s, hp, min(SLOT_CAP[s], NBP[hp])))
    return gs


GROUPS = group_struct()
NJOBS = sum(n for _, _, n in GROUPS)


def make_jobs(j):
    jobs = []
    ch = chunks_of(j)
    for s, hp, njobs in GROUPS:
        pos = ch[s]
        for jb in range(njobs):
            kch = max(pos - jb, 0)
            r, s2 = owner_of(kch)
            jobs.append(dict(s=s, hp=hp, jb=jb, kch=kch, pad=(jb > pos),
                             rank=r, kslot=s2))
    return jobs


def split3(x):
    a = x.astype(BF)
    r = x - a.astype(np.float32)
    b = r.astype(BF)
    c = (r - b.astype(np.float32)).astype(BF)
    return a, b, c


_SHARED = None


def build_shared_consts():
    global _SHARED
    if _SHARED is not None:
        return _SHARED
    c = {}
    c["ident_f"] = np.eye(128, dtype=np.float32)
    c["ident_b"] = np.eye(128, dtype=BF)
    P = np.zeros((128, 128), dtype=np.float32)
    for m in range(128):
        blk, r = divmod(m, 32)
        P[[1, 0, 3, 2][blk] * 32 + r, m] = 1.0
    c["pswap"] = P.astype(BF)
    mk = np.zeros((128, 4 * SL), dtype=np.float32)
    for kb in range(4):
        for p in range(128):
            kl = kb * 128 + p
            mk[p, kb * SL:(kb + 1) * SL][np.arange(SL) < kl] = -4e9
    c["maskadd"] = mk
    bet = np.zeros((NH, 3, SL), dtype=BF)
    t = np.arange(SL, dtype=np.float32)
    for h in range(NH):
        bet[h, 0], bet[h, 1], bet[h, 2] = split3(slope8(h) * t * SQ192)
    c["beta"] = bet
    inv = 1.0 / (10000.0 ** (np.arange(0, 64, 2, dtype=np.float32) / 64))
    fr = np.outer(np.arange(16 * SL, dtype=np.float32), inv)
    sgn = np.concatenate([-np.ones(32), np.ones(32), -np.ones(32), np.ones(32)])
    c["cos_full"] = np.tile(np.cos(fr).T, (4, 1)).astype(BF)
    c["sin_full"] = (np.tile(np.sin(fr).T, (4, 1)) * sgn[:, None]).astype(BF)
    _SHARED = c
    return c


def build_core_inputs(core, inp, shared):
    b, j = divmod(core, 4)
    ch = chunks_of(j)
    d = {}
    tok = np.asarray(inp["L0_attn_tokens"])
    d["tok"] = np.ascontiguousarray(
        np.stack([tok[b, g * SL:(g + 1) * SL, :] for g in ch], axis=0))
    halo = np.zeros((NCH, D), dtype=np.float32)
    for s, g in enumerate(ch):
        if g > 0:
            halo[s] = tok[b, g * SL - 1, :]
    d["halo"] = halo

    def dmaj(v, ng):
        return np.ascontiguousarray(np.asarray(v).reshape(ng, 128).T)
    d["b_in"] = dmaj(inp["L0_attn_b_in"], 2)
    d["ldec"] = dmaj(inp["L0_attn_log_decay"], 2)
    d["normw"] = dmaj(inp["L0_attn_norm_w"], 8)
    d["bsum"] = np.ascontiguousarray(
        (np.asarray(inp["L0_attn_b_out"]).reshape(8, 128)
         + np.asarray(inp["L0_attn_b_skip"]).reshape(8, 128)).T)
    d["mur"] = dmaj(inp["L0_ffn_mu_r"], 8)
    d["muk"] = dmaj(inp["L0_ffn_mu_k"], 8)
    d["bout1"] = dmaj(inp["L1_attn_b_out"], 8)
    bq = np.asarray(inp["L1_attn_b_qkv"])
    qb = np.zeros((128, 8), np.float32)
    kb_ = np.zeros((128, 8), np.float32)
    vb = np.zeros((8, 128), np.float32)
    for hp in range(8):
        qb[0:64, hp] = bq[2 * hp, 0:64]
        qb[64:128, hp] = bq[2 * hp + 1, 0:64]
        kb_[0:64, hp] = bq[2 * hp, 64:128]
        kb_[64:128, hp] = bq[2 * hp + 1, 64:128]
        vb[hp, 0:64] = bq[2 * hp, 128:192]
        vb[hp, 64:128] = bq[2 * hp + 1, 128:192]
    d["bq_q"], d["bq_k"] = qb, kb_
    d["bq_v"] = np.ascontiguousarray(vb.reshape(1, 8 * 128))
    for nm, key in [("W_in", "L0_attn_W_in"), ("W_out", "L0_attn_W_out"),
                    ("W_skip", "L0_attn_W_skip"), ("W_r", "L0_ffn_W_r"),
                    ("W_k", "L0_ffn_W_k"), ("W_v", "L0_ffn_W_v"),
                    ("wg", "L1_ffn_w_gate"), ("wu", "L1_ffn_w_up"),
                    ("wd", "L1_ffn_w_down")]:
        d[nm] = np.asarray(inp[key])
    d["wqkv"] = np.ascontiguousarray(np.asarray(inp["L1_attn_w_qkv"]).reshape(D, NH * 192))
    d["wout1"] = np.ascontiguousarray(np.asarray(inp["L1_attn_w_out"]).reshape(NH * 64, D))
    for k in ("ident_f", "ident_b", "pswap", "maskadd", "beta"):
        d[k] = shared[k]
    cs = np.zeros((NCH, 128, SL), BF)
    sn = np.zeros((NCH, 128, SL), BF)
    for s, g in enumerate(ch):
        cs[s] = shared["cos_full"][:, g * SL:(g + 1) * SL]
        sn[s] = shared["sin_full"][:, g * SL:(g + 1) * SL]
    d["ropec"], d["ropes"] = cs, sn
    jobs = make_jobs(j)
    jof = np.zeros((128, NJOBS * 4), np.int32)
    tb2 = np.zeros((128, NJOBS * 2), np.float32)
    prow = np.arange(128, dtype=np.int32)
    for idx, jb in enumerate(jobs):
        for h in range(2):
            kr = ((jb["rank"] * NHP + jb["hp"]) * NCH + jb["kslot"]) * 128 + 64 * h
            vr = ((jb["rank"] * NH + 2 * jb["hp"] + h) * NCH + jb["kslot"]) * 128
            jof[:, idx * 4 + h] = kr + prow
            jof[:, idx * 4 + 2 + h] = vr + prow
            tb2[:, idx * 2 + h] = (-1e9 if jb["pad"]
                                   else slope8(2 * jb["hp"] + h) * (jb["kch"] * SL)) - 16.0
    d["jof"], d["tb2"] = jof, tb2
    qaug = np.zeros((NHP, NCH, 2, 3, SL), BF)
    for hp in range(NHP):
        for s in range(NCH):
            q = (ch[s] * SL + np.arange(SL)).astype(np.float32)
            for h in range(2):
                a, bb, cc = split3(-slope8(2 * hp + h) * q * SQ192)
                qaug[hp, s, h, 0], qaug[hp, s, h, 1], qaug[hp, s, h, 2] = a, bb, cc
    d["qaug"] = qaug
    sp = np.zeros((1, 8), np.int32)
    for g in range(2):
        for s in range(NCH):
            sp[0, g * NCH + s] = g * 17 + ch[s]
    d["spick"] = sp
    return d


class Prog:
    def __init__(self, nc, ctx):
        self.nc = nc
        self.ctx = ctx
        self.ops = {n: [] for n in ENG_NAMES}
        self.h = {}
        self.cnt = {}
        self.owner = {}
        self.lastwait = {n: {} for n in ENG_NAMES}
        self.dma_sems = {"sd", "gd"}

    def sem(self, name, owner=None):
        if name not in self.h:
            self.h[name] = self.ctx.enter_context(self.nc.semaphore(name))
            self.cnt[name] = 0
            self.owner[name] = owner
        return self.h[name]

    def emit(self, eng, fn, deps=(), inc=None, incn=1):
        for dep in deps:
            if dep is None:
                continue
            s, v = dep
            if self.owner.get(s) == eng and s not in self.dma_sems:
                continue
            if s in self.dma_sems:
                v = self.cnt[s]
            if self.lastwait[eng].get(s, -1) >= v:
                continue
            self.lastwait[eng][s] = v
            self.ops[eng].append(("w", (s, v)))
        if inc:
            self.cnt[inc] += incn
            self.ops[eng].append(("o", (fn, inc, incn)))
            return (inc, self.cnt[inc])
        self.ops[eng].append(("o", (fn, None, 0)))
        return None

    def barrier(self):
        self.sem("ph", owner=None)
        for n in ENG_NAMES:
            self.cnt["ph"] += 1
            self.ops[n].append(("o", (lambda e: e.nop(), "ph", 1)))
        v = self.cnt["ph"]
        for n in ENG_NAMES:
            self.lastwait[n]["ph"] = v
            self.ops[n].append(("w", ("ph", v)))

    def replay(self, block):
        for name in ENG_NAMES:
            mine = self.ops[name]

            def sec(eng, mine=mine, name=name):
                for oi, (kind, payload) in enumerate(mine):
                    if kind == "w":
                        s, v = payload
                        eng.wait_ge(self.h[s], v)
                    else:
                        fn, inc, incn = payload
                        try:
                            inst = fn(eng)
                        except Exception as ex:
                            raise RuntimeError(f"replay fail {name} op {oi}: {ex}") from ex
                        if inc is not None:
                            inst.then_inc(self.h[inc], incn)
            getattr(block, name)(sec)


def build_program(upto=99, dbg=()):
    from contextlib import ExitStack
    nc = bass.Bass("TRN2", target_bir_lowering=False, detect_race_conditions=False)
    ctx = ExitStack()
    e = ctx.enter_context

    def din(name, shape, dt):
        return nc.dram_tensor(name, list(shape), dt, kind="ExternalInput")

    tok = din("tok", (NCH, SL, D), F32)
    halo = din("halo", (NCH, D), F32)
    b_in = din("b_in", (128, 2), F32)
    ldec = din("ldec", (128, 2), F32)
    normw = din("normw", (128, 8), F32)
    bsum = din("bsum", (128, 8), F32)
    mur = din("mur", (128, 8), F32)
    muk = din("muk", (128, 8), F32)
    bout1 = din("bout1", (128, 8), F32)
    bq_q = din("bq_q", (128, 8), F32)
    bq_k = din("bq_k", (128, 8), F32)
    bq_v = din("bq_v", (1, 8 * 128), F32)
    W_in = din("W_in", (256, D), F32)
    W_out = din("W_out", (D, 256), F32)
    W_skip = din("W_skip", (D, D), F32)
    W_r = din("W_r", (D, D), F32)
    W_k = din("W_k", (D, D), F32)
    W_v = din("W_v", (D, D), F32)
    wqkv = din("wqkv", (D, NH * 192), F32)
    wout1 = din("wout1", (NH * 64, D), F32)
    wg = din("wg", (3072, D), I32)
    wu = din("wu", (3072, D), I32)
    wd = din("wd", (D, 3072), I32)
    ident_f = din("ident_f", (128, 128), F32)
    ident_b = din("ident_b", (128, 128), BF16)
    pswap = din("pswap", (128, 128), BF16)
    maskadd = din("maskadd", (128, 4 * SL), F32)
    beta = din("beta", (NH, 3, SL), BF16)
    ropec = din("ropec", (NCH, 128, SL), BF16)
    ropes = din("ropes", (NCH, 128, SL), BF16)
    jof = din("jof", (128, NJOBS * 4), I32)
    tb2 = din("tb2", (128, NJOBS * 2), F32)
    qaug = din("qaug", (NHP, NCH, 2, 3, SL), BF16)
    spick = din("spick", (1, 8), I32)

    out = nc.dram_tensor("out", [NCH, SL, D], F32, kind="ExternalOutput")
    dbg_t = {}
    for nm, shape, dt in dbg:
        dbg_t[nm] = nc.dram_tensor(nm, list(shape), dt, kind="ExternalOutput")

    wc_l0 = nc.dram_tensor("wc_l0", [4, 128, 8 * 8 * 128], BF16)
    wc_wd = nc.dram_tensor("wc_wd", [2, 128, 4 * 24 * 128], BF16)
    wc_wo = nc.dram_tensor("wc_wo", [2, 128, 4 * 8 * 128], BF16)
    wc_qkv = nc.dram_tensor("wc_qkv", [NHP, 128, 8 * 384], BF16)
    qcache = nc.dram_tensor("qcache", [NHP, NCH, 128, SL], BF16)
    acache = nc.dram_tensor("acache", [NHP, NCH, 128, SL], BF16)
    hcache = nc.dram_tensor("hcache", [NCH, 128, 24 * SL], BF16)
    sum_in = nc.dram_tensor("sum_in", [8, 128], F32)
    sum_out = nc.dram_tensor("sum_out", [32, 128], F32)
    kv_kin = nc.dram_tensor("kv_kin", [NHP * NCH * 128, SL], BF16)
    kv_vin = nc.dram_tensor("kv_vin", [NH * NCH * 128, 260], BF16)
    kv_kall = nc.dram_tensor("kv_kall", [4 * NHP * NCH * 128, SL], BF16)
    kv_vall = nc.dram_tensor("kv_vall", [4 * NH * NCH * 128, 260], BF16)
    bounce = nc.dram_tensor("bounce", [128, 34], F32)

    x_ar = e(nc.sbuf_tensor("x_ar", [128, NG * NCH * CW], F32))
    xn_ar = e(nc.sbuf_tensor("xn_ar", [128, NG * NCH * CW], BF16))
    arA = e(nc.sbuf_tensor("arA", [128, NG * CW], BF16))
    arB = e(nc.sbuf_tensor("arB", [128, NG * CW], BF16))
    arC = e(nc.sbuf_tensor("arC", [128, 6144], BF16))
    st_ar = e(nc.sbuf_tensor("st_ar", [128, 2 * NCH * SL], F32))
    row_ar = e(nc.sbuf_tensor("row_ar", [128, 2 * CW], F32))
    scr_f = e(nc.sbuf_tensor("scr_f", [128, 2 * CW], F32))
    stg = e(nc.sbuf_tensor("stg", [128, 3072], F32))
    wbufB = e(nc.sbuf_tensor("wbufB", [128, 3072], BF16))
    vec_sb = e(nc.sbuf_tensor("vec_sb", [128, 66], F32))
    vrow_b = e(nc.sbuf_tensor("vrow_b", [1, 8 * 128], BF16))
    ident_fs = e(nc.sbuf_tensor("ident_fs", [128, 128], F32))
    ident_bs = e(nc.sbuf_tensor("ident_bs", [128, 128], BF16))
    pswap_s = e(nc.sbuf_tensor("pswap_s", [128, 128], BF16))
    sh8 = e(nc.sbuf_tensor("sh8", [128, 2048], F32))
    iotab = e(nc.sbuf_tensor("iotab", [128, SL], F32))
    onesb = e(nc.sbuf_tensor("onesb", [128, 1], BF16))
    ones_f = e(nc.sbuf_tensor("ones_f", [65, 128], F32))
    ones_bb = e(nc.sbuf_tensor("ones_bb", [1, 128], BF16))
    sum_sb = e(nc.sbuf_tensor("sum_sb", [128, 72], F32))
    spick_s = e(nc.sbuf_tensor("spick_s", [1, 8], I32))
    jot = e(nc.sbuf_tensor("jot", [128, NJOBS * 4], I32))
    tbb = e(nc.sbuf_tensor("tbb", [128, NJOBS * 2], F32))
    sin_sb = e(nc.sbuf_tensor("sin_sb", [128, 8], F32))
    w_in_sb = e(nc.sbuf_tensor("w_in_sb", [128, 8 * 256], BF16))
    w_out_sb = e(nc.sbuf_tensor("w_out_sb", [128, 2 * 1024], BF16))
    kbuf = [e(nc.sbuf_tensor(f"kbuf{i}", [128, SL], BF16)) for i in range(2)]
    vbuf = [e(nc.sbuf_tensor(f"vbuf{i}", [128, 4 * 65], BF16)) for i in range(2)]
    qbuf = e(nc.sbuf_tensor("qbuf", [128, SL], BF16))
    qbuf2 = e(nc.sbuf_tensor("qbuf2", [128, SL], BF16))
    pbuf = [e(nc.sbuf_tensor(f"pbuf{i}", [128, 4 * SL], BF16)) for i in range(2)]
    epsc = e(nc.sbuf_tensor("epsc", [128, 2], F32))

    p_st = e(nc.psum_tensor("p_st", [128, 4 * SL], F32))
    p_a0 = e(nc.psum_tensor("p_a0", [128, SL], F32))
    p_a1 = e(nc.psum_tensor("p_a1", [128, SL], F32))
    p_m0 = e(nc.psum_tensor("p_m0", [128, SL], F32))
    p_m1 = e(nc.psum_tensor("p_m1", [128, SL], F32))

    P = Prog(nc, ctx)
    for s_, o_ in [("sd", "sync"), ("gd", "gpsimd"), ("pe", "tensor"),
                   ("se", "scalar"), ("ve", "vector"), ("gc", "gpsimd"),
                   ("cc", None)]:
        P.sem(s_, owner=o_)

    rS = nc.gpsimd.alloc_register("rS")

    def xsl(g, c, c0, w):
        return x_ar[:, (g * NCH + c) * CW + c0:(g * NCH + c) * CW + c0 + w]

    def xnsl(g, c, c0, w):
        return xn_ar[:, (g * NCH + c) * CW + c0:(g * NCH + c) * CW + c0 + w]

    def dma(q, o, i, deps=(), slow=False, dbl=False, sem=None):
        kw = dict(allow_slow_non_contiguous=True) if slow else {}
        s = sem or ("sd" if q == "sync" else "gd")
        if s not in P.h:
            P.sem(s, owner=q)
            P.dma_sems.add(s)
        m = P.emit(q, lambda e_, o=o, i=i, kw=kw: e_.dma_start(out=o, in_=i, **kw),
                   deps=deps, inc=s, incn=16)
        if dbl:
            m = P.emit(q, lambda e_, o=o, i=i, kw=kw: e_.dma_start(out=o, in_=i, **kw),
                       inc=s, incn=16)
        return m

    def mm(pp, lhsT, rhs, start, stop, deps=(), inc=False):
        return P.emit("tensor", lambda e_, pp=pp, l=lhsT, r=rhs, s1=start, s2=stop:
                      e_.matmul(pp, l, r, start=s1, stop=s2), deps=deps,
                      inc=("pe" if inc else None))

    def act(out_, in_, f, deps=(), bias=None, scale=None, inc=True, settle=False):
        kw = {}
        if bias is not None:
            kw["bias"] = bias
        if scale is not None:
            kw["scale"] = scale
        m = P.emit("scalar", lambda e_, o=out_, i=in_, f=f, kw=kw:
                   e_.activation(o, i, f, **kw), deps=deps,
                   inc=("se" if (inc or settle) else None))
        return m

    def vop(fn, deps=(), inc=True, settle=False):
        m = P.emit("vector", fn, deps=deps,
                   inc=("ve" if (inc or settle) else None))
        return m

    def trans(out_, in_, ident, deps=()):
        return P.emit("tensor", lambda e_, o=out_, i=in_, d=ident:
                      e_.transpose(o, i, d), deps=deps, inc="pe")

    def pesettle():
        return P.emit("tensor", lambda e_: e_.nop(), inc="pe")

    m_vec = None
    for src, dst in [(b_in, vec_sb[:, 0:2]), (ldec, vec_sb[:, 2:4]),
                     (normw, vec_sb[:, 4:12]), (bsum, vec_sb[:, 12:20]),
                     (mur, vec_sb[:, 20:28]), (muk, vec_sb[:, 28:36]),
                     (bq_q, vec_sb[:, 36:44]), (bq_k, vec_sb[:, 44:52]),
                     (bout1, vec_sb[:, 52:60]),
                     (ident_f, ident_fs[:, :]), (ident_b, ident_bs[:, :]),
                     (pswap, pswap_s[:, :]), (spick, spick_s[:, :])]:
        m_vec = dma("sync", dst, src.ap())
    for c in range(NCH):
        for g in range(NG):
            dma("sync", xsl(g, c, 0, 1),
                halo.ap()[c:c + 1, g * 128:(g + 1) * 128].rearrange("a p -> p a"),
                slow=True)
    m_mem = None
    for t_, v_ in [(onesb[:, :], 1.0), (ones_f[:, :], 1.0), (ones_bb[:, :], 1.0),
                   (qbuf[64:128, :], 0.0), (qbuf2[64:128, :], 0.0),
                   (qbuf[64:67, :], 1.0), (qbuf2[64:67, :], 1.0),
                   (kbuf[0][64:96, :], 1.0), (kbuf[1][64:96, :], 1.0),
                   (kbuf[0][96:128, :], 0.0), (kbuf[1][96:128, :], 0.0),
                   (sum_sb[:, :], 0.0), (epsc[:, 0:1], 1e-5), (epsc[:, 1:2], 1e-4)]:
        m_mem = P.emit("gpsimd", lambda e_, t=t_, v=v_: e_.memset(t, v), inc="gc")
    m_iota = P.emit("gpsimd", lambda e_: e_.iota(
        iotab[:, :], pattern=[[1, SL]], base=1, channel_multiplier=0,
        allow_small_or_imprecise_dtypes=True), inc="gc")
    dma("gpsimd", bass.AP(bounce, 0, [[34, 128], [1, 1]]), sum_sb[:, 0:1], deps=[m_mem], slow=True)
    dma("gpsimd", bass.AP(bounce, 17, [[34, 128], [1, 1]]), sum_sb[:, 0:1], slow=True)
    m_dec = act(vec_sb[:, 60:62], vec_sb[:, 2:4], AF.Sigmoid, deps=[m_vec],
                settle=True)
    m_L = act(vec_sb[:, 62:64], vec_sb[:, 60:62], AF.Ln)
    m_l = dma("sync", scr_f[0:1, 0:1024], bq_v.ap())
    m_bqv = act(vrow_b[:, :], scr_f[0:1, 0:1024], AF.Copy, deps=[m_l], settle=True)

    if upto <= 0:
        ctx_close_out(P, nc, ctx, out, x_ar, xsl, dbg_t, dma, vop, act, locals())
        return nc

    P.barrier()
    ing_ev = {}
    for it in range(NCH * 4):
        c, tb = divmod(it, 4)
        half = it % 2
        m_l = dma(["sync", "gpsimd"][half], stg[:, half * 1024:half * 1024 + 1024],
                  tok.ap()[c, tb * 128:(tb + 1) * 128, :],
                  deps=[ing_ev.get(it - 2)], sem=f"il{half}")
        m_tr = None
        for g in range(NG):
            m_tr = trans(p_st[:, half * 1024 + g * 128:half * 1024 + (g + 1) * 128],
                         stg[:, half * 1024 + g * 128:half * 1024 + (g + 1) * 128],
                         ident_fs[:, :], deps=[m_l, ing_ev.get(it - 2)])
        m_tr = pesettle()
        evs = None
        for g in range(NG):
            evs = vop(lambda e_, g=g, c=c, tb=tb, half=half: e_.tensor_copy(
                xsl(g, c, 1 + tb * 128, 128),
                p_st[:, half * 1024 + g * 128:half * 1024 + (g + 1) * 128]),
                deps=[m_tr], inc=(g == NG - 1))
        ing_ev[it] = evs
    if upto <= 1:
        ctx_close_out(P, nc, ctx, out, x_ar, xsl, dbg_t, dma, vop, act, locals())
        return nc

    P.barrier()

    prep_chain = [None]
    prep_chain_cv = [None]
    prep_chain_out = [(None, None)]

    PREP_SLOTS = [(stg, 0), (stg, 1024), (stg, 2048),
                  (sh8, 0), (sh8, 1024), (scr_f, 0)]
    PREP_QS = ["sync", "gpsimd", "scalar"]

    def prep_tiles(n4, src_fn, dst_fn, out_fn=None, int_scale=None, ntr=4,
                   load_q=None, load_sem=None):
        prev_cv = {k: prep_chain_cv[0] for k in range(-6, 0)}
        prev_ev = {-1: prep_chain[0], -2: prep_chain[0]}
        prev_out = {-1: prep_chain_out[0][1] or prep_chain[0],
                    -2: prep_chain_out[0][0] or prep_chain[0]}
        w_ = 128 * ntr

        def slot_view(i4):
            st_t, st_c = PREP_SLOTS[i4 % 6]
            return (st_t[:, :].bitcast(I32)[:, st_c:st_c + w_]
                    if int_scale else st_t[:, st_c:st_c + w_])

        def emit_ld(i4):
            return dma(PREP_QS[i4 % 3], slot_view(i4), src_fn(i4),
                       deps=[prev_cv.get(i4 - 6)], sem=f"pl{i4 % 6}")

        lds = {}
        for i4 in range(min(6, n4)):
            lds[i4] = emit_ld(i4)
        for i4 in range(n4):
            half = i4 % 2
            stg_v = slot_view(i4)
            m_ld = lds[i4]
            m_cv = act(arA[:, half * 2052:half * 2052 + w_], stg_v, AF.Copy,
                       deps=[m_ld, prev_ev.get(i4 - 2)], scale=int_scale,
                       settle=True)
            prev_cv[i4] = m_cv
            def ptile(q, i4=i4):
                if i4 % 2 == 0:
                    return p_st[:, :].bitcast(BF16)[:, q * 1024:q * 1024 + 128]
                pb = p_m0 if q < 2 else p_m1
                return pb[:, :].bitcast(BF16)[:, (q % 2) * 128:(q % 2) * 128 + 128]
            m_tr = None
            for q in range(ntr):
                m_tr = trans(
                    ptile(q),
                    arA[:, half * 2052 + q * 128:half * 2052 + (q + 1) * 128],
                    ident_bs[:, :], deps=[m_cv, prev_ev.get(i4 - 2)])
            m_tr = pesettle()
            m_ev = None
            for q in range(ntr):
                last = (q == ntr - 1)
                m_ev = vop(lambda e_, q=q, i4=i4: e_.tensor_copy(
                    dst_fn(i4, q), ptile(q, i4)),
                    deps=[m_tr, prev_out.get(i4 - 2)] if q == 0 else [m_tr],
                    inc=last, settle=last)
            prev_ev[i4] = m_ev
            prev_out[i4] = out_fn(i4, m_ev) if out_fn else m_ev
            if i4 + 6 < n4:
                lds[i4 + 6] = emit_ld(i4 + 6)
        prep_chain[0] = prev_ev[n4 - 1]
        prep_chain_cv[0] = prev_cv[n4 - 1]
        if out_fn:
            prep_chain_out[0] = (prev_out.get(n4 - 2), prev_out.get(n4 - 1))
        return prev_out

    l0_srcs = [W_r, W_k, W_v, W_skip]
    l0_outs = prep_tiles(
        64,
        lambda i4: l0_srcs[i4 // 16].ap()[((i4 // 2) % 8) * 128:((i4 // 2) % 8 + 1) * 128,
                                          (i4 % 2) * 512:(i4 % 2 + 1) * 512],
        lambda i4, q: arB[:, (i4 % 2) * 2052 + q * 128:(i4 % 2) * 2052 + (q + 1) * 128],
        out_fn=lambda i4, mev: dma(
            "gpsimd",
            wc_l0.ap()[i4 // 16, :, ((i4 // 2) % 8) * 1024 + (i4 % 2) * 512:
                       ((i4 // 2) % 8) * 1024 + (i4 % 2) * 512 + 512],
            arB[:, (i4 % 2) * 2052:(i4 % 2) * 2052 + 512], deps=[mev],
            sem=f"pc{i4 % 2}"))
    prep_tiles(
        4,
        lambda i4: W_in.ap()[(i4 // 2) * 128:(i4 // 2 + 1) * 128,
                             (i4 % 2) * 512:(i4 % 2 + 1) * 512],
        lambda i4, q: w_in_sb[:, ((i4 % 2) * 4 + q) * 256 + (i4 // 2) * 128:
                              ((i4 % 2) * 4 + q) * 256 + (i4 // 2 + 1) * 128])
    prep_tiles(
        8,
        lambda i4: W_out.ap()[i4 * 128:(i4 + 1) * 128, 0:256],
        lambda i4, q: w_out_sb[:, q * 1024 + i4 * 128:q * 1024 + (i4 + 1) * 128],
        ntr=2)
    wd_outs = prep_tiles(
        48,
        lambda i4: wd.ap()[(i4 // 6) * 128:(i4 // 6 + 1) * 128,
                           (i4 % 6) * 512:(i4 % 6 + 1) * 512],
        lambda i4, q: arB[:, (i4 % 2) * 2052 + q * 128:(i4 % 2) * 2052 + (q + 1) * 128],
        out_fn=lambda i4, mev: dma(
            "scalar",
            wc_wd.ap()[(i4 // 6) // 4, :, ((i4 // 6) % 4) * 3072 + (i4 % 6) * 512:
                       ((i4 // 6) % 4) * 3072 + (i4 % 6) * 512 + 512],
            arB[:, (i4 % 2) * 2052:(i4 % 2) * 2052 + 512], deps=[mev],
            sem=f"pc{2 + i4 % 2}"),
        int_scale=1.0 / 127.0, load_q="gpsimd", load_sem="pl2")
    prevo = {}
    prevo_cv = {}
    for m in range(8):
        half = m % 2
        m_ld = None
        for a in range(8):
            m_ld = dma("sync", stg[:, half * 1024 + a * 128:half * 1024 + (a + 1) * 128],
                       wout1.ap()[a * 128:(a + 1) * 128, m * 128:(m + 1) * 128],
                       deps=([prevo.get(m - 2), prep_chain_cv[0]] if a == 0 else ()),
                       sem="pl")
        m_cv = act(arA[:, half * 2052:half * 2052 + 1024],
                   stg[:, half * 1024:half * 1024 + 1024], AF.Copy,
                   deps=[m_ld, prep_chain[0] if m < 2 else None], settle=True)
        prevo_cv[m] = m_cv
        prevo[m] = dma("gpsimd", wc_wo.ap()[m // 4, :, (m % 4) * 1024:(m % 4) * 1024 + 1024],
                       arA[:, half * 2052:half * 2052 + 1024], deps=[m_cv],
                       sem="pw")
    prevq = {}
    prevsh = {}
    for dc in range(8):
        m_ld2 = dma("gpsimd", stg[:, 1536:3072],
                    wqkv.ap()[dc * 128:(dc + 1) * 128, 1536:3072],
                    sem="pl4", deps=[prevsh.get(dc - 1),
                          (prevo_cv[7] if dc == 0 else None),
                          (prevo_cv[6] if dc == 0 else None)])
        m_ld = dma("scalar", stg[:, 0:1536], wqkv.ap()[dc * 128:(dc + 1) * 128, 0:1536],
                   sem="pl3", deps=[prevsh.get(dc - 1),
                         (prevo_cv[7] if dc == 0 else None),
                         (prevo_cv[6] if dc == 0 else None)])
        m_sh = None
        for kind in range(3):
            m_sh = vop(lambda e_, kind=kind: e_.tensor_copy(
                bass.AP(arB, kind * 128, [[4104, 128], [384, 8], [64, 2], [1, 64]]),
                bass.AP(stg, kind * 64, [[3072, 128], [384, 8], [192, 2], [1, 64]])),
                deps=([m_ld, m_ld2, prevq.get(dc - 1),
                       (prevo[7] if dc == 0 else None),
                       (prevo[6] if dc == 0 else None),
                       (wd_outs.get(46) if dc == 0 else None),
                       (wd_outs.get(47) if dc == 0 else None)]
                      if kind == 0 else [m_ld, m_ld2]),
                inc=(kind == 2), settle=(kind == 2))
        prevsh[dc] = m_sh
        m_o = None
        for hp in range(8):
            m_o = dma("gpsimd", wc_qkv.ap()[hp, :, dc * 384:(dc + 1) * 384],
                      arB[:, hp * 384:(hp + 1) * 384],
                      deps=[m_sh] if hp == 0 else (), sem="pw")
        prevq[dc] = m_o
    m_prepq = prevq[7]
    if upto <= 2:
        ctx_close_out(P, nc, ctx, out, x_ar, xsl, dbg_t, dma, vop, act, locals())
        return nc

    P.barrier()

    def emit_layernorm():
        outs = []
        pfree = None
        for c in range(NCH):
            m_cv = None
            for g in range(NG):
                m_cv = act(arA[:, g * CW:(g + 1) * CW], xsl(g, c, 0, CW), AF.Copy,
                           inc=(g == NG - 1))
            m_sq = None
            for g in range(NG):
                m_sq = act(arB[:, g * CW:(g + 1) * CW], arA[:, g * CW:(g + 1) * CW],
                           AF.Square, inc=(g == NG - 1))
            m_s1 = None
            for w0, w1 in [(0, 512), (512, 513)]:
                for g in range(NG):
                    m_s1 = mm(p_st[0:1, w0:w1], onesb[:, :],
                              arA[:, g * CW + w0:g * CW + w1],
                              g == 0, g == NG - 1, deps=[m_cv, pfree],
                              inc=(g == NG - 1 and w0 == 512))
            m_s2 = None
            for w0, w1 in [(0, 512), (512, 513)]:
                for g in range(NG):
                    m_s2 = mm(p_st[0:1, 1024 + w0:1024 + w1], onesb[:, :],
                              arB[:, g * CW + w0:g * CW + w1],
                              g == 0, g == NG - 1, deps=[m_sq, pfree],
                              inc=(g == NG - 1 and w0 == 512))
            m_mean = vop(lambda e_: e_.tensor_scalar(
                row_ar[0:1, 0:CW], p_st[0:1, 0:CW], 1.0 / 1024.0, None, ALU.mult),
                deps=[m_s1])
            m_e2 = vop(lambda e_: e_.tensor_scalar(
                row_ar[64:65, 0:CW], p_st[0:1, 1024:1024 + CW], 1.0 / 1024.0, None,
                ALU.mult), deps=[m_s2])
            vop(lambda e_: e_.tensor_tensor(
                row_ar[64:65, CW:2 * CW], row_ar[0:1, 0:CW], row_ar[0:1, 0:CW],
                ALU.mult), inc=False)
            m_var = vop(lambda e_: e_.tensor_tensor(
                scr_f[0:1, 0:CW], row_ar[64:65, 0:CW], row_ar[64:65, CW:2 * CW],
                ALU.subtract))
            m_ln = act(scr_f[0:1, CW:2 * CW], scr_f[0:1, 0:CW], AF.Ln,
                       deps=[m_var], bias=epsc[0:1, 0:1])
            m_rstd = act(row_ar[0:1, CW:2 * CW], scr_f[0:1, CW:2 * CW], AF.Exp,
                         scale=-0.5, settle=True)
            m_bm = None
            for w0, w1 in [(0, 512), (512, 513)]:
                m_bm = mm(p_st[:, w0:w1], ones_f[0:1, :], row_ar[0:1, w0:w1], True, True,
                          deps=[m_mean, m_e2], inc=(w0 == 512))
            m_br = None
            for w0, w1 in [(0, 512), (512, 513)]:
                m_br = mm(p_st[:, 1024 + w0:1024 + w1], ones_f[0:1, :],
                          row_ar[0:1, CW + w0:CW + w1], True, True,
                          deps=[m_rstd], inc=(w0 == 512))
            m_xn = None
            for g in range(NG):
                vop(lambda e_, g=g, c=c: e_.tensor_tensor(
                    scr_f[:, 0:CW], xsl(g, c, 0, CW), p_st[:, 0:CW], ALU.subtract),
                    deps=[m_bm, m_br], inc=False)
                m_xn = vop(lambda e_, g=g, c=c: e_.tensor_tensor(
                    xnsl(g, c, 0, CW), scr_f[:, 0:CW], p_st[:, 1024:1024 + CW],
                    ALU.mult), inc=(g == NG - 1))
            pfree = m_xn
            outs.append(m_xn)
        return outs

    xn_ms = emit_layernorm()
    if upto <= 3:
        ctx_close_out(P, nc, ctx, out, x_ar, xsl, dbg_t, dma, vop, act, locals())
        return nc

    P.barrier()
    m_dt = []
    for n2 in range(2):
        m_dt.append(act(sh8[:, 1024 + n2 * SL:1024 + (n2 + 1) * SL], iotab[:, :],
                        AF.Exp, deps=[m_iota, m_L], scale=vec_sb[:, 62 + n2:63 + n2],
                        settle=True))
    act(sum_sb[:, 64:66], vec_sb[:, 62:64], AF.Copy, deps=[m_L], scale=512.0)
    m_d512 = act(sum_sb[:, 64:66], sum_sb[:, 64:66], AF.Exp, settle=True)

    m_scan = {}
    scr_guard = [None]
    fr = dict(a0=None, a1=None, m0=None, m1=None)
    wload = {}
    wfree = {}
    nload = [0]

    def wload_next(src_ap, deps=()):
        i = nload[0]
        nload[0] += 1
        sl_ = i % 3
        extra = []
        if i < 3:
            extra = [("pc0", P.cnt.get("pc0", 0)), ("pc1", P.cnt.get("pc1", 0))]
        wload[i] = dma(["sync", "gpsimd", "sync"][sl_],
                       wbufB[:, sl_ * 1024:(sl_ + 1) * 1024], src_ap,
                       sem=f"ws{sl_}", deps=list(deps) + [wfree.get(i - 3)] + extra)
        return i

    def wtile(i, k):
        sl_ = i % 3
        return wbufB[:, sl_ * 1024 + k * 128:sl_ * 1024 + (k + 1) * 128]

    for c in range(NCH):
        m_sq = None
        for g in range(NG):
            m_sq = act(arB[:, g * SL:(g + 1) * SL], xnsl(g, c, 1, SL), AF.Square,
                       deps=[xn_ms[c]] if g == 0 else (), inc=(g == NG - 1))
        m_ms = None
        for g in range(NG):
            m_ms = mm(p_a0[0:1, 0:SL], onesb[:, :], arB[:, g * SL:(g + 1) * SL],
                      g == 0, g == NG - 1, deps=[m_sq, fr["a0"]], inc=(g == NG - 1))
        m_msr = vop(lambda e_: e_.tensor_scalar(
            row_ar[64:65, 0:SL], p_a0[0:1, 0:SL], 1.0 / 1024.0, None, ALU.mult),
            deps=[m_ms])
        fr["a0"] = m_msr
        act(row_ar[64:65, CW:CW + SL], row_ar[64:65, 0:SL], AF.Ln,
            deps=[m_msr], bias=epsc[64:65, 1:2])
        m_rs2 = act(row_ar[64:65, 0:SL], row_ar[64:65, CW:CW + SL], AF.Exp,
                    scale=-0.5, settle=True)
        m_b2 = mm(p_m1[:, 0:SL], ones_f[64:65, :], row_ar[64:65, 0:SL], True, True,
                  deps=[m_rs2, fr["m1"]], inc=True)
        m_xh = None
        for g in range(NG):
            m_xh = vop(lambda e_, g=g, c=c: e_.scalar_tensor_tensor(
                arA[:, g * SL:(g + 1) * SL], xnsl(g, c, 1, SL), vec_sb[:, 4 + g:5 + g],
                p_m1[:, 0:SL], ALU.mult, ALU.mult), deps=[m_b2],
                inc=(g == NG - 1), settle=(g == NG - 1))
        fr["m1"] = m_xh
        m_u = []
        for n2 in range(2):
            pp = p_a0 if n2 == 0 else p_a1
            mmx = None
            for g in range(NG):
                mmx = mm(pp[:, 0:SL], w_in_sb[:, g * 256 + n2 * 128:g * 256 + (n2 + 1) * 128],
                         arA[:, g * SL:(g + 1) * SL], g == 0, g == NG - 1,
                         deps=[m_xh, fr["a0"] if n2 == 0 else fr["a1"],
                               m_scan.get((c - 1, n2))], inc=(g == NG - 1))
            m_u.append(mmx)
        m_ue = []
        for n2 in range(2):
            pp = p_a0 if n2 == 0 else p_a1
            m_ue.append(vop(lambda e_, n2=n2, pp=pp: e_.tensor_scalar(
                scr_f[:, n2 * CW:n2 * CW + SL], pp[:, 0:SL],
                vec_sb[:, 0 + n2:1 + n2], None, ALU.add),
                deps=[m_u[n2], scr_guard[0]]))
            fr["a0" if n2 == 0 else "a1"] = m_ue[-1]
        if c == 0 and "dbg_u" in dbg_t:
            dma("sync", dbg_t["dbg_u"].ap(), scr_f[:, 0:2 * CW],
                deps=[m_ue[0], m_ue[1]])
        for n2 in range(2):
            dcol = vec_sb[:, 60 + n2:61 + n2]
            d0 = bass.AP(dcol.tensor, dcol.offset,
                         [[list(dcol.ap)[0][0], 128], [0, SL]])
            m_scan[(c, n2)] = vop(lambda e_, n2=n2, c=c, d0=d0: e_.tensor_tensor_scan(
                st_ar[:, (n2 * NCH + c) * SL:(n2 * NCH + c + 1) * SL], d0,
                scr_f[:, n2 * CW:n2 * CW + SL], 0.0, ALU.mult, ALU.add),
                deps=[m_ue[n2], m_dec])
        if c == NCH - 1:
            m_smc = None
            for n2 in range(2):
                for cc_ in range(NCH):
                    m_smc = vop(lambda e_, n2=n2, cc_=cc_: e_.tensor_copy(
                        sum_sb[:, cc_ * 2 + n2:cc_ * 2 + n2 + 1],
                        st_ar[:, (n2 * NCH + cc_ + 1) * SL - 1:(n2 * NCH + cc_ + 1) * SL]),
                        deps=[m_scan[(cc_, n2)]], inc=(n2 == 1 and cc_ == NCH - 1),
                        settle=(n2 == 1 and cc_ == NCH - 1))
            m_str = trans(p_m1[0:8, 0:128], sum_sb[:, 0:8], ident_fs[:, :],
                          deps=[m_smc, fr["m1"]])
            m_str = pesettle()
            m_sev = vop(lambda e_: e_.tensor_copy(sh8[0:8, 0:128], p_m1[0:8, 0:128]),
                        deps=[m_str], settle=True)
            fr["m1"] = m_sev
            m_sdma = dma("gpsimd", sum_in.ap(), sh8[0:8, 0:128], deps=[m_sev])
            m_sdma = dma("gpsimd", sum_in.ap(), sh8[0:8, 0:128], deps=[m_sdma])
            m_cc1 = P.emit("gpsimd", lambda e_: e_.collective_compute(
                "AllGather", ALU.bypass, replica_groups=[[0, 1, 2, 3], [4, 5, 6, 7]],
                ins=[sum_in.ap().opt()], outs=[sum_out.ap().opt()]),
                deps=[m_sdma], inc="cc")
            m_cc1 = P.emit("gpsimd", lambda e_: e_.collective_compute(
                "AllGather", ALU.bypass, replica_groups=[[0, 1, 2, 3], [4, 5, 6, 7]],
                ins=[sum_in.ap().opt()], outs=[sum_out.ap().opt()]),
                inc="cc")
        m_add = {}
        for m in range(8):
            i = wload_next(wc_l0.ap()[3, :, m * 1024:(m + 1) * 1024])
            pp = p_m0 if m % 2 == 0 else p_a0
            mmx = None
            for k in range(8):
                mmx = mm(pp[:, 0:SL], wtile(i, k), arA[:, k * SL:(k + 1) * SL],
                         k == 0, k == 7,
                         deps=[wload[i], m_xh, fr["m0"] if m % 2 == 0 else fr["a0"],
                               m_scan[(c, 0)] if m % 2 == 1 else None],
                         inc=(k == 7))
            wfree[i] = mmx
            m_add[m] = vop(lambda e_, m=m, c=c, pp=pp: e_.scalar_tensor_tensor(
                xsl(m, c, 1, SL), pp[:, 0:SL], vec_sb[:, 12 + m:13 + m],
                xsl(m, c, 1, SL), ALU.add, ALU.add), deps=[mmx])
            fr["m0" if m % 2 == 0 else "a0"] = m_add[m]
        for g in range(NG):
            vop(lambda e_, g=g, c=c: e_.tensor_tensor(
                xsl(g, c, 1, SL), xsl(g, c, 1, SL), xnsl(g, c, 1, SL), ALU.add),
                deps=[m_add[g]], inc=(g == NG - 1))
        m_mix = None
        for g in range(NG):
            vop(lambda e_, g=g, c=c: e_.tensor_tensor(
                arC[:, 0:SL], xnsl(g, c, 1, SL), xnsl(g, c, 0, SL), ALU.subtract),
                deps=[m_add[7], m_u[0], m_u[1]] if g == 0 else (), inc=False)
            vop(lambda e_, g=g, c=c: e_.scalar_tensor_tensor(
                arA[:, g * SL:(g + 1) * SL], arC[:, 0:SL], vec_sb[:, 20 + g:21 + g],
                xnsl(g, c, 0, SL), ALU.mult, ALU.add), inc=False)
            m_mix = vop(lambda e_, g=g, c=c: e_.scalar_tensor_tensor(
                arB[:, g * SL:(g + 1) * SL], arC[:, 0:SL], vec_sb[:, 28 + g:29 + g],
                xnsl(g, c, 0, SL), ALU.mult, ALU.add), inc=(g == NG - 1),
                settle=(g == NG - 1))
        m_r = {}
        for m in range(8):
            i = wload_next(wc_l0.ap()[0, :, m * 1024:(m + 1) * 1024])
            pp = p_m0 if m % 2 == 0 else p_a0
            mmx = None
            for k in range(8):
                mmx = mm(pp[:, 0:SL], wtile(i, k), arA[:, k * SL:(k + 1) * SL],
                         k == 0, k == 7,
                         deps=[wload[i], m_mix, fr["m0"] if m % 2 == 0 else fr["a0"]],
                         inc=(k == 7))
            wfree[i] = mmx
            m_r[m] = act(arC[:, 512 + m * SL:512 + (m + 1) * SL], pp[:, 0:SL],
                         AF.Sigmoid, deps=[mmx], settle=True)
            fr["m0" if m % 2 == 0 else "a0"] = m_r[m]
        m_k2 = {}
        for m in range(8):
            i = wload_next(wc_l0.ap()[1, :, m * 1024:(m + 1) * 1024])
            pp = p_m1 if m % 2 == 0 else p_a1
            mmx = None
            for k in range(8):
                mmx = mm(pp[:, 0:SL], wtile(i, k), arB[:, k * SL:(k + 1) * SL],
                         k == 0, k == 7,
                         deps=[wload[i], m_mix, m_r[7] if m <= 1 else None,
                               fr["m1"] if m % 2 == 0 else fr["a1"]],
                         inc=(k == 7))
            wfree[i] = mmx
            m_rl = vop(lambda e_, m=m, pp=pp: e_.tensor_scalar(
                arC[:, 4608 + (m % 2) * SL:4608 + (m % 2 + 1) * SL], pp[:, 0:SL],
                0.0, None, ALU.max), deps=[mmx, m_k2.get(m - 2)], settle=True)
            fr["m1" if m % 2 == 0 else "a1"] = m_rl
            m_k2[m] = act(arA[:, m * SL:(m + 1) * SL],
                          arC[:, 4608 + (m % 2) * SL:4608 + (m % 2 + 1) * SL],
                          AF.Square, deps=[m_rl], settle=True)
        for m in range(8):
            i = wload_next(wc_l0.ap()[2, :, m * 1024:(m + 1) * 1024])
            pp = p_m0 if m % 2 == 0 else p_a0
            mmx = None
            for k in range(8):
                mmx = mm(pp[:, 0:SL], wtile(i, k), arA[:, k * SL:(k + 1) * SL],
                         k == 0, k == 7,
                         deps=[wload[i], m_k2[7], fr["m0"] if m % 2 == 0 else fr["a0"]],
                         inc=(k == 7))
            wfree[i] = mmx
            m_t = vop(lambda e_, m=m, pp=pp: e_.tensor_tensor(
                scr_f[:, (m % 2) * CW:(m % 2) * CW + SL], pp[:, 0:SL],
                arC[:, 512 + m * SL:512 + (m + 1) * SL], ALU.mult), deps=[mmx])
            fr["m0" if m % 2 == 0 else "a0"] = m_t
            m_xvadd = vop(lambda e_, m=m, c=c: e_.tensor_tensor(
                xsl(m, c, 1, SL), xsl(m, c, 1, SL),
                scr_f[:, (m % 2) * CW:(m % 2) * CW + SL], ALU.add))
        scr_guard[0] = m_xvadd
    if upto <= 4:
        ctx_close_out(P, nc, ctx, out, x_ar, xsl, dbg_t, dma, vop, act, locals())
        return nc

    P.barrier()
    m_so = dma("sync", scr_f[0:32, 0:128], sum_out.ap(),
               deps=[m_cc1, scr_guard[0], m_sdma])
    m_str2 = trans(p_m0[:, 0:32], scr_f[0:32, 0:128], ident_fs[0:32, 0:32],
                   deps=[m_so, fr["m0"]])
    m_str2 = pesettle()
    m_sev2 = vop(lambda e_: e_.tensor_copy(sum_sb[:, 32:64], p_m0[:, 0:32]),
                 deps=[m_str2])
    fr["m0"] = m_sev2
    m_ro = None
    for g2 in range(2):
        for r in range(4):
            for s2 in range(NCH):
                gch = chunks_of(r)[s2]
                src = 32 + r * 8 + s2 * 2 + g2
                m_ro = vop(lambda e_, g2=g2, gch=gch, src=src: e_.tensor_copy(
                    sum_sb[:, (0 if g2 == 0 else 16) + gch:(0 if g2 == 0 else 16) + gch + 1],
                    sum_sb[:, src:src + 1]), deps=[m_sev2],
                    inc=(g2 == 1 and r == 3 and s2 == NCH - 1))
    m_isc = []
    for g2 in range(2):
        dcol = sum_sb[:, 64 + g2:65 + g2]
        d0 = bass.AP(dcol.tensor, dcol.offset, [[list(dcol.ap)[0][0], 128], [0, 16]])
        m_isc.append(vop(lambda e_, g2=g2, d0=d0: e_.tensor_tensor_scan(
            sum_sb[:, 32 + g2 * 16:48 + g2 * 16], d0,
            sum_sb[:, g2 * 16:16 + g2 * 16], 0.0, ALU.mult, ALU.add),
            deps=[m_ro, m_d512], settle=True))
    m_bo = [dma("gpsimd", bounce.ap()[:, g2 * 17 + 1:g2 * 17 + 17],
                sum_sb[:, 32 + g2 * 16:48 + g2 * 16], deps=[m_isc[g2]])
            for g2 in range(2)]
    m_bo = [dma("gpsimd", bounce.ap()[:, g2 * 17 + 1:g2 * 17 + 17],
                sum_sb[:, 32 + g2 * 16:48 + g2 * 16], deps=[m_bo[g2]])
            for g2 in range(2)]
    m_pick = {}
    for g2 in range(2):
        for c in range(NCH):
            idx = g2 * NCH + c
            P.emit("gpsimd", lambda e_, idx=idx: e_.reg_load(
                rS, spick_s[0:1, idx:idx + 1]), deps=[m_bo[g2], m_vec])
            m_pick[idx] = dma("gpsimd", sin_sb[:, idx:idx + 1],
                              bass.AP(bounce, rS, [[34, 128], [1, 1]]), slow=True)
    m_cor = {}
    for g2 in range(2):
        for c in range(NCH):
            idx = g2 * NCH + c
            m_cor[(c, g2)] = vop(lambda e_, g2=g2, c=c, idx=idx: e_.scalar_tensor_tensor(
                st_ar[:, (g2 * NCH + c) * SL:(g2 * NCH + c + 1) * SL],
                sh8[:, 1024 + g2 * SL:1024 + (g2 + 1) * SL],
                sin_sb[:, idx:idx + 1],
                st_ar[:, (g2 * NCH + c) * SL:(g2 * NCH + c + 1) * SL],
                ALU.mult, ALU.add), deps=[m_pick[idx], m_dt[g2]])
    for c in range(NCH):
        m_cast = None
        for g2 in range(2):
            m_cast = vop(lambda e_, g2=g2, c=c: e_.tensor_copy(
                arA[:, g2 * SL:(g2 + 1) * SL],
                st_ar[:, (g2 * NCH + c) * SL:(g2 * NCH + c + 1) * SL]),
                deps=[m_cor[(c, 0)], m_cor[(c, 1)]], inc=(g2 == 1),
                settle=(g2 == 1))
        for m in range(8):
            pp = p_m0 if m % 2 == 0 else p_a0
            mmx = None
            for n2 in range(2):
                mmx = mm(pp[:, 0:SL],
                         w_out_sb[:, n2 * 1024 + m * 128:n2 * 1024 + (m + 1) * 128],
                         arA[:, n2 * SL:(n2 + 1) * SL], n2 == 0, n2 == 1,
                         deps=[m_cast, fr["m0"] if m % 2 == 0 else fr["a0"]],
                         inc=(n2 == 1))
            m_ya = vop(lambda e_, m=m, c=c, pp=pp: e_.tensor_tensor(
                xsl(m, c, 1, SL), xsl(m, c, 1, SL), pp[:, 0:SL], ALU.add),
                deps=[mmx])
            fr["m0" if m % 2 == 0 else "a0"] = m_ya
    if upto <= 5:
        ctx_close_out(P, nc, ctx, out, x_ar, xsl, dbg_t, dma, vop, act, locals())
        return nc

    P.barrier()
    xn1_ms = emit_layernorm()

    P.barrier()
    fr = dict(a0=None, a1=None, m0=None, m1=None, ropeB=None)
    m_wq = {}
    m_kv_w = []
    m_ones = P.emit("gpsimd", lambda e_: e_.memset(arA[:, 2048:2568], 1.0),
                    inc="gc")
    fr["vdma0"] = fr["vdma1"] = m_ones
    for hp in range(NHP):
        sl_ = hp % 2
        m_wq[hp] = dma("sync", arC[:, sl_ * 3072:(sl_ + 1) * 3072],
                       wc_qkv.ap()[hp, :, :], deps=[m_wq.get(hp - 2), m_prepq],
                       sem="wq")
        for c in range(NCH):
            m_rope = dma("sync", arB[:, 2048:2560], ropec.ap()[c, :, :],
                         deps=[fr["ropeB"]], sem="wr")
            m_rope2 = dma("sync", arB[:, 2560:3072], ropes.ap()[c, :, :],
                          sem="wr")
            m_q = None
            m_k = None
            for g in range(NG):
                m_q = mm(p_a0[:, 0:SL],
                         arC[:, sl_ * 3072 + g * 384:sl_ * 3072 + g * 384 + 128],
                         xnsl(g, c, 1, SL), g == 0, g == NG - 1,
                         deps=[m_wq[hp], xn1_ms[c], fr["a0"]], inc=(g == NG - 1))
            for g in range(NG):
                m_k = mm(p_a1[:, 0:SL],
                         arC[:, sl_ * 3072 + g * 384 + 128:sl_ * 3072 + g * 384 + 256],
                         xnsl(g, c, 1, SL), g == 0, g == NG - 1,
                         deps=[m_wq[hp], xn1_ms[c], fr["a1"]], inc=(g == NG - 1))
            for qk in range(2):
                pp = p_a0 if qk == 0 else p_a1
                m_src = m_q if qk == 0 else m_k
                bias_c = vec_sb[:, (36 if qk == 0 else 44) + hp:
                                (37 if qk == 0 else 45) + hp]
                m_ev = act(arA[:, qk * 1024:qk * 1024 + SL], pp[:, 0:SL], AF.Identity,
                           deps=[m_src, fr.get("qdma" if qk == 0 else "kdma")],
                           bias=bias_c, settle=True)
                fr["a0" if qk == 0 else "a1"] = m_ev
                m_pm = mm(p_m0[:, 0:SL], pswap_s[:, :], arA[:, qk * 1024:qk * 1024 + SL],
                          True, True, deps=[m_ev, fr["m0"]], inc=True)
                m_t1 = vop(lambda e_, qk=qk: e_.tensor_tensor(
                    arA[:, qk * 1024 + 512:qk * 1024 + 1024],
                    arA[:, qk * 1024:qk * 1024 + 512], arB[:, 2048:2560], ALU.mult),
                    deps=[m_ev, m_rope, m_rope2])
                m_t2 = vop(lambda e_, qk=qk: e_.tensor_tensor(
                    arA[:, qk * 1024:qk * 1024 + 512], p_m0[:, 0:SL],
                    arB[:, 2560:3072], ALU.mult), deps=[m_pm])
                fr["m0"] = m_t2
                m_rr = vop(lambda e_, qk=qk: e_.tensor_tensor(
                    arA[:, qk * 1024:qk * 1024 + 512],
                    arA[:, qk * 1024:qk * 1024 + 512],
                    arA[:, qk * 1024 + 512:qk * 1024 + 1024], ALU.add),
                    deps=[m_t1, m_t2], settle=True)
                fr["ropeB"] = m_rr
                if qk == 0:
                    fr["qdma"] = dma("sync", qcache.ap()[hp, c, :, :],
                                     arA[:, 0:512], deps=[m_rr], sem="qw")
                    m_kv_w.append(fr["qdma"])
                else:
                    fr["kdma"] = dma(
                        "sync",
                        kv_kin.ap()[(hp * NCH + c) * 128:(hp * NCH + c + 1) * 128, :],
                        arA[:, 1024:1536], deps=[m_rr], sem="kw")
                    m_kv_w.append(fr["kdma"])
            m_ve2 = {}
            for tb in range(4):
                m_v = None
                for g in range(NG):
                    m_v = mm(p_m1[:, 0:128],
                             xnsl(g, c, 1 + tb * 128, 128),
                             arC[:, sl_ * 3072 + g * 384 + 256:sl_ * 3072 + g * 384 + 384],
                             g == 0, False,
                             deps=[m_wq[hp], xn1_ms[c], fr["m1"]], inc=False)
                m_v = mm(p_m1[:, 0:128], ones_bb[:, :],
                         vrow_b[:, hp * 128:(hp + 1) * 128], False, True,
                         deps=[m_bqv], inc=True)
                for h in range(2):
                    m_ve2[(tb, h)] = act(
                        arA[:, 2048 + h * 260 + tb * 65:2048 + h * 260 + tb * 65 + 64],
                        p_m1[:, h * 64:(h + 1) * 64], AF.Copy,
                        deps=[m_v, fr.get(f"vdma{h}")], settle=True)
                fr["m1"] = m_ve2[(tb, 1)]
            for h in range(2):
                fr[f"vdma{h}"] = dma(
                    "gpsimd",
                    kv_vin.ap()[((2 * hp + h) * NCH + c) * 128:
                                ((2 * hp + h) * NCH + c + 1) * 128, :],
                    arA[:, 2048 + h * 260:2048 + (h + 1) * 260],
                    deps=[m_ve2[(3, h)]], sem="vw")
                m_kv_w.append(fr[f"vdma{h}"])
    m_cc2 = P.emit("gpsimd", lambda e_: e_.collective_compute(
        "AllGather", ALU.bypass, replica_groups=[[0, 1, 2, 3], [4, 5, 6, 7]],
        ins=[kv_kin.ap().opt()], outs=[kv_kall.ap().opt()]),
        deps=m_kv_w, inc="cc")
    m_cc3 = P.emit("gpsimd", lambda e_: e_.collective_compute(
        "AllGather", ALU.bypass, replica_groups=[[0, 1, 2, 3], [4, 5, 6, 7]],
        ins=[kv_vin.ap().opt()], outs=[kv_vall.ap().opt()]),
        inc="cc")
    if upto <= 6:
        ctx_close_out(P, nc, ctx, out, x_ar, xsl, dbg_t, dma, vop, act, locals())
        return nc

    P.barrier()
    fr = dict(a0=None, a1=None, m0=None, m1=None)
    prev_cv = {}
    prev_mh = {}
    prev_ev = {}
    for hg in range(24):
        ping = hg % 2
        stgv = (stg[:, :].bitcast(I32) if ping == 0 else arB[:, :].bitcast(I32))
        m_lg = dma("sync", stgv[:, 0:1024], wg.ap()[hg * 128:(hg + 1) * 128, :],
                   deps=[prev_cv.get(hg - 2)], sem="gl")
        m_lu = dma("gpsimd", stgv[:, 1024:2048], wu.ap()[hg * 128:(hg + 1) * 128, :],
                   deps=[prev_cv.get(hg - 2)], sem="glv")
        m_cv = act(arA[:, ping * 2052:ping * 2052 + 2048], stgv[:, 0:2048],
                   AF.Copy, deps=[m_lg, m_lu, prev_mh.get(hg - 2)],
                   scale=1.0 / 127.0, settle=True)
        prev_cv[hg] = m_cv
        for rnd in range(2):
            pptI = p_m0 if rnd == 0 else p_m1
            m_tr = None
            for qq in range(8):
                q = rnd * 8 + qq
                m_tr = trans(
                    pptI[:, :].bitcast(BF16)[:, qq * 128:(qq + 1) * 128],
                    arA[:, ping * 2052 + q * 128:ping * 2052 + (q + 1) * 128],
                    ident_bs[:, :],
                    deps=[m_cv, prev_ev.get((hg - 1, rnd))])
            m_tr = pesettle()
            prev_ev[(hg, rnd)] = None
            for qq in range(8):
                q = rnd * 8 + qq
                last = (qq == 7)
                prev_ev[(hg, rnd)] = vop(lambda e_, q=q, qq=qq, ping=ping,
                                         pptI=pptI: e_.tensor_copy(
                    arC[:, ping * 2048 + q * 128:ping * 2048 + (q + 1) * 128],
                    pptI[:, :].bitcast(BF16)[:, qq * 128:(qq + 1) * 128]),
                    deps=[m_tr], inc=last, settle=last)
        m_w = prev_ev[(hg, 1)]
        for c in range(NCH):
            pg = p_a0[:, 0:SL] if c % 2 == 0 else p_st[:, 0:SL]
            pu = p_a1[:, 0:SL] if c % 2 == 0 else p_st[:, SL:2 * SL]
            gk, uk = f"g{c % 2}", f"u{c % 2}"
            m_g = None
            m_uu = None
            for g in range(NG):
                m_g = mm(pg, arC[:, ping * 2048 + g * 128:ping * 2048 + (g + 1) * 128],
                         xnsl(g, c, 1, SL), g == 0, g == NG - 1,
                         deps=[m_w, fr.get(gk)], inc=(g == NG - 1))
            for g in range(NG):
                m_uu = mm(pu,
                          arC[:, ping * 2048 + 1024 + g * 128:ping * 2048 + 1024 + (g + 1) * 128],
                          xnsl(g, c, 1, SL), g == 0, g == NG - 1,
                          deps=[m_w, fr.get(uk)], inc=(g == NG - 1))
            m_si = act(arA[:, ping * 2052 + (c % 2) * 512:ping * 2052 + (c % 2) * 512 + 512],
                       pg, AF.Silu, deps=[m_g, fr.get(f"mh{c % 2}")], settle=True)
            fr[gk] = m_si
            hsl = wbufB[:, c * 512:(c + 1) * 512]
            m_h = vop(lambda e_, hsl=hsl, ping=ping, c=c, pu=pu: e_.tensor_tensor(
                hsl, pu,
                arA[:, ping * 2052 + (c % 2) * 512:ping * 2052 + (c % 2) * 512 + 512],
                ALU.mult), deps=[m_si, m_uu, fr.get(f"hdma{c}")], settle=True)
            fr[uk] = m_h
            fr[f"mh{c % 2}"] = m_h
            m_use = dma("scalar", hcache.ap()[c, :, hg * SL:(hg + 1) * SL],
                        hsl, deps=[m_h], sem=f"hw{c}")
            fr[f"hdma{c}"] = m_use
        prev_mh[hg] = m_h
    if upto <= 7:
        ctx_close_out(P, nc, ctx, out, x_ar, xsl, dbg_t, dma, vop, act, locals())
        return nc

    P.barrier()
    m_mask = dma("sync", sh8[:, 0:2048], maskadd.ap())
    st_b = st_ar[:, :].bitcast(BF16)
    stg_b = stg[:, :].bitcast(BF16)
    kw_dep = [("pc2", 0), ("pc3", 0), ("pw", 0), ("gl", 0),
              ("hw0", 0), ("hw1", 0), ("hw2", 0), ("hw3", 0)]
    m_kw = dma("scalar", xn_ar[:, 0:12288], wc_wd.ap()[0, :, :], deps=kw_dep,
               sem="wd")
    dma("scalar", st_b[:, 0:8192], wc_wd.ap()[1, :, 0:8192], sem="wd")
    dma("scalar", stg_b[:, 0:4096], wc_wd.ap()[1, :, 8192:12288], sem="wd")
    dma("scalar", wbufB[:, 0:3072], wc_wo.ap()[0, :, 0:3072], sem="wd")
    dma("scalar", xn_ar[:, 12288:13312], wc_wo.ap()[0, :, 3072:4096], sem="wd")
    dma("scalar", stg_b[:, 4096:6144], wc_wo.ap()[1, :, 0:2048], sem="wd")
    m_kw = dma("scalar", xn_ar[:, 13312:15360], wc_wo.ap()[1, :, 2048:4096],
               sem="wd")
    m_jt = dma("sync", jot[:, :], jof.ap(), sem="jt")
    m_jt = dma("sync", tbb[:, :], tb2.ap(), sem="jt")
    qbufs = [qbuf, qbuf2]
    seq = []
    jb0 = 0
    for (s, hp, njobs) in GROUPS:
        for h in range(2):
            seq.append((s, hp, h, njobs, jb0))
        jb0 += njobs
    kb_use = [None, None]
    vb_use = [None, None]
    pb_use = [None, None]
    arB_use = None
    rrow_use = None
    exp_prev = [None, None]
    lastqk_item = {}
    acc_free = {0: None, 1: None}
    qload = {}

    def emit_qload(i):
        si, hpi, hi, _, _ = seq[i]
        qb_ = qbufs[i % 2]
        dma("sync", qb_[0:64, :], qcache.ap()[hpi, si, 64 * hi:64 * hi + 64, :],
            deps=[lastqk_item.get(i - 2)], sem=f"qb{i % 2}")
        return dma("sync", qb_[67:70, :], qaug.ap()[hpi, si, hi, :, :],
                   sem=f"qb{i % 2}")

    qload[0] = emit_qload(0)
    qload[1] = emit_qload(1)
    for i, (s, hp, h, njobs, jb0) in enumerate(seq):
        hh = 2 * hp + h
        pacc = p_a0 if h == 0 else p_a1
        qb_ = qbufs[i % 2]
        m_qload = qload[i]
        m_b = [dma("sync", kbuf[ii][64:67, :], beta.ap()[hh, :, :],
                   deps=[kb_use[ii]], sem="bb") for ii in range(2)]
        m_pv = None
        for jb in range(njobs):
            jidx = jb0 + jb
            bs = jb % 2
            m_k = P.emit("gpsimd", lambda e_, bs=bs, col=4 * jidx + h:
                         e_.indirect_dma_start(
                kbuf[bs][0:64, :], None, kv_kall.ap()[0:64, :],
                bass.IndirectOffsetOnAxis(ap=jot[0:64, col:col + 1], axis=0)),
                deps=[m_jt, kb_use[bs], ("cc", P.cnt["cc"])],
                inc="gd", incn=16)
            m_vg = P.emit("gpsimd", lambda e_, bs=bs, col=4 * jidx + 2 + h:
                          e_.indirect_dma_start(
                vbuf[bs][:, :], None, kv_vall.ap()[0:128, :],
                bass.IndirectOffsetOnAxis(ap=jot[:, col:col + 1], axis=0)),
                deps=[m_jt, vb_use[bs], ("cc", P.cnt["cc"])],
                inc="gd", incn=16)
            m_qkh = [None, None]
            m_qke = [None] * 4
            for half in range(2):
                for kb in ((0, 1) if half == 0 else (2, 3)):
                    m_qke[kb] = mm(p_st[:, kb * SL:(kb + 1) * SL],
                                   kbuf[bs][0:128, kb * 128:(kb + 1) * 128],
                                   qb_[0:128, :], True, True,
                                   deps=[m_k, m_qload, m_b[bs], exp_prev[half]],
                                   inc=True)
                m_qkh[half] = m_qke[2 * half + 1]
            kb_use[bs] = m_qkh[1]
            lastqk_item[i] = m_qkh[1]
            m_mah = [None] * 4
            if jb == 0:
                for kb in range(4):
                    m_mah[kb] = vop(lambda e_, kb=kb: e_.tensor_tensor(
                        p_st[:, kb * SL:(kb + 1) * SL],
                        p_st[:, kb * SL:(kb + 1) * SL],
                        sh8[:, kb * SL:(kb + 1) * SL], ALU.add),
                        deps=[m_qke[kb], m_mask], settle=True)
            m_exph = [None, None]
            for half in range(2):
                m_exph[half] = act(pbuf[bs][:, half * 1024:(half + 1) * 1024],
                                   p_st[:, half * 1024:(half + 1) * 1024], AF.Exp,
                                   deps=[m_qkh[half], m_mah[2 * half],
                                         m_mah[2 * half + 1], pb_use[bs], m_jt],
                                   bias=tbb[:, jidx * 2 + h:jidx * 2 + h + 1],
                                   scale=SC, settle=True)
                exp_prev[half] = m_exph[half]
            for half in range(2):
                for kb in ((0, 1) if half == 0 else (2, 3)):
                    m_pv = mm(pacc[0:65, 0:SL], vbuf[bs][:, kb * 65:(kb + 1) * 65],
                              pbuf[bs][:, kb * SL:(kb + 1) * SL],
                              (jb == 0 and kb == 0), (jb == njobs - 1 and kb == 3),
                              deps=[m_exph[half], m_vg,
                                    acc_free[h] if jb == 0 else None],
                              inc=(kb == 3))
            vb_use[bs] = m_pv
            pb_use[bs] = m_pv
        if i + 2 < len(seq):
            qload[i + 2] = emit_qload(i + 2)
        m_sx = vop(lambda e_, pacc=pacc: e_.tensor_copy(
            scr_f[0:1, 514:514 + SL], pacc[64:65, 0:SL]), deps=[m_pv, rrow_use])
        m_rb = mm(p_m0[0:64, 0:SL], ones_f[0:1, 0:64], scr_f[0:1, 514:514 + SL],
                  True, True, deps=[m_sx, fr["m0"]], inc=True)
        rrow_use = m_rb
        m_rcb = vop(lambda e_: e_.reciprocal(
            scr_f[0:64, 0:SL], p_m0[0:64, 0:SL]), deps=[m_rb], settle=True)
        fr["m0"] = m_rcb
        m_at = vop(lambda e_, pacc=pacc: e_.tensor_tensor(
            arB[0:64, 0:SL], pacc[0:64, 0:SL], scr_f[0:64, 0:SL], ALU.mult),
            deps=[m_rcb, arB_use], settle=True)
        acc_free[h] = m_at
        arB_use = dma("sync", acache.ap()[hp, s, 64 * h:64 * h + 64, :],
                      arB[0:64, 0:SL], deps=[m_at], sem="aw")
    if upto <= 8:
        ctx_close_out(P, nc, ctx, out, x_ar, xsl, dbg_t, dma, vop, act, locals())
        return nc

    P.barrier()
    fr = dict(m0=None, m1=None)
    scr_b = scr_f[:, :].bitcast(BF16)

    def wd_t(mi, k):
        idx = (mi % 4) * 24 + k
        if mi < 4:
            return xn_ar[:, idx * 128:(idx + 1) * 128]
        if idx < 64:
            return st_b[:, idx * 128:(idx + 1) * 128]
        return stg_b[:, (idx - 64) * 128:(idx - 63) * 128]

    def wo_t(mi, hp):
        if mi < 3:
            return wbufB[:, mi * 1024 + hp * 128:mi * 1024 + (hp + 1) * 128]
        if mi == 3:
            return xn_ar[:, 12288 + hp * 128:12288 + (hp + 1) * 128]
        if mi < 6:
            return stg_b[:, 4096 + (mi - 4) * 1024 + hp * 128:
                         4096 + (mi - 4) * 1024 + (hp + 1) * 128]
        return xn_ar[:, 13312 + (mi - 6) * 1024 + hp * 128:
                     13312 + (mi - 6) * 1024 + (hp + 1) * 128]

    def hct(k):
        if k < 12:
            return arC[:, k * 512:(k + 1) * 512]
        if k < 20:
            return pbuf[(k - 12) // 4][:, ((k - 12) % 4) * 512:((k - 12) % 4 + 1) * 512]
        return [qbuf[:, 0:512], kbuf[0][:, 0:512], kbuf[1][:, 0:512],
                scr_b[:, 0:512]][k - 20]

    def act_t(hp):
        return arA[:, hp * 512:(hp + 1) * 512]

    m_last_mm = None
    m_hr = m_h2 = None
    for c in range(NCH):
        ex = ([("hw0", 0), ("hw1", 0), ("hw2", 0), ("hw3", 0), ("aw", 0)]
              if c == 0 else [])
        for hp in range(NHP):
            m_h2 = dma("gpsimd", act_t(hp), acache.ap()[hp, c, :, :],
                       deps=(ex + [m_last_mm]) if hp == 0 else (), sem="h2")
        for k in range(24):
            q, s = (("sync", "hr") if (k < 12 or k >= 20) else ("gpsimd", "h2"))
            m = dma(q, hct(k), hcache.ap()[c, :, k * SL:(k + 1) * SL],
                    deps=(ex + [m_last_mm]) if k in (0, 12) else (), sem=s)
            if s == "hr":
                m_hr = m
            else:
                m_h2 = m
        for mi in range(8):
            pp = p_m0 if mi % 2 == 0 else p_m1
            frk = "m0" if mi % 2 == 0 else "m1"
            mmx = None
            for hp in range(NHP):
                mmx = mm(pp[:, 0:SL], wo_t(mi, hp), act_t(hp),
                         hp == 0, False,
                         deps=[m_h2, m_kw, fr[frk]], inc=False)
            for k in range(24):
                mmx = mm(pp[:, 0:SL], wd_t(mi, k), hct(k), False, k == 23,
                         deps=([m_hr, m_h2] if k == 0 else ()), inc=(k == 23))
            m_x2 = vop(lambda e_, mi=mi, c=c, pp=pp: e_.scalar_tensor_tensor(
                xsl(mi, c, 1, SL), pp[:, 0:SL],
                vec_sb[:, 52 + mi:53 + mi],
                xsl(mi, c, 1, SL), ALU.add, ALU.add),
                deps=[mmx])
            fr[frk] = m_x2
            m_last_mm = mmx
    P.barrier()
    m_out = []
    oev = {}
    for it in range(NCH * 4):
        c, tb = divmod(it, 4)
        half = it % 2
        m_tr3 = None
        for g in range(NG):
            m_tr3 = trans(p_st[:, half * 1024 + g * 128:half * 1024 + (g + 1) * 128],
                          xsl(g, c, 1 + tb * 128, 128), ident_fs[:, :],
                          deps=[oev.get(it - 2)])
        m_tr3 = pesettle()
        m_cp = None
        for g in range(NG):
            m_cp = vop(lambda e_, g=g, half=half: e_.tensor_copy(
                stg[:, half * 1024 + g * 128:half * 1024 + (g + 1) * 128],
                p_st[:, half * 1024 + g * 128:half * 1024 + (g + 1) * 128]),
                deps=[m_tr3, m_out[it - 2] if it >= 2 else None], inc=(g == NG - 1),
                settle=(g == NG - 1))
        oev[it] = m_cp
        m_out.append(dma(["sync", "gpsimd"][half], out.ap()[c, tb * 128:(tb + 1) * 128, :],
                         stg[:, half * 1024:half * 1024 + 1024], deps=[m_cp],
                         sem=f"ow{half}"))
    P.emit("sync", lambda e_: e_.nop(), deps=[m_out[-1], m_out[-2]])
    P.emit("gpsimd", lambda e_: e_.nop(), deps=[("cc", P.cnt["cc"])])
    if dbg_t:
        ctx_close_out(P, nc, ctx, out, x_ar, xsl, dbg_t, dma, vop, act, locals())
        return nc
    P.barrier()

    with nc.Block() as block:
        P.replay(block)
    ctx.close()
    return nc


def ctx_close_out(P, nc, ctx, out, x_ar, xsl, dbg_t, dma, vop, act, env):
    P.barrier()
    srcs = {"dbg_x": env["x_ar"], "dbg_xn": env["xn_ar"], "dbg_st": env["st_ar"],
            "dbg_row": env["row_ar"], "dbg_sum": env["sum_sb"],
            "dbg_arA": env["arA"], "dbg_arB": env["arB"], "dbg_arC": env["arC"],
            "dbg_scr": env["scr_f"], "dbg_sin": env["sin_sb"],
            "dbg_wi": env["w_in_sb"], "dbg_wo": env["w_out_sb"]}
    dram_srcs = {"dbg_ac": "acache", "dbg_h": "hcache", "dbg_q": "qcache",
                 "dbg_kall": "kv_kall", "dbg_vall": "kv_vall",
                 "dbg_kin": "kv_kin", "dbg_sout": "sum_out",
                 "dbg_wc0": "wc_l0", "dbg_wcd": "wc_wd",
                 "dbg_wco": "wc_wo", "dbg_wcq": "wc_qkv"}
    m_last = None
    if P.cnt.get("cc", 0):
        P.emit("sync", lambda e_: e_.nop(), deps=[("cc", P.cnt["cc"])])
    for nm, t in dbg_t.items():
        if nm == "dbg_u":
            continue
        if nm in dram_srcs:
            s_t = env[dram_srcs[nm]]
            m_last = dma("sync", t.ap().flatten(), s_t.ap().flatten())
        else:
            src = srcs[nm]
            w = t.ap().shape[1]
            m_last = dma("sync", t.ap(), src[:, 0:w])
    if m_last is not None:
        P.emit("sync", lambda e_: e_.nop(), deps=[m_last])
    if P.cnt.get("cc", 0):
        P.emit("gpsimd", lambda e_: e_.nop(), deps=[("cc", P.cnt["cc"])])
    P.barrier()
    with nc.Block() as block:
        P.replay(block)
    ctx.close()


def kernel(**inputs):
    shared = build_shared_consts()
    nc = build_program(upto=99)
    in_maps = [build_core_inputs(c, inputs, shared) for c in range(8)]
    res = run_bass_kernel_spmd(nc, in_maps, core_ids=list(range(8)))
    outp = np.zeros((2, 16 * SL, D), dtype=np.float32)
    for core in range(8):
        b, j = divmod(core, 4)
        for s, g in enumerate(chunks_of(j)):
            outp[b, g * SL:(g + 1) * SL, :] = res.results[core]["out"][s]
    return outp


# version 18
# speedup vs baseline: 1.1511x; 1.0034x over previous
import math
import numpy as np
import ml_dtypes
import concourse.bass as bass
import concourse.mybir as mybir
from concourse.bass_utils import run_bass_kernel_spmd

F32 = mybir.dt.float32
BF16 = mybir.dt.bfloat16
I32 = mybir.dt.int32
AF = mybir.ActivationFunctionType
ALU = mybir.AluOpType
AX = mybir.AxisListType.X
BF = ml_dtypes.bfloat16

D = 1024
NG = 8
SL = 512
NCH = 4
CW = SL + 1
NH = 16
NHP = 8
SC = 1.0 / math.sqrt(192.0)
SQ192 = math.sqrt(192.0)
NB = [2, 2, 2, 2, 2, 2, 3, 3, 4, 5, 7, 9, 12, 16, 16, 16]
NBP = [max(NB[2 * i], NB[2 * i + 1]) for i in range(8)]
SLOT_CAP = [4, 8, 12, 16]
ENG_NAMES = ["sync", "gpsimd", "tensor", "scalar", "vector"]


def slope8(h):
    return 8.0 * 2.0 ** (-(h + 1) / 2.0)


def chunks_of(j):
    return [j, 7 - j, 8 + j, 15 - j]


def owner_of(g):
    j = min(g, 7 - g) if g < 8 else min(g - 8, 15 - g)
    return j, chunks_of(j).index(g)


def group_struct():
    gs = []
    for s in range(NCH):
        for hp in range(NHP):
            gs.append((s, hp, min(SLOT_CAP[s], NBP[hp])))
    return gs


GROUPS = group_struct()
NJOBS = sum(n for _, _, n in GROUPS)


def make_jobs(j):
    jobs = []
    ch = chunks_of(j)
    for s, hp, njobs in GROUPS:
        pos = ch[s]
        for jb in range(njobs):
            kch = max(pos - jb, 0)
            r, s2 = owner_of(kch)
            jobs.append(dict(s=s, hp=hp, jb=jb, kch=kch, pad=(jb > pos),
                             rank=r, kslot=s2))
    return jobs


def split3(x):
    a = x.astype(BF)
    r = x - a.astype(np.float32)
    b = r.astype(BF)
    c = (r - b.astype(np.float32)).astype(BF)
    return a, b, c


_SHARED = None


def build_shared_consts():
    global _SHARED
    if _SHARED is not None:
        return _SHARED
    c = {}
    c["ident_f"] = np.eye(128, dtype=np.float32)
    c["ident_b"] = np.eye(128, dtype=BF)
    P = np.zeros((128, 128), dtype=np.float32)
    for m in range(128):
        blk, r = divmod(m, 32)
        P[[1, 0, 3, 2][blk] * 32 + r, m] = 1.0
    c["pswap"] = P.astype(BF)
    mk = np.zeros((128, 4 * SL), dtype=np.float32)
    for kb in range(4):
        for p in range(128):
            kl = kb * 128 + p
            mk[p, kb * SL:(kb + 1) * SL][np.arange(SL) < kl] = -4e9
    c["maskadd"] = mk
    bet = np.zeros((NH, 3, SL), dtype=BF)
    t = np.arange(SL, dtype=np.float32)
    for h in range(NH):
        bet[h, 0], bet[h, 1], bet[h, 2] = split3(slope8(h) * t * SQ192)
    c["beta"] = bet
    inv = 1.0 / (10000.0 ** (np.arange(0, 64, 2, dtype=np.float32) / 64))
    fr = np.outer(np.arange(16 * SL, dtype=np.float32), inv)
    sgn = np.concatenate([-np.ones(32), np.ones(32), -np.ones(32), np.ones(32)])
    c["cos_full"] = np.tile(np.cos(fr).T, (4, 1)).astype(BF)
    c["sin_full"] = (np.tile(np.sin(fr).T, (4, 1)) * sgn[:, None]).astype(BF)
    _SHARED = c
    return c


def build_core_inputs(core, inp, shared):
    b, j = divmod(core, 4)
    ch = chunks_of(j)
    d = {}
    tok = np.asarray(inp["L0_attn_tokens"])
    d["tok"] = np.ascontiguousarray(
        np.stack([tok[b, g * SL:(g + 1) * SL, :] for g in ch], axis=0))
    halo = np.zeros((NCH, D), dtype=np.float32)
    for s, g in enumerate(ch):
        if g > 0:
            halo[s] = tok[b, g * SL - 1, :]
    d["halo"] = halo

    def dmaj(v, ng):
        return np.ascontiguousarray(np.asarray(v).reshape(ng, 128).T)
    d["b_in"] = dmaj(inp["L0_attn_b_in"], 2)
    d["ldec"] = dmaj(inp["L0_attn_log_decay"], 2)
    d["normw"] = dmaj(inp["L0_attn_norm_w"], 8)
    d["bsum"] = np.ascontiguousarray(
        (np.asarray(inp["L0_attn_b_out"]).reshape(8, 128)
         + np.asarray(inp["L0_attn_b_skip"]).reshape(8, 128)).T)
    d["mur"] = dmaj(inp["L0_ffn_mu_r"], 8)
    d["muk"] = dmaj(inp["L0_ffn_mu_k"], 8)
    d["bout1"] = dmaj(inp["L1_attn_b_out"], 8)
    bq = np.asarray(inp["L1_attn_b_qkv"])
    qb = np.zeros((128, 8), np.float32)
    kb_ = np.zeros((128, 8), np.float32)
    vb = np.zeros((8, 128), np.float32)
    for hp in range(8):
        qb[0:64, hp] = bq[2 * hp, 0:64]
        qb[64:128, hp] = bq[2 * hp + 1, 0:64]
        kb_[0:64, hp] = bq[2 * hp, 64:128]
        kb_[64:128, hp] = bq[2 * hp + 1, 64:128]
        vb[hp, 0:64] = bq[2 * hp, 128:192]
        vb[hp, 64:128] = bq[2 * hp + 1, 128:192]
    d["bq_q"], d["bq_k"] = qb, kb_
    d["bq_v"] = np.ascontiguousarray(vb.reshape(1, 8 * 128))
    for nm, key in [("W_in", "L0_attn_W_in"), ("W_out", "L0_attn_W_out"),
                    ("W_skip", "L0_attn_W_skip"), ("W_r", "L0_ffn_W_r"),
                    ("W_k", "L0_ffn_W_k"), ("W_v", "L0_ffn_W_v"),
                    ("wg", "L1_ffn_w_gate"), ("wu", "L1_ffn_w_up"),
                    ("wd", "L1_ffn_w_down")]:
        d[nm] = np.asarray(inp[key])
    d["wqkv"] = np.ascontiguousarray(np.asarray(inp["L1_attn_w_qkv"]).reshape(D, NH * 192))
    d["wout1"] = np.ascontiguousarray(np.asarray(inp["L1_attn_w_out"]).reshape(NH * 64, D))
    for k in ("ident_f", "ident_b", "pswap", "maskadd", "beta"):
        d[k] = shared[k]
    cs = np.zeros((NCH, 128, SL), BF)
    sn = np.zeros((NCH, 128, SL), BF)
    for s, g in enumerate(ch):
        cs[s] = shared["cos_full"][:, g * SL:(g + 1) * SL]
        sn[s] = shared["sin_full"][:, g * SL:(g + 1) * SL]
    d["ropec"], d["ropes"] = cs, sn
    jobs = make_jobs(j)
    jof = np.zeros((128, NJOBS * 4), np.int32)
    tb2 = np.zeros((128, NJOBS * 2), np.float32)
    prow = np.arange(128, dtype=np.int32)
    for idx, jb in enumerate(jobs):
        for h in range(2):
            kr = ((jb["rank"] * NHP + jb["hp"]) * NCH + jb["kslot"]) * 128 + 64 * h
            vr = ((jb["rank"] * NH + 2 * jb["hp"] + h) * NCH + jb["kslot"]) * 128
            jof[:, idx * 4 + h] = kr + prow
            jof[:, idx * 4 + 2 + h] = vr + prow
            tb2[:, idx * 2 + h] = (-1e9 if jb["pad"]
                                   else slope8(2 * jb["hp"] + h) * (jb["kch"] * SL)) - 16.0
    d["jof"], d["tb2"] = jof, tb2
    qaug = np.zeros((NHP, NCH, 2, 3, SL), BF)
    for hp in range(NHP):
        for s in range(NCH):
            q = (ch[s] * SL + np.arange(SL)).astype(np.float32)
            for h in range(2):
                a, bb, cc = split3(-slope8(2 * hp + h) * q * SQ192)
                qaug[hp, s, h, 0], qaug[hp, s, h, 1], qaug[hp, s, h, 2] = a, bb, cc
    d["qaug"] = qaug
    sp = np.zeros((1, 8), np.int32)
    for g in range(2):
        for s in range(NCH):
            sp[0, g * NCH + s] = g * 17 + ch[s]
    d["spick"] = sp
    return d


class Prog:
    def __init__(self, nc, ctx):
        self.nc = nc
        self.ctx = ctx
        self.ops = {n: [] for n in ENG_NAMES}
        self.h = {}
        self.cnt = {}
        self.owner = {}
        self.lastwait = {n: {} for n in ENG_NAMES}
        self.dma_sems = {"sd", "gd"}

    def sem(self, name, owner=None):
        if name not in self.h:
            self.h[name] = self.ctx.enter_context(self.nc.semaphore(name))
            self.cnt[name] = 0
            self.owner[name] = owner
        return self.h[name]

    def emit(self, eng, fn, deps=(), inc=None, incn=1):
        for dep in deps:
            if dep is None:
                continue
            s, v = dep
            if self.owner.get(s) == eng and s not in self.dma_sems:
                continue
            if s in self.dma_sems:
                v = self.cnt[s]
            if self.lastwait[eng].get(s, -1) >= v:
                continue
            self.lastwait[eng][s] = v
            self.ops[eng].append(("w", (s, v)))
        if inc:
            self.cnt[inc] += incn
            self.ops[eng].append(("o", (fn, inc, incn)))
            return (inc, self.cnt[inc])
        self.ops[eng].append(("o", (fn, None, 0)))
        return None

    def barrier(self):
        self.sem("ph", owner=None)
        for n in ENG_NAMES:
            self.cnt["ph"] += 1
            self.ops[n].append(("o", (lambda e: e.nop(), "ph", 1)))
        v = self.cnt["ph"]
        for n in ENG_NAMES:
            self.lastwait[n]["ph"] = v
            self.ops[n].append(("w", ("ph", v)))

    def replay(self, block):
        for name in ENG_NAMES:
            mine = self.ops[name]

            def sec(eng, mine=mine, name=name):
                for oi, (kind, payload) in enumerate(mine):
                    if kind == "w":
                        s, v = payload
                        eng.wait_ge(self.h[s], v)
                    else:
                        fn, inc, incn = payload
                        try:
                            inst = fn(eng)
                        except Exception as ex:
                            raise RuntimeError(f"replay fail {name} op {oi}: {ex}") from ex
                        if inc is not None:
                            inst.then_inc(self.h[inc], incn)
            getattr(block, name)(sec)


def build_program(upto=99, dbg=()):
    from contextlib import ExitStack
    nc = bass.Bass("TRN2", target_bir_lowering=False, detect_race_conditions=False)
    ctx = ExitStack()
    e = ctx.enter_context

    def din(name, shape, dt):
        return nc.dram_tensor(name, list(shape), dt, kind="ExternalInput")

    tok = din("tok", (NCH, SL, D), F32)
    halo = din("halo", (NCH, D), F32)
    b_in = din("b_in", (128, 2), F32)
    ldec = din("ldec", (128, 2), F32)
    normw = din("normw", (128, 8), F32)
    bsum = din("bsum", (128, 8), F32)
    mur = din("mur", (128, 8), F32)
    muk = din("muk", (128, 8), F32)
    bout1 = din("bout1", (128, 8), F32)
    bq_q = din("bq_q", (128, 8), F32)
    bq_k = din("bq_k", (128, 8), F32)
    bq_v = din("bq_v", (1, 8 * 128), F32)
    W_in = din("W_in", (256, D), F32)
    W_out = din("W_out", (D, 256), F32)
    W_skip = din("W_skip", (D, D), F32)
    W_r = din("W_r", (D, D), F32)
    W_k = din("W_k", (D, D), F32)
    W_v = din("W_v", (D, D), F32)
    wqkv = din("wqkv", (D, NH * 192), F32)
    wout1 = din("wout1", (NH * 64, D), F32)
    wg = din("wg", (3072, D), I32)
    wu = din("wu", (3072, D), I32)
    wd = din("wd", (D, 3072), I32)
    ident_f = din("ident_f", (128, 128), F32)
    ident_b = din("ident_b", (128, 128), BF16)
    pswap = din("pswap", (128, 128), BF16)
    maskadd = din("maskadd", (128, 4 * SL), F32)
    beta = din("beta", (NH, 3, SL), BF16)
    ropec = din("ropec", (NCH, 128, SL), BF16)
    ropes = din("ropes", (NCH, 128, SL), BF16)
    jof = din("jof", (128, NJOBS * 4), I32)
    tb2 = din("tb2", (128, NJOBS * 2), F32)
    qaug = din("qaug", (NHP, NCH, 2, 3, SL), BF16)
    spick = din("spick", (1, 8), I32)

    out = nc.dram_tensor("out", [NCH, SL, D], F32, kind="ExternalOutput")
    dbg_t = {}
    for nm, shape, dt in dbg:
        dbg_t[nm] = nc.dram_tensor(nm, list(shape), dt, kind="ExternalOutput")

    wc_l0 = nc.dram_tensor("wc_l0", [4, 128, 8 * 8 * 128], BF16)
    wc_wd = nc.dram_tensor("wc_wd", [2, 128, 4 * 24 * 128], BF16)
    wc_wo = nc.dram_tensor("wc_wo", [2, 128, 4 * 8 * 128], BF16)
    wc_qkv = nc.dram_tensor("wc_qkv", [NHP, 128, 8 * 384], BF16)
    qcache = nc.dram_tensor("qcache", [NHP, NCH, 128, SL], BF16)
    acache = nc.dram_tensor("acache", [NHP, NCH, 128, SL], BF16)
    hcache = nc.dram_tensor("hcache", [NCH, 128, 24 * SL], BF16)
    sum_in = nc.dram_tensor("sum_in", [8, 128], F32)
    sum_out = nc.dram_tensor("sum_out", [32, 128], F32)
    kv_kin = nc.dram_tensor("kv_kin", [NHP * NCH * 128, SL], BF16)
    kv_vin = nc.dram_tensor("kv_vin", [NH * NCH * 128, 260], BF16)
    kv_kall = nc.dram_tensor("kv_kall", [4 * NHP * NCH * 128, SL], BF16)
    kv_vall = nc.dram_tensor("kv_vall", [4 * NH * NCH * 128, 260], BF16)
    bounce = nc.dram_tensor("bounce", [128, 34], F32)

    x_ar = e(nc.sbuf_tensor("x_ar", [128, NG * NCH * CW], F32))
    xn_ar = e(nc.sbuf_tensor("xn_ar", [128, NG * NCH * CW], BF16))
    arA = e(nc.sbuf_tensor("arA", [128, NG * CW], BF16))
    arB = e(nc.sbuf_tensor("arB", [128, NG * CW], BF16))
    arC = e(nc.sbuf_tensor("arC", [128, 6144], BF16))
    st_ar = e(nc.sbuf_tensor("st_ar", [128, 2 * NCH * SL], F32))
    row_ar = e(nc.sbuf_tensor("row_ar", [128, 2 * CW], F32))
    scr_f = e(nc.sbuf_tensor("scr_f", [128, 2 * CW], F32))
    stg = e(nc.sbuf_tensor("stg", [128, 3072], F32))
    wbufB = e(nc.sbuf_tensor("wbufB", [128, 3072], BF16))
    vec_sb = e(nc.sbuf_tensor("vec_sb", [128, 66], F32))
    vrow_b = e(nc.sbuf_tensor("vrow_b", [1, 8 * 128], BF16))
    ident_fs = e(nc.sbuf_tensor("ident_fs", [128, 128], F32))
    ident_bs = e(nc.sbuf_tensor("ident_bs", [128, 128], BF16))
    pswap_s = e(nc.sbuf_tensor("pswap_s", [128, 128], BF16))
    sh8 = e(nc.sbuf_tensor("sh8", [128, 2048], F32))
    iotab = e(nc.sbuf_tensor("iotab", [128, SL], F32))
    onesb = e(nc.sbuf_tensor("onesb", [128, 1], BF16))
    ones_f = e(nc.sbuf_tensor("ones_f", [65, 128], F32))
    ones_bb = e(nc.sbuf_tensor("ones_bb", [1, 128], BF16))
    sum_sb = e(nc.sbuf_tensor("sum_sb", [128, 72], F32))
    spick_s = e(nc.sbuf_tensor("spick_s", [1, 8], I32))
    jot = e(nc.sbuf_tensor("jot", [128, NJOBS * 4], I32))
    tbb = e(nc.sbuf_tensor("tbb", [128, NJOBS * 2], F32))
    sin_sb = e(nc.sbuf_tensor("sin_sb", [128, 8], F32))
    w_in_sb = e(nc.sbuf_tensor("w_in_sb", [128, 8 * 256], BF16))
    w_out_sb = e(nc.sbuf_tensor("w_out_sb", [128, 2 * 1024], BF16))
    kbuf = [e(nc.sbuf_tensor(f"kbuf{i}", [128, SL], BF16)) for i in range(2)]
    vbuf = [e(nc.sbuf_tensor(f"vbuf{i}", [128, 4 * 65], BF16)) for i in range(2)]
    qbuf = e(nc.sbuf_tensor("qbuf", [128, SL], BF16))
    qbuf2 = e(nc.sbuf_tensor("qbuf2", [128, SL], BF16))
    pbuf = [e(nc.sbuf_tensor(f"pbuf{i}", [128, 4 * SL], BF16)) for i in range(2)]
    epsc = e(nc.sbuf_tensor("epsc", [128, 2], F32))

    p_st = e(nc.psum_tensor("p_st", [128, 4 * SL], F32))
    p_a0 = e(nc.psum_tensor("p_a0", [128, SL], F32))
    p_a1 = e(nc.psum_tensor("p_a1", [128, SL], F32))
    p_m0 = e(nc.psum_tensor("p_m0", [128, SL], F32))
    p_m1 = e(nc.psum_tensor("p_m1", [128, SL], F32))

    P = Prog(nc, ctx)
    for s_, o_ in [("sd", "sync"), ("gd", "gpsimd"), ("pe", "tensor"),
                   ("se", "scalar"), ("ve", "vector"), ("gc", "gpsimd"),
                   ("cc", None)]:
        P.sem(s_, owner=o_)

    rS = nc.gpsimd.alloc_register("rS")

    def xsl(g, c, c0, w):
        return x_ar[:, (g * NCH + c) * CW + c0:(g * NCH + c) * CW + c0 + w]

    def xnsl(g, c, c0, w):
        return xn_ar[:, (g * NCH + c) * CW + c0:(g * NCH + c) * CW + c0 + w]

    def dma(q, o, i, deps=(), slow=False, dbl=False, sem=None):
        kw = dict(allow_slow_non_contiguous=True) if slow else {}
        s = sem or ("sd" if q == "sync" else "gd")
        if s not in P.h:
            P.sem(s, owner=q)
            P.dma_sems.add(s)
        m = P.emit(q, lambda e_, o=o, i=i, kw=kw: e_.dma_start(out=o, in_=i, **kw),
                   deps=deps, inc=s, incn=16)
        if dbl:
            m = P.emit(q, lambda e_, o=o, i=i, kw=kw: e_.dma_start(out=o, in_=i, **kw),
                       inc=s, incn=16)
        return m

    def mm(pp, lhsT, rhs, start, stop, deps=(), inc=False):
        return P.emit("tensor", lambda e_, pp=pp, l=lhsT, r=rhs, s1=start, s2=stop:
                      e_.matmul(pp, l, r, start=s1, stop=s2), deps=deps,
                      inc=("pe" if inc else None))

    def act(out_, in_, f, deps=(), bias=None, scale=None, inc=True, settle=False):
        kw = {}
        if bias is not None:
            kw["bias"] = bias
        if scale is not None:
            kw["scale"] = scale
        m = P.emit("scalar", lambda e_, o=out_, i=in_, f=f, kw=kw:
                   e_.activation(o, i, f, **kw), deps=deps,
                   inc=("se" if (inc or settle) else None))
        return m

    def vop(fn, deps=(), inc=True, settle=False):
        m = P.emit("vector", fn, deps=deps,
                   inc=("ve" if (inc or settle) else None))
        return m

    def trans(out_, in_, ident, deps=()):
        return P.emit("tensor", lambda e_, o=out_, i=in_, d=ident:
                      e_.transpose(o, i, d), deps=deps, inc="pe")

    def pesettle():
        return P.emit("tensor", lambda e_: e_.nop(), inc="pe")

    m_vec = None
    for src, dst in [(b_in, vec_sb[:, 0:2]), (ldec, vec_sb[:, 2:4]),
                     (normw, vec_sb[:, 4:12]), (bsum, vec_sb[:, 12:20]),
                     (mur, vec_sb[:, 20:28]), (muk, vec_sb[:, 28:36]),
                     (bq_q, vec_sb[:, 36:44]), (bq_k, vec_sb[:, 44:52]),
                     (bout1, vec_sb[:, 52:60]),
                     (ident_f, ident_fs[:, :]), (ident_b, ident_bs[:, :]),
                     (pswap, pswap_s[:, :]), (spick, spick_s[:, :])]:
        m_vec = dma("sync", dst, src.ap())
    for c in range(NCH):
        for g in range(NG):
            dma("sync", xsl(g, c, 0, 1),
                halo.ap()[c:c + 1, g * 128:(g + 1) * 128].rearrange("a p -> p a"),
                slow=True)
    m_mem = None
    for t_, v_ in [(onesb[:, :], 1.0), (ones_f[:, :], 1.0), (ones_bb[:, :], 1.0),
                   (qbuf[64:128, :], 0.0), (qbuf2[64:128, :], 0.0),
                   (qbuf[64:67, :], 1.0), (qbuf2[64:67, :], 1.0),
                   (kbuf[0][64:96, :], 1.0), (kbuf[1][64:96, :], 1.0),
                   (kbuf[0][96:128, :], 0.0), (kbuf[1][96:128, :], 0.0),
                   (sum_sb[:, :], 0.0), (epsc[:, 0:1], 1e-5), (epsc[:, 1:2], 1e-4)]:
        m_mem = P.emit("gpsimd", lambda e_, t=t_, v=v_: e_.memset(t, v), inc="gc")
    m_iota = P.emit("gpsimd", lambda e_: e_.iota(
        iotab[:, :], pattern=[[1, SL]], base=1, channel_multiplier=0,
        allow_small_or_imprecise_dtypes=True), inc="gc")
    dma("gpsimd", bass.AP(bounce, 0, [[34, 128], [1, 1]]), sum_sb[:, 0:1], deps=[m_mem], slow=True)
    dma("gpsimd", bass.AP(bounce, 17, [[34, 128], [1, 1]]), sum_sb[:, 0:1], slow=True)
    m_dec = act(vec_sb[:, 60:62], vec_sb[:, 2:4], AF.Sigmoid, deps=[m_vec],
                settle=True)
    m_L = act(vec_sb[:, 62:64], vec_sb[:, 60:62], AF.Ln)
    m_l = dma("sync", scr_f[0:1, 0:1024], bq_v.ap())
    m_bqv = act(vrow_b[:, :], scr_f[0:1, 0:1024], AF.Copy, deps=[m_l], settle=True)

    if upto <= 0:
        ctx_close_out(P, nc, ctx, out, x_ar, xsl, dbg_t, dma, vop, act, locals())
        return nc

    P.barrier()
    ing_ev = {}
    for it in range(NCH * 4):
        c, tb = divmod(it, 4)
        half = it % 2
        m_l = dma(["sync", "gpsimd"][half], stg[:, half * 1024:half * 1024 + 1024],
                  tok.ap()[c, tb * 128:(tb + 1) * 128, :],
                  deps=[ing_ev.get(it - 2)], sem=f"il{half}")
        m_tr = None
        for g in range(NG):
            m_tr = trans(p_st[:, half * 1024 + g * 128:half * 1024 + (g + 1) * 128],
                         stg[:, half * 1024 + g * 128:half * 1024 + (g + 1) * 128],
                         ident_fs[:, :], deps=[m_l, ing_ev.get(it - 2)])
        m_tr = pesettle()
        evs = None
        for g in range(NG):
            evs = vop(lambda e_, g=g, c=c, tb=tb, half=half: e_.tensor_copy(
                xsl(g, c, 1 + tb * 128, 128),
                p_st[:, half * 1024 + g * 128:half * 1024 + (g + 1) * 128]),
                deps=[m_tr], inc=(g == NG - 1))
        ing_ev[it] = evs
    if upto <= 1:
        ctx_close_out(P, nc, ctx, out, x_ar, xsl, dbg_t, dma, vop, act, locals())
        return nc

    P.barrier()

    prep_chain = [None]
    prep_chain_cv = [None]
    prep_chain_out = [(None, None)]

    PREP_SLOTS = [(stg, 0), (stg, 1024), (stg, 2048),
                  (sh8, 0), (sh8, 1024), (scr_f, 0)]
    PREP_QS = ["sync", "gpsimd", "scalar"]

    def prep_tiles(n4, src_fn, dst_fn, out_fn=None, int_scale=None, ntr=4,
                   load_q=None, load_sem=None):
        prev_cv = {k: prep_chain_cv[0] for k in range(-6, 0)}
        prev_ev = {-1: prep_chain[0], -2: prep_chain[0]}
        prev_out = {-1: prep_chain_out[0][1] or prep_chain[0],
                    -2: prep_chain_out[0][0] or prep_chain[0]}
        w_ = 128 * ntr

        def slot_view(i4):
            st_t, st_c = PREP_SLOTS[i4 % 6]
            return (st_t[:, :].bitcast(I32)[:, st_c:st_c + w_]
                    if int_scale else st_t[:, st_c:st_c + w_])

        def emit_ld(i4):
            return dma(PREP_QS[i4 % 3], slot_view(i4), src_fn(i4),
                       deps=[prev_cv.get(i4 - 6)], sem=f"pl{i4 % 6}")

        lds = {}
        for i4 in range(min(6, n4)):
            lds[i4] = emit_ld(i4)
        for i4 in range(n4):
            half = i4 % 2
            stg_v = slot_view(i4)
            m_ld = lds[i4]
            m_cv = act(arA[:, half * 2052:half * 2052 + w_], stg_v, AF.Copy,
                       deps=[m_ld, prev_ev.get(i4 - 2)], scale=int_scale,
                       settle=True)
            prev_cv[i4] = m_cv
            def ptile(q, i4=i4):
                if i4 % 2 == 0:
                    return p_st[:, :].bitcast(BF16)[:, q * 1024:q * 1024 + 128]
                pb = p_m0 if q < 2 else p_m1
                return pb[:, :].bitcast(BF16)[:, (q % 2) * 128:(q % 2) * 128 + 128]
            m_tr = None
            for q in range(ntr):
                m_tr = trans(
                    ptile(q),
                    arA[:, half * 2052 + q * 128:half * 2052 + (q + 1) * 128],
                    ident_bs[:, :], deps=[m_cv, prev_ev.get(i4 - 2)])
            m_tr = pesettle()
            m_ev = None
            for q in range(ntr):
                last = (q == ntr - 1)
                m_ev = vop(lambda e_, q=q, i4=i4: e_.tensor_copy(
                    dst_fn(i4, q), ptile(q, i4)),
                    deps=[m_tr, prev_out.get(i4 - 2)] if q == 0 else [m_tr],
                    inc=last, settle=last)
            prev_ev[i4] = m_ev
            prev_out[i4] = out_fn(i4, m_ev) if out_fn else m_ev
            if i4 + 6 < n4:
                lds[i4 + 6] = emit_ld(i4 + 6)
        prep_chain[0] = prev_ev[n4 - 1]
        prep_chain_cv[0] = prev_cv[n4 - 1]
        if out_fn:
            prep_chain_out[0] = (prev_out.get(n4 - 2), prev_out.get(n4 - 1))
        return prev_out

    l0_srcs = [W_r, W_k, W_v, W_skip]
    l0_outs = prep_tiles(
        64,
        lambda i4: l0_srcs[i4 // 16].ap()[((i4 // 2) % 8) * 128:((i4 // 2) % 8 + 1) * 128,
                                          (i4 % 2) * 512:(i4 % 2 + 1) * 512],
        lambda i4, q: arB[:, (i4 % 2) * 2052 + q * 128:(i4 % 2) * 2052 + (q + 1) * 128],
        out_fn=lambda i4, mev: dma(
            "gpsimd",
            wc_l0.ap()[i4 // 16, :, ((i4 // 2) % 8) * 1024 + (i4 % 2) * 512:
                       ((i4 // 2) % 8) * 1024 + (i4 % 2) * 512 + 512],
            arB[:, (i4 % 2) * 2052:(i4 % 2) * 2052 + 512], deps=[mev],
            sem=f"pc{i4 % 2}"))
    prep_tiles(
        4,
        lambda i4: W_in.ap()[(i4 // 2) * 128:(i4 // 2 + 1) * 128,
                             (i4 % 2) * 512:(i4 % 2 + 1) * 512],
        lambda i4, q: w_in_sb[:, ((i4 % 2) * 4 + q) * 256 + (i4 // 2) * 128:
                              ((i4 % 2) * 4 + q) * 256 + (i4 // 2 + 1) * 128])
    prep_tiles(
        8,
        lambda i4: W_out.ap()[i4 * 128:(i4 + 1) * 128, 0:256],
        lambda i4, q: w_out_sb[:, q * 1024 + i4 * 128:q * 1024 + (i4 + 1) * 128],
        ntr=2)
    wd_outs = prep_tiles(
        48,
        lambda i4: wd.ap()[(i4 // 6) * 128:(i4 // 6 + 1) * 128,
                           (i4 % 6) * 512:(i4 % 6 + 1) * 512],
        lambda i4, q: arB[:, (i4 % 2) * 2052 + q * 128:(i4 % 2) * 2052 + (q + 1) * 128],
        out_fn=lambda i4, mev: dma(
            "scalar",
            wc_wd.ap()[(i4 // 6) // 4, :, ((i4 // 6) % 4) * 3072 + (i4 % 6) * 512:
                       ((i4 // 6) % 4) * 3072 + (i4 % 6) * 512 + 512],
            arB[:, (i4 % 2) * 2052:(i4 % 2) * 2052 + 512], deps=[mev],
            sem=f"pc{2 + i4 % 2}"),
        int_scale=1.0 / 127.0, load_q="gpsimd", load_sem="pl2")
    prevo = {}
    prevo_cv = {}
    for m in range(8):
        half = m % 2
        m_ld = None
        for a in range(8):
            m_ld = dma("sync", stg[:, half * 1024 + a * 128:half * 1024 + (a + 1) * 128],
                       wout1.ap()[a * 128:(a + 1) * 128, m * 128:(m + 1) * 128],
                       deps=([prevo.get(m - 2), prep_chain_cv[0]] if a == 0 else ()),
                       sem="pl")
        m_cv = act(arA[:, half * 2052:half * 2052 + 1024],
                   stg[:, half * 1024:half * 1024 + 1024], AF.Copy,
                   deps=[m_ld, prep_chain[0] if m < 2 else None], settle=True)
        prevo_cv[m] = m_cv
        prevo[m] = dma("gpsimd", wc_wo.ap()[m // 4, :, (m % 4) * 1024:(m % 4) * 1024 + 1024],
                       arA[:, half * 2052:half * 2052 + 1024], deps=[m_cv],
                       sem="pw")
    prevq = {}
    prevsh = {}
    for dc in range(8):
        m_ld2 = dma("gpsimd", stg[:, 1536:3072],
                    wqkv.ap()[dc * 128:(dc + 1) * 128, 1536:3072],
                    sem="pl4", deps=[prevsh.get(dc - 1),
                          (prevo_cv[7] if dc == 0 else None),
                          (prevo_cv[6] if dc == 0 else None)])
        m_ld = dma("scalar", stg[:, 0:1536], wqkv.ap()[dc * 128:(dc + 1) * 128, 0:1536],
                   sem="pl3", deps=[prevsh.get(dc - 1),
                         (prevo_cv[7] if dc == 0 else None),
                         (prevo_cv[6] if dc == 0 else None)])
        m_sh = None
        for kind in range(3):
            m_sh = vop(lambda e_, kind=kind: e_.tensor_copy(
                bass.AP(arB, kind * 128, [[4104, 128], [384, 8], [64, 2], [1, 64]]),
                bass.AP(stg, kind * 64, [[3072, 128], [384, 8], [192, 2], [1, 64]])),
                deps=([m_ld, m_ld2, prevq.get(dc - 1),
                       (prevo[7] if dc == 0 else None),
                       (prevo[6] if dc == 0 else None),
                       (wd_outs.get(46) if dc == 0 else None),
                       (wd_outs.get(47) if dc == 0 else None)]
                      if kind == 0 else [m_ld, m_ld2]),
                inc=(kind == 2), settle=(kind == 2))
        prevsh[dc] = m_sh
        m_o = None
        for hp in range(8):
            m_o = dma("gpsimd", wc_qkv.ap()[hp, :, dc * 384:(dc + 1) * 384],
                      arB[:, hp * 384:(hp + 1) * 384],
                      deps=[m_sh] if hp == 0 else (), sem="pw")
        prevq[dc] = m_o
    m_prepq = prevq[7]
    if upto <= 2:
        ctx_close_out(P, nc, ctx, out, x_ar, xsl, dbg_t, dma, vop, act, locals())
        return nc

    P.barrier()

    def emit_layernorm():
        outs = []
        pfree = None
        for c in range(NCH):
            m_cv = None
            for g in range(NG):
                m_cv = act(arA[:, g * CW:(g + 1) * CW], xsl(g, c, 0, CW), AF.Copy,
                           inc=(g == NG - 1))
            m_sq = None
            for g in range(NG):
                m_sq = act(arB[:, g * CW:(g + 1) * CW], arA[:, g * CW:(g + 1) * CW],
                           AF.Square, inc=(g == NG - 1))
            m_s1 = None
            for w0, w1 in [(0, 512), (512, 513)]:
                for g in range(NG):
                    m_s1 = mm(p_st[0:1, w0:w1], onesb[:, :],
                              arA[:, g * CW + w0:g * CW + w1],
                              g == 0, g == NG - 1, deps=[m_cv, pfree],
                              inc=(g == NG - 1 and w0 == 512))
            m_s2 = None
            for w0, w1 in [(0, 512), (512, 513)]:
                for g in range(NG):
                    m_s2 = mm(p_st[0:1, 1024 + w0:1024 + w1], onesb[:, :],
                              arB[:, g * CW + w0:g * CW + w1],
                              g == 0, g == NG - 1, deps=[m_sq, pfree],
                              inc=(g == NG - 1 and w0 == 512))
            m_mean = vop(lambda e_: e_.tensor_scalar(
                row_ar[0:1, 0:CW], p_st[0:1, 0:CW], 1.0 / 1024.0, None, ALU.mult),
                deps=[m_s1])
            m_e2 = vop(lambda e_: e_.tensor_scalar(
                row_ar[64:65, 0:CW], p_st[0:1, 1024:1024 + CW], 1.0 / 1024.0, None,
                ALU.mult), deps=[m_s2])
            vop(lambda e_: e_.tensor_tensor(
                row_ar[64:65, CW:2 * CW], row_ar[0:1, 0:CW], row_ar[0:1, 0:CW],
                ALU.mult), inc=False)
            m_var = vop(lambda e_: e_.tensor_tensor(
                scr_f[0:1, 0:CW], row_ar[64:65, 0:CW], row_ar[64:65, CW:2 * CW],
                ALU.subtract))
            m_ln = act(scr_f[0:1, CW:2 * CW], scr_f[0:1, 0:CW], AF.Ln,
                       deps=[m_var], bias=epsc[0:1, 0:1])
            m_rstd = act(row_ar[0:1, CW:2 * CW], scr_f[0:1, CW:2 * CW], AF.Exp,
                         scale=-0.5, settle=True)
            m_bm = None
            for w0, w1 in [(0, 512), (512, 513)]:
                m_bm = mm(p_st[:, w0:w1], ones_f[0:1, :], row_ar[0:1, w0:w1], True, True,
                          deps=[m_mean, m_e2], inc=(w0 == 512))
            m_br = None
            for w0, w1 in [(0, 512), (512, 513)]:
                m_br = mm(p_st[:, 1024 + w0:1024 + w1], ones_f[0:1, :],
                          row_ar[0:1, CW + w0:CW + w1], True, True,
                          deps=[m_rstd], inc=(w0 == 512))
            m_xn = None
            for g in range(NG):
                vop(lambda e_, g=g, c=c: e_.tensor_tensor(
                    scr_f[:, 0:CW], xsl(g, c, 0, CW), p_st[:, 0:CW], ALU.subtract),
                    deps=[m_bm, m_br], inc=False)
                m_xn = vop(lambda e_, g=g, c=c: e_.tensor_tensor(
                    xnsl(g, c, 0, CW), scr_f[:, 0:CW], p_st[:, 1024:1024 + CW],
                    ALU.mult), inc=(g == NG - 1))
            pfree = m_xn
            outs.append(m_xn)
        return outs

    xn_ms = emit_layernorm()
    if upto <= 3:
        ctx_close_out(P, nc, ctx, out, x_ar, xsl, dbg_t, dma, vop, act, locals())
        return nc

    P.barrier()
    m_dt = []
    for n2 in range(2):
        m_dt.append(act(sh8[:, 1024 + n2 * SL:1024 + (n2 + 1) * SL], iotab[:, :],
                        AF.Exp, deps=[m_iota, m_L], scale=vec_sb[:, 62 + n2:63 + n2],
                        settle=True))
    act(sum_sb[:, 64:66], vec_sb[:, 62:64], AF.Copy, deps=[m_L], scale=512.0)
    m_d512 = act(sum_sb[:, 64:66], sum_sb[:, 64:66], AF.Exp, settle=True)

    m_scan = {}
    scr_guard = [None]
    fr = dict(a0=None, a1=None, m0=None, m1=None)
    wload = {}
    wfree = {}
    nload = [0]

    def wload_next(src_ap, deps=()):
        i = nload[0]
        nload[0] += 1
        sl_ = i % 3
        extra = []
        if i < 3:
            extra = [("pc0", P.cnt.get("pc0", 0)), ("pc1", P.cnt.get("pc1", 0))]
        wload[i] = dma(["sync", "gpsimd", "sync"][sl_],
                       wbufB[:, sl_ * 1024:(sl_ + 1) * 1024], src_ap,
                       sem=f"ws{sl_}", deps=list(deps) + [wfree.get(i - 3)] + extra)
        return i

    def wtile(i, k):
        sl_ = i % 3
        return wbufB[:, sl_ * 1024 + k * 128:sl_ * 1024 + (k + 1) * 128]

    for c in range(NCH):
        m_sq = None
        for g in range(NG):
            m_sq = act(arB[:, g * SL:(g + 1) * SL], xnsl(g, c, 1, SL), AF.Square,
                       deps=[xn_ms[c]] if g == 0 else (), inc=(g == NG - 1))
        m_ms = None
        for g in range(NG):
            m_ms = mm(p_a0[0:1, 0:SL], onesb[:, :], arB[:, g * SL:(g + 1) * SL],
                      g == 0, g == NG - 1, deps=[m_sq, fr["a0"]], inc=(g == NG - 1))
        m_msr = vop(lambda e_: e_.tensor_scalar(
            row_ar[64:65, 0:SL], p_a0[0:1, 0:SL], 1.0 / 1024.0, None, ALU.mult),
            deps=[m_ms])
        fr["a0"] = m_msr
        act(row_ar[64:65, CW:CW + SL], row_ar[64:65, 0:SL], AF.Ln,
            deps=[m_msr], bias=epsc[64:65, 1:2])
        m_rs2 = act(row_ar[64:65, 0:SL], row_ar[64:65, CW:CW + SL], AF.Exp,
                    scale=-0.5, settle=True)
        m_b2 = mm(p_m1[:, 0:SL], ones_f[64:65, :], row_ar[64:65, 0:SL], True, True,
                  deps=[m_rs2, fr["m1"]], inc=True)
        m_xh = None
        for g in range(NG):
            m_xh = vop(lambda e_, g=g, c=c: e_.scalar_tensor_tensor(
                arA[:, g * SL:(g + 1) * SL], xnsl(g, c, 1, SL), vec_sb[:, 4 + g:5 + g],
                p_m1[:, 0:SL], ALU.mult, ALU.mult), deps=[m_b2],
                inc=(g == NG - 1), settle=(g == NG - 1))
        fr["m1"] = m_xh
        m_u = []
        for n2 in range(2):
            pp = p_a0 if n2 == 0 else p_a1
            mmx = None
            for g in range(NG):
                mmx = mm(pp[:, 0:SL], w_in_sb[:, g * 256 + n2 * 128:g * 256 + (n2 + 1) * 128],
                         arA[:, g * SL:(g + 1) * SL], g == 0, g == NG - 1,
                         deps=[m_xh, fr["a0"] if n2 == 0 else fr["a1"],
                               m_scan.get((c - 1, n2))], inc=(g == NG - 1))
            m_u.append(mmx)
        m_ue = []
        for n2 in range(2):
            pp = p_a0 if n2 == 0 else p_a1
            m_ue.append(vop(lambda e_, n2=n2, pp=pp: e_.tensor_scalar(
                scr_f[:, n2 * CW:n2 * CW + SL], pp[:, 0:SL],
                vec_sb[:, 0 + n2:1 + n2], None, ALU.add),
                deps=[m_u[n2], scr_guard[0]]))
            fr["a0" if n2 == 0 else "a1"] = m_ue[-1]
        if c == 0 and "dbg_u" in dbg_t:
            dma("sync", dbg_t["dbg_u"].ap(), scr_f[:, 0:2 * CW],
                deps=[m_ue[0], m_ue[1]])
        for n2 in range(2):
            dcol = vec_sb[:, 60 + n2:61 + n2]
            d0 = bass.AP(dcol.tensor, dcol.offset,
                         [[list(dcol.ap)[0][0], 128], [0, SL]])
            m_scan[(c, n2)] = vop(lambda e_, n2=n2, c=c, d0=d0: e_.tensor_tensor_scan(
                st_ar[:, (n2 * NCH + c) * SL:(n2 * NCH + c + 1) * SL], d0,
                scr_f[:, n2 * CW:n2 * CW + SL], 0.0, ALU.mult, ALU.add),
                deps=[m_ue[n2], m_dec])
        if c == NCH - 1:
            m_smc = None
            for n2 in range(2):
                for cc_ in range(NCH):
                    m_smc = vop(lambda e_, n2=n2, cc_=cc_: e_.tensor_copy(
                        sum_sb[:, cc_ * 2 + n2:cc_ * 2 + n2 + 1],
                        st_ar[:, (n2 * NCH + cc_ + 1) * SL - 1:(n2 * NCH + cc_ + 1) * SL]),
                        deps=[m_scan[(cc_, n2)]], inc=(n2 == 1 and cc_ == NCH - 1),
                        settle=(n2 == 1 and cc_ == NCH - 1))
            m_str = trans(p_m1[0:8, 0:128], sum_sb[:, 0:8], ident_fs[:, :],
                          deps=[m_smc, fr["m1"]])
            m_str = pesettle()
            m_sev = vop(lambda e_: e_.tensor_copy(sh8[0:8, 0:128], p_m1[0:8, 0:128]),
                        deps=[m_str], settle=True)
            fr["m1"] = m_sev
            m_sdma = dma("gpsimd", sum_in.ap(), sh8[0:8, 0:128], deps=[m_sev])
            m_sdma = dma("gpsimd", sum_in.ap(), sh8[0:8, 0:128], deps=[m_sdma])
            m_cc1 = P.emit("gpsimd", lambda e_: e_.collective_compute(
                "AllGather", ALU.bypass, replica_groups=[[0, 1, 2, 3], [4, 5, 6, 7]],
                ins=[sum_in.ap().opt()], outs=[sum_out.ap().opt()]),
                deps=[m_sdma], inc="cc")
            m_cc1 = P.emit("gpsimd", lambda e_: e_.collective_compute(
                "AllGather", ALU.bypass, replica_groups=[[0, 1, 2, 3], [4, 5, 6, 7]],
                ins=[sum_in.ap().opt()], outs=[sum_out.ap().opt()]),
                inc="cc")
        m_add = {}
        for m in range(8):
            i = wload_next(wc_l0.ap()[3, :, m * 1024:(m + 1) * 1024])
            pp = p_m0 if m % 2 == 0 else p_a0
            mmx = None
            for k in range(8):
                mmx = mm(pp[:, 0:SL], wtile(i, k), arA[:, k * SL:(k + 1) * SL],
                         k == 0, k == 7,
                         deps=[wload[i], m_xh, fr["m0"] if m % 2 == 0 else fr["a0"],
                               m_scan[(c, 0)] if m % 2 == 1 else None],
                         inc=(k == 7))
            wfree[i] = mmx
            m_add[m] = vop(lambda e_, m=m, c=c, pp=pp: e_.scalar_tensor_tensor(
                xsl(m, c, 1, SL), pp[:, 0:SL], vec_sb[:, 12 + m:13 + m],
                xsl(m, c, 1, SL), ALU.add, ALU.add), deps=[mmx])
            fr["m0" if m % 2 == 0 else "a0"] = m_add[m]
        for g in range(NG):
            vop(lambda e_, g=g, c=c: e_.tensor_tensor(
                xsl(g, c, 1, SL), xsl(g, c, 1, SL), xnsl(g, c, 1, SL), ALU.add),
                deps=[m_add[g]], inc=(g == NG - 1))
        mix_ev = {}
        for g in range(NG):
            vop(lambda e_, g=g, c=c: e_.tensor_tensor(
                arC[:, 0:SL], xnsl(g, c, 1, SL), xnsl(g, c, 0, SL), ALU.subtract),
                deps=[m_add[7], m_u[0], m_u[1]] if g == 0 else (), inc=False)
            vop(lambda e_, g=g, c=c: e_.scalar_tensor_tensor(
                arA[:, g * SL:(g + 1) * SL], arC[:, 0:SL], vec_sb[:, 20 + g:21 + g],
                xnsl(g, c, 0, SL), ALU.mult, ALU.add), inc=False)
            mix_ev[g] = vop(lambda e_, g=g, c=c: e_.scalar_tensor_tensor(
                arB[:, g * SL:(g + 1) * SL], arC[:, 0:SL], vec_sb[:, 28 + g:29 + g],
                xnsl(g, c, 0, SL), ALU.mult, ALU.add), inc=True,
                settle=(g == NG - 1))
        m_mix = mix_ev[NG - 1]
        m_r = {}
        for m in range(8):
            i = wload_next(wc_l0.ap()[0, :, m * 1024:(m + 1) * 1024])
            pp = p_m0 if m % 2 == 0 else p_a0
            mmx = None
            for k in range(8):
                mmx = mm(pp[:, 0:SL], wtile(i, k), arA[:, k * SL:(k + 1) * SL],
                         k == 0, k == 7,
                         deps=[wload[i], mix_ev[k],
                               fr["m0"] if m % 2 == 0 else fr["a0"]],
                         inc=(k == 7))
            wfree[i] = mmx
            m_r[m] = act(arC[:, 512 + m * SL:512 + (m + 1) * SL], pp[:, 0:SL],
                         AF.Sigmoid, deps=[mmx], settle=True)
            fr["m0" if m % 2 == 0 else "a0"] = m_r[m]
        m_k2 = {}
        for m in range(8):
            i = wload_next(wc_l0.ap()[1, :, m * 1024:(m + 1) * 1024])
            pp = p_m1 if m % 2 == 0 else p_a1
            mmx = None
            for k in range(8):
                mmx = mm(pp[:, 0:SL], wtile(i, k), arB[:, k * SL:(k + 1) * SL],
                         k == 0, k == 7,
                         deps=[wload[i], mix_ev[k], m_r[7] if m <= 1 else None,
                               fr["m1"] if m % 2 == 0 else fr["a1"]],
                         inc=(k == 7))
            wfree[i] = mmx
            m_rl = vop(lambda e_, m=m, pp=pp: e_.tensor_scalar(
                arC[:, 4608 + (m % 2) * SL:4608 + (m % 2 + 1) * SL], pp[:, 0:SL],
                0.0, None, ALU.max), deps=[mmx, m_k2.get(m - 2)], settle=True)
            fr["m1" if m % 2 == 0 else "a1"] = m_rl
            m_k2[m] = act(arA[:, m * SL:(m + 1) * SL],
                          arC[:, 4608 + (m % 2) * SL:4608 + (m % 2 + 1) * SL],
                          AF.Square, deps=[m_rl], settle=True)
        for m in range(8):
            i = wload_next(wc_l0.ap()[2, :, m * 1024:(m + 1) * 1024])
            pp = p_m0 if m % 2 == 0 else p_a0
            mmx = None
            for k in range(8):
                mmx = mm(pp[:, 0:SL], wtile(i, k), arA[:, k * SL:(k + 1) * SL],
                         k == 0, k == 7,
                         deps=[wload[i], m_k2[k],
                               fr["m0"] if m % 2 == 0 else fr["a0"]],
                         inc=(k == 7))
            wfree[i] = mmx
            m_t = vop(lambda e_, m=m, pp=pp: e_.tensor_tensor(
                scr_f[:, (m % 2) * CW:(m % 2) * CW + SL], pp[:, 0:SL],
                arC[:, 512 + m * SL:512 + (m + 1) * SL], ALU.mult), deps=[mmx])
            fr["m0" if m % 2 == 0 else "a0"] = m_t
            m_xvadd = vop(lambda e_, m=m, c=c: e_.tensor_tensor(
                xsl(m, c, 1, SL), xsl(m, c, 1, SL),
                scr_f[:, (m % 2) * CW:(m % 2) * CW + SL], ALU.add))
        scr_guard[0] = m_xvadd
    if upto <= 4:
        ctx_close_out(P, nc, ctx, out, x_ar, xsl, dbg_t, dma, vop, act, locals())
        return nc

    P.barrier()
    m_so = dma("sync", scr_f[0:32, 0:128], sum_out.ap(),
               deps=[m_cc1, scr_guard[0], m_sdma])
    m_str2 = trans(p_m0[:, 0:32], scr_f[0:32, 0:128], ident_fs[0:32, 0:32],
                   deps=[m_so, fr["m0"]])
    m_str2 = pesettle()
    m_sev2 = vop(lambda e_: e_.tensor_copy(sum_sb[:, 32:64], p_m0[:, 0:32]),
                 deps=[m_str2])
    fr["m0"] = m_sev2
    m_ro = None
    for g2 in range(2):
        for r in range(4):
            for s2 in range(NCH):
                gch = chunks_of(r)[s2]
                src = 32 + r * 8 + s2 * 2 + g2
                m_ro = vop(lambda e_, g2=g2, gch=gch, src=src: e_.tensor_copy(
                    sum_sb[:, (0 if g2 == 0 else 16) + gch:(0 if g2 == 0 else 16) + gch + 1],
                    sum_sb[:, src:src + 1]), deps=[m_sev2],
                    inc=(g2 == 1 and r == 3 and s2 == NCH - 1))
    m_isc = []
    for g2 in range(2):
        dcol = sum_sb[:, 64 + g2:65 + g2]
        d0 = bass.AP(dcol.tensor, dcol.offset, [[list(dcol.ap)[0][0], 128], [0, 16]])
        m_isc.append(vop(lambda e_, g2=g2, d0=d0: e_.tensor_tensor_scan(
            sum_sb[:, 32 + g2 * 16:48 + g2 * 16], d0,
            sum_sb[:, g2 * 16:16 + g2 * 16], 0.0, ALU.mult, ALU.add),
            deps=[m_ro, m_d512], settle=True))
    m_bo = [dma("gpsimd", bounce.ap()[:, g2 * 17 + 1:g2 * 17 + 17],
                sum_sb[:, 32 + g2 * 16:48 + g2 * 16], deps=[m_isc[g2]])
            for g2 in range(2)]
    m_bo = [dma("gpsimd", bounce.ap()[:, g2 * 17 + 1:g2 * 17 + 17],
                sum_sb[:, 32 + g2 * 16:48 + g2 * 16], deps=[m_bo[g2]])
            for g2 in range(2)]
    m_pick = {}
    for g2 in range(2):
        for c in range(NCH):
            idx = g2 * NCH + c
            P.emit("gpsimd", lambda e_, idx=idx: e_.reg_load(
                rS, spick_s[0:1, idx:idx + 1]), deps=[m_bo[g2], m_vec])
            m_pick[idx] = dma("gpsimd", sin_sb[:, idx:idx + 1],
                              bass.AP(bounce, rS, [[34, 128], [1, 1]]), slow=True)
    m_cor = {}
    for g2 in range(2):
        for c in range(NCH):
            idx = g2 * NCH + c
            m_cor[(c, g2)] = vop(lambda e_, g2=g2, c=c, idx=idx: e_.scalar_tensor_tensor(
                st_ar[:, (g2 * NCH + c) * SL:(g2 * NCH + c + 1) * SL],
                sh8[:, 1024 + g2 * SL:1024 + (g2 + 1) * SL],
                sin_sb[:, idx:idx + 1],
                st_ar[:, (g2 * NCH + c) * SL:(g2 * NCH + c + 1) * SL],
                ALU.mult, ALU.add), deps=[m_pick[idx], m_dt[g2]])
    for c in range(NCH):
        m_cast = None
        for g2 in range(2):
            m_cast = vop(lambda e_, g2=g2, c=c: e_.tensor_copy(
                arA[:, g2 * SL:(g2 + 1) * SL],
                st_ar[:, (g2 * NCH + c) * SL:(g2 * NCH + c + 1) * SL]),
                deps=[m_cor[(c, 0)], m_cor[(c, 1)]], inc=(g2 == 1),
                settle=(g2 == 1))
        for m in range(8):
            pp = p_m0 if m % 2 == 0 else p_a0
            mmx = None
            for n2 in range(2):
                mmx = mm(pp[:, 0:SL],
                         w_out_sb[:, n2 * 1024 + m * 128:n2 * 1024 + (m + 1) * 128],
                         arA[:, n2 * SL:(n2 + 1) * SL], n2 == 0, n2 == 1,
                         deps=[m_cast, fr["m0"] if m % 2 == 0 else fr["a0"]],
                         inc=(n2 == 1))
            m_ya = vop(lambda e_, m=m, c=c, pp=pp: e_.tensor_tensor(
                xsl(m, c, 1, SL), xsl(m, c, 1, SL), pp[:, 0:SL], ALU.add),
                deps=[mmx])
            fr["m0" if m % 2 == 0 else "a0"] = m_ya
    if upto <= 5:
        ctx_close_out(P, nc, ctx, out, x_ar, xsl, dbg_t, dma, vop, act, locals())
        return nc

    P.barrier()
    xn1_ms = emit_layernorm()

    P.barrier()
    fr = dict(a0=None, a1=None, m0=None, m1=None, ropeB=None)
    m_wq = {}
    m_kv_w = []
    m_ones = P.emit("gpsimd", lambda e_: e_.memset(arA[:, 2048:2568], 1.0),
                    inc="gc")
    fr["vdma0"] = fr["vdma1"] = m_ones
    for hp in range(NHP):
        sl_ = hp % 2
        m_wq[hp] = dma("sync", arC[:, sl_ * 3072:(sl_ + 1) * 3072],
                       wc_qkv.ap()[hp, :, :], deps=[m_wq.get(hp - 2), m_prepq],
                       sem="wq")
        for c in range(NCH):
            m_rope = dma("sync", arB[:, 2048:2560], ropec.ap()[c, :, :],
                         deps=[fr["ropeB"]], sem="wr")
            m_rope2 = dma("sync", arB[:, 2560:3072], ropes.ap()[c, :, :],
                          sem="wr")
            m_q = None
            m_k = None
            for g in range(NG):
                m_q = mm(p_a0[:, 0:SL],
                         arC[:, sl_ * 3072 + g * 384:sl_ * 3072 + g * 384 + 128],
                         xnsl(g, c, 1, SL), g == 0, g == NG - 1,
                         deps=[m_wq[hp], xn1_ms[c], fr["a0"]], inc=(g == NG - 1))
            for g in range(NG):
                m_k = mm(p_a1[:, 0:SL],
                         arC[:, sl_ * 3072 + g * 384 + 128:sl_ * 3072 + g * 384 + 256],
                         xnsl(g, c, 1, SL), g == 0, g == NG - 1,
                         deps=[m_wq[hp], xn1_ms[c], fr["a1"]], inc=(g == NG - 1))
            for qk in range(2):
                pp = p_a0 if qk == 0 else p_a1
                m_src = m_q if qk == 0 else m_k
                bias_c = vec_sb[:, (36 if qk == 0 else 44) + hp:
                                (37 if qk == 0 else 45) + hp]
                m_ev = act(arA[:, qk * 1024:qk * 1024 + SL], pp[:, 0:SL], AF.Identity,
                           deps=[m_src, fr.get("qdma" if qk == 0 else "kdma")],
                           bias=bias_c, settle=True)
                fr["a0" if qk == 0 else "a1"] = m_ev
                m_pm = mm(p_m0[:, 0:SL], pswap_s[:, :], arA[:, qk * 1024:qk * 1024 + SL],
                          True, True, deps=[m_ev, fr["m0"]], inc=True)
                m_t1 = vop(lambda e_, qk=qk: e_.tensor_tensor(
                    arA[:, qk * 1024 + 512:qk * 1024 + 1024],
                    arA[:, qk * 1024:qk * 1024 + 512], arB[:, 2048:2560], ALU.mult),
                    deps=[m_ev, m_rope, m_rope2])
                m_t2 = vop(lambda e_, qk=qk: e_.tensor_tensor(
                    arA[:, qk * 1024:qk * 1024 + 512], p_m0[:, 0:SL],
                    arB[:, 2560:3072], ALU.mult), deps=[m_pm])
                fr["m0"] = m_t2
                m_rr = vop(lambda e_, qk=qk: e_.tensor_tensor(
                    arA[:, qk * 1024:qk * 1024 + 512],
                    arA[:, qk * 1024:qk * 1024 + 512],
                    arA[:, qk * 1024 + 512:qk * 1024 + 1024], ALU.add),
                    deps=[m_t1, m_t2], settle=True)
                fr["ropeB"] = m_rr
                if qk == 0:
                    fr["qdma"] = dma("sync", qcache.ap()[hp, c, :, :],
                                     arA[:, 0:512], deps=[m_rr], sem="qw")
                    m_kv_w.append(fr["qdma"])
                else:
                    fr["kdma"] = dma(
                        "sync",
                        kv_kin.ap()[(hp * NCH + c) * 128:(hp * NCH + c + 1) * 128, :],
                        arA[:, 1024:1536], deps=[m_rr], sem="kw")
                    m_kv_w.append(fr["kdma"])
            m_ve2 = {}
            for tb in range(4):
                m_v = None
                for g in range(NG):
                    m_v = mm(p_m1[:, 0:128],
                             xnsl(g, c, 1 + tb * 128, 128),
                             arC[:, sl_ * 3072 + g * 384 + 256:sl_ * 3072 + g * 384 + 384],
                             g == 0, False,
                             deps=[m_wq[hp], xn1_ms[c], fr["m1"]], inc=False)
                m_v = mm(p_m1[:, 0:128], ones_bb[:, :],
                         vrow_b[:, hp * 128:(hp + 1) * 128], False, True,
                         deps=[m_bqv], inc=True)
                for h in range(2):
                    m_ve2[(tb, h)] = act(
                        arA[:, 2048 + h * 260 + tb * 65:2048 + h * 260 + tb * 65 + 64],
                        p_m1[:, h * 64:(h + 1) * 64], AF.Copy,
                        deps=[m_v, fr.get(f"vdma{h}")], settle=True)
                fr["m1"] = m_ve2[(tb, 1)]
            for h in range(2):
                fr[f"vdma{h}"] = dma(
                    "gpsimd",
                    kv_vin.ap()[((2 * hp + h) * NCH + c) * 128:
                                ((2 * hp + h) * NCH + c + 1) * 128, :],
                    arA[:, 2048 + h * 260:2048 + (h + 1) * 260],
                    deps=[m_ve2[(3, h)]], sem="vw")
                m_kv_w.append(fr[f"vdma{h}"])
    m_cc2 = P.emit("gpsimd", lambda e_: e_.collective_compute(
        "AllGather", ALU.bypass, replica_groups=[[0, 1, 2, 3], [4, 5, 6, 7]],
        ins=[kv_kin.ap().opt()], outs=[kv_kall.ap().opt()]),
        deps=m_kv_w, inc="cc")
    m_cc3 = P.emit("gpsimd", lambda e_: e_.collective_compute(
        "AllGather", ALU.bypass, replica_groups=[[0, 1, 2, 3], [4, 5, 6, 7]],
        ins=[kv_vin.ap().opt()], outs=[kv_vall.ap().opt()]),
        inc="cc")
    if upto <= 6:
        ctx_close_out(P, nc, ctx, out, x_ar, xsl, dbg_t, dma, vop, act, locals())
        return nc

    P.barrier()
    fr = dict(a0=None, a1=None, m0=None, m1=None)
    prev_cv = {}
    prev_mh = {}
    prev_ev = {}
    for hg in range(24):
        ping = hg % 2
        stgv = (stg[:, :].bitcast(I32) if ping == 0 else arB[:, :].bitcast(I32))
        m_lg = dma("sync", stgv[:, 0:1024], wg.ap()[hg * 128:(hg + 1) * 128, :],
                   deps=[prev_cv.get(hg - 2)], sem="gl")
        m_lu = dma("gpsimd", stgv[:, 1024:2048], wu.ap()[hg * 128:(hg + 1) * 128, :],
                   deps=[prev_cv.get(hg - 2)], sem="glv")
        m_cv = act(arA[:, ping * 2052:ping * 2052 + 2048], stgv[:, 0:2048],
                   AF.Copy, deps=[m_lg, m_lu, prev_mh.get(hg - 2)],
                   scale=1.0 / 127.0, settle=True)
        prev_cv[hg] = m_cv
        for rnd in range(2):
            pptI = p_m0 if rnd == 0 else p_m1
            m_tr = None
            for qq in range(8):
                q = rnd * 8 + qq
                m_tr = trans(
                    pptI[:, :].bitcast(BF16)[:, qq * 128:(qq + 1) * 128],
                    arA[:, ping * 2052 + q * 128:ping * 2052 + (q + 1) * 128],
                    ident_bs[:, :],
                    deps=[m_cv, prev_ev.get((hg - 1, rnd))])
            m_tr = pesettle()
            prev_ev[(hg, rnd)] = None
            for qq in range(8):
                q = rnd * 8 + qq
                last = (qq == 7)
                prev_ev[(hg, rnd)] = vop(lambda e_, q=q, qq=qq, ping=ping,
                                         pptI=pptI: e_.tensor_copy(
                    arC[:, ping * 2048 + q * 128:ping * 2048 + (q + 1) * 128],
                    pptI[:, :].bitcast(BF16)[:, qq * 128:(qq + 1) * 128]),
                    deps=[m_tr], inc=last, settle=last)
        m_w = prev_ev[(hg, 1)]
        for c in range(NCH):
            pg = p_a0[:, 0:SL] if c % 2 == 0 else p_st[:, 0:SL]
            pu = p_a1[:, 0:SL] if c % 2 == 0 else p_st[:, SL:2 * SL]
            gk, uk = f"g{c % 2}", f"u{c % 2}"
            m_g = None
            m_uu = None
            for g in range(NG):
                m_g = mm(pg, arC[:, ping * 2048 + g * 128:ping * 2048 + (g + 1) * 128],
                         xnsl(g, c, 1, SL), g == 0, g == NG - 1,
                         deps=[m_w, fr.get(gk)], inc=(g == NG - 1))
            for g in range(NG):
                m_uu = mm(pu,
                          arC[:, ping * 2048 + 1024 + g * 128:ping * 2048 + 1024 + (g + 1) * 128],
                          xnsl(g, c, 1, SL), g == 0, g == NG - 1,
                          deps=[m_w, fr.get(uk)], inc=(g == NG - 1))
            m_si = act(arA[:, ping * 2052 + (c % 2) * 512:ping * 2052 + (c % 2) * 512 + 512],
                       pg, AF.Silu, deps=[m_g, fr.get(f"mh{c % 2}")], settle=True)
            fr[gk] = m_si
            hsl = wbufB[:, c * 512:(c + 1) * 512]
            m_h = vop(lambda e_, hsl=hsl, ping=ping, c=c, pu=pu: e_.tensor_tensor(
                hsl, pu,
                arA[:, ping * 2052 + (c % 2) * 512:ping * 2052 + (c % 2) * 512 + 512],
                ALU.mult), deps=[m_si, m_uu, fr.get(f"hdma{c}")], settle=True)
            fr[uk] = m_h
            fr[f"mh{c % 2}"] = m_h
            m_use = dma("scalar", hcache.ap()[c, :, hg * SL:(hg + 1) * SL],
                        hsl, deps=[m_h], sem=f"hw{c}")
            fr[f"hdma{c}"] = m_use
        prev_mh[hg] = m_h
    if upto <= 7:
        ctx_close_out(P, nc, ctx, out, x_ar, xsl, dbg_t, dma, vop, act, locals())
        return nc

    P.barrier()
    m_mask = dma("sync", sh8[:, 0:2048], maskadd.ap())
    st_b = st_ar[:, :].bitcast(BF16)
    stg_b = stg[:, :].bitcast(BF16)
    kw_dep = [("pc2", 0), ("pc3", 0), ("pw", 0), ("gl", 0),
              ("hw0", 0), ("hw1", 0), ("hw2", 0), ("hw3", 0)]
    m_kw = dma("scalar", xn_ar[:, 0:12288], wc_wd.ap()[0, :, :], deps=kw_dep,
               sem="wd")
    dma("scalar", st_b[:, 0:8192], wc_wd.ap()[1, :, 0:8192], sem="wd")
    dma("scalar", stg_b[:, 0:4096], wc_wd.ap()[1, :, 8192:12288], sem="wd")
    dma("scalar", wbufB[:, 0:3072], wc_wo.ap()[0, :, 0:3072], sem="wd")
    dma("scalar", xn_ar[:, 12288:13312], wc_wo.ap()[0, :, 3072:4096], sem="wd")
    dma("scalar", stg_b[:, 4096:6144], wc_wo.ap()[1, :, 0:2048], sem="wd")
    m_kw = dma("scalar", xn_ar[:, 13312:15360], wc_wo.ap()[1, :, 2048:4096],
               sem="wd")
    m_jt = dma("sync", jot[:, :], jof.ap(), sem="jt")
    m_jt = dma("sync", tbb[:, :], tb2.ap(), sem="jt")
    qbufs = [qbuf, qbuf2]
    seq = []
    jb0 = 0
    for (s, hp, njobs) in GROUPS:
        for h in range(2):
            seq.append((s, hp, h, njobs, jb0))
        jb0 += njobs
    kb_use = [None, None]
    vb_use = [None, None]
    pb_use = [None, None]
    arB_use = None
    rrow_use = None
    exp_prev = [None, None]
    lastqk_item = {}
    acc_free = {0: None, 1: None}
    qload = {}

    def emit_qload(i):
        si, hpi, hi, _, _ = seq[i]
        qb_ = qbufs[i % 2]
        dma("sync", qb_[0:64, :], qcache.ap()[hpi, si, 64 * hi:64 * hi + 64, :],
            deps=[lastqk_item.get(i - 2)], sem=f"qb{i % 2}")
        return dma("sync", qb_[67:70, :], qaug.ap()[hpi, si, hi, :, :],
                   sem=f"qb{i % 2}")

    qload[0] = emit_qload(0)
    qload[1] = emit_qload(1)
    for i, (s, hp, h, njobs, jb0) in enumerate(seq):
        hh = 2 * hp + h
        pacc = p_a0 if h == 0 else p_a1
        qb_ = qbufs[i % 2]
        m_qload = qload[i]
        m_b = [dma("sync", kbuf[ii][64:67, :], beta.ap()[hh, :, :],
                   deps=[kb_use[ii]], sem="bb") for ii in range(2)]
        m_pv = None
        for jb in range(njobs):
            jidx = jb0 + jb
            bs = jb % 2
            m_k = P.emit("gpsimd", lambda e_, bs=bs, col=4 * jidx + h:
                         e_.indirect_dma_start(
                kbuf[bs][0:64, :], None, kv_kall.ap()[0:64, :],
                bass.IndirectOffsetOnAxis(ap=jot[0:64, col:col + 1], axis=0)),
                deps=[m_jt, kb_use[bs], ("cc", P.cnt["cc"])],
                inc="gd", incn=16)
            m_vg = P.emit("gpsimd", lambda e_, bs=bs, col=4 * jidx + 2 + h:
                          e_.indirect_dma_start(
                vbuf[bs][:, :], None, kv_vall.ap()[0:128, :],
                bass.IndirectOffsetOnAxis(ap=jot[:, col:col + 1], axis=0)),
                deps=[m_jt, vb_use[bs], ("cc", P.cnt["cc"])],
                inc="gd", incn=16)
            m_qkh = [None, None]
            m_qke = [None] * 4
            for half in range(2):
                for kb in ((0, 1) if half == 0 else (2, 3)):
                    m_qke[kb] = mm(p_st[:, kb * SL:(kb + 1) * SL],
                                   kbuf[bs][0:128, kb * 128:(kb + 1) * 128],
                                   qb_[0:128, :], True, True,
                                   deps=[m_k, m_qload, m_b[bs], exp_prev[half]],
                                   inc=True)
                m_qkh[half] = m_qke[2 * half + 1]
            kb_use[bs] = m_qkh[1]
            lastqk_item[i] = m_qkh[1]
            m_mah = [None] * 4
            if jb == 0:
                for kb in range(4):
                    m_mah[kb] = vop(lambda e_, kb=kb: e_.tensor_tensor(
                        p_st[:, kb * SL:(kb + 1) * SL],
                        p_st[:, kb * SL:(kb + 1) * SL],
                        sh8[:, kb * SL:(kb + 1) * SL], ALU.add),
                        deps=[m_qke[kb], m_mask], settle=True)
            m_exph = [None, None]
            for half in range(2):
                m_exph[half] = act(pbuf[bs][:, half * 1024:(half + 1) * 1024],
                                   p_st[:, half * 1024:(half + 1) * 1024], AF.Exp,
                                   deps=[m_qkh[half], m_mah[2 * half],
                                         m_mah[2 * half + 1], pb_use[bs], m_jt],
                                   bias=tbb[:, jidx * 2 + h:jidx * 2 + h + 1],
                                   scale=SC, settle=True)
                exp_prev[half] = m_exph[half]
            for half in range(2):
                for kb in ((0, 1) if half == 0 else (2, 3)):
                    m_pv = mm(pacc[0:65, 0:SL], vbuf[bs][:, kb * 65:(kb + 1) * 65],
                              pbuf[bs][:, kb * SL:(kb + 1) * SL],
                              (jb == 0 and kb == 0), (jb == njobs - 1 and kb == 3),
                              deps=[m_exph[half], m_vg,
                                    acc_free[h] if jb == 0 else None],
                              inc=(kb == 3))
            vb_use[bs] = m_pv
            pb_use[bs] = m_pv
        if i + 2 < len(seq):
            qload[i + 2] = emit_qload(i + 2)
        m_sx = vop(lambda e_, pacc=pacc: e_.tensor_copy(
            scr_f[0:1, 514:514 + SL], pacc[64:65, 0:SL]), deps=[m_pv, rrow_use])
        m_rb = mm(p_m0[0:64, 0:SL], ones_f[0:1, 0:64], scr_f[0:1, 514:514 + SL],
                  True, True, deps=[m_sx, fr["m0"]], inc=True)
        rrow_use = m_rb
        m_rcb = vop(lambda e_: e_.reciprocal(
            scr_f[0:64, 0:SL], p_m0[0:64, 0:SL]), deps=[m_rb], settle=True)
        fr["m0"] = m_rcb
        m_at = vop(lambda e_, pacc=pacc: e_.tensor_tensor(
            arB[0:64, 0:SL], pacc[0:64, 0:SL], scr_f[0:64, 0:SL], ALU.mult),
            deps=[m_rcb, arB_use], settle=True)
        acc_free[h] = m_at
        arB_use = dma("sync", acache.ap()[hp, s, 64 * h:64 * h + 64, :],
                      arB[0:64, 0:SL], deps=[m_at], sem="aw")
    if upto <= 8:
        ctx_close_out(P, nc, ctx, out, x_ar, xsl, dbg_t, dma, vop, act, locals())
        return nc

    P.barrier()
    fr = dict(m0=None, m1=None)
    scr_b = scr_f[:, :].bitcast(BF16)

    def wd_t(mi, k):
        idx = (mi % 4) * 24 + k
        if mi < 4:
            return xn_ar[:, idx * 128:(idx + 1) * 128]
        if idx < 64:
            return st_b[:, idx * 128:(idx + 1) * 128]
        return stg_b[:, (idx - 64) * 128:(idx - 63) * 128]

    def wo_t(mi, hp):
        if mi < 3:
            return wbufB[:, mi * 1024 + hp * 128:mi * 1024 + (hp + 1) * 128]
        if mi == 3:
            return xn_ar[:, 12288 + hp * 128:12288 + (hp + 1) * 128]
        if mi < 6:
            return stg_b[:, 4096 + (mi - 4) * 1024 + hp * 128:
                         4096 + (mi - 4) * 1024 + (hp + 1) * 128]
        return xn_ar[:, 13312 + (mi - 6) * 1024 + hp * 128:
                     13312 + (mi - 6) * 1024 + (hp + 1) * 128]

    def hct(k):
        if k < 12:
            return arC[:, k * 512:(k + 1) * 512]
        if k < 20:
            return pbuf[(k - 12) // 4][:, ((k - 12) % 4) * 512:((k - 12) % 4 + 1) * 512]
        return [qbuf[:, 0:512], kbuf[0][:, 0:512], kbuf[1][:, 0:512],
                scr_b[:, 0:512]][k - 20]

    def act_t(hp):
        return arA[:, hp * 512:(hp + 1) * 512]

    m_last_mm = None
    m_hr = m_h2 = None
    for c in range(NCH):
        ex = ([("hw0", 0), ("hw1", 0), ("hw2", 0), ("hw3", 0), ("aw", 0)]
              if c == 0 else [])
        for hp in range(NHP):
            m_h2 = dma("gpsimd", act_t(hp), acache.ap()[hp, c, :, :],
                       deps=(ex + [m_last_mm]) if hp == 0 else (), sem="h2")
        for k in range(24):
            q, s = (("sync", "hr") if (k < 12 or k >= 20) else ("gpsimd", "h2"))
            m = dma(q, hct(k), hcache.ap()[c, :, k * SL:(k + 1) * SL],
                    deps=(ex + [m_last_mm]) if k in (0, 12) else (), sem=s)
            if s == "hr":
                m_hr = m
            else:
                m_h2 = m
        for mi in range(8):
            pp = p_m0 if mi % 2 == 0 else p_m1
            frk = "m0" if mi % 2 == 0 else "m1"
            mmx = None
            for hp in range(NHP):
                mmx = mm(pp[:, 0:SL], wo_t(mi, hp), act_t(hp),
                         hp == 0, False,
                         deps=[m_h2, m_kw, fr[frk]], inc=False)
            for k in range(24):
                mmx = mm(pp[:, 0:SL], wd_t(mi, k), hct(k), False, k == 23,
                         deps=([m_hr, m_h2] if k == 0 else ()), inc=(k == 23))
            m_x2 = vop(lambda e_, mi=mi, c=c, pp=pp: e_.scalar_tensor_tensor(
                xsl(mi, c, 1, SL), pp[:, 0:SL],
                vec_sb[:, 52 + mi:53 + mi],
                xsl(mi, c, 1, SL), ALU.add, ALU.add),
                deps=[mmx])
            fr[frk] = m_x2
            m_last_mm = mmx
    P.barrier()
    m_out = []
    oev = {}
    for it in range(NCH * 4):
        c, tb = divmod(it, 4)
        half = it % 2
        m_tr3 = None
        for g in range(NG):
            m_tr3 = trans(p_st[:, half * 1024 + g * 128:half * 1024 + (g + 1) * 128],
                          xsl(g, c, 1 + tb * 128, 128), ident_fs[:, :],
                          deps=[oev.get(it - 2)])
        m_tr3 = pesettle()
        m_cp = None
        for g in range(NG):
            m_cp = vop(lambda e_, g=g, half=half: e_.tensor_copy(
                stg[:, half * 1024 + g * 128:half * 1024 + (g + 1) * 128],
                p_st[:, half * 1024 + g * 128:half * 1024 + (g + 1) * 128]),
                deps=[m_tr3, m_out[it - 2] if it >= 2 else None], inc=(g == NG - 1),
                settle=(g == NG - 1))
        oev[it] = m_cp
        m_out.append(dma(["sync", "gpsimd"][half], out.ap()[c, tb * 128:(tb + 1) * 128, :],
                         stg[:, half * 1024:half * 1024 + 1024], deps=[m_cp],
                         sem=f"ow{half}"))
    P.emit("sync", lambda e_: e_.nop(), deps=[m_out[-1], m_out[-2]])
    P.emit("gpsimd", lambda e_: e_.nop(), deps=[("cc", P.cnt["cc"])])
    if dbg_t:
        ctx_close_out(P, nc, ctx, out, x_ar, xsl, dbg_t, dma, vop, act, locals())
        return nc
    P.barrier()

    with nc.Block() as block:
        P.replay(block)
    ctx.close()
    return nc


def ctx_close_out(P, nc, ctx, out, x_ar, xsl, dbg_t, dma, vop, act, env):
    P.barrier()
    srcs = {"dbg_x": env["x_ar"], "dbg_xn": env["xn_ar"], "dbg_st": env["st_ar"],
            "dbg_row": env["row_ar"], "dbg_sum": env["sum_sb"],
            "dbg_arA": env["arA"], "dbg_arB": env["arB"], "dbg_arC": env["arC"],
            "dbg_scr": env["scr_f"], "dbg_sin": env["sin_sb"],
            "dbg_wi": env["w_in_sb"], "dbg_wo": env["w_out_sb"]}
    dram_srcs = {"dbg_ac": "acache", "dbg_h": "hcache", "dbg_q": "qcache",
                 "dbg_kall": "kv_kall", "dbg_vall": "kv_vall",
                 "dbg_kin": "kv_kin", "dbg_sout": "sum_out",
                 "dbg_wc0": "wc_l0", "dbg_wcd": "wc_wd",
                 "dbg_wco": "wc_wo", "dbg_wcq": "wc_qkv"}
    m_last = None
    if P.cnt.get("cc", 0):
        P.emit("sync", lambda e_: e_.nop(), deps=[("cc", P.cnt["cc"])])
    for nm, t in dbg_t.items():
        if nm == "dbg_u":
            continue
        if nm in dram_srcs:
            s_t = env[dram_srcs[nm]]
            m_last = dma("sync", t.ap().flatten(), s_t.ap().flatten())
        else:
            src = srcs[nm]
            w = t.ap().shape[1]
            m_last = dma("sync", t.ap(), src[:, 0:w])
    if m_last is not None:
        P.emit("sync", lambda e_: e_.nop(), deps=[m_last])
    if P.cnt.get("cc", 0):
        P.emit("gpsimd", lambda e_: e_.nop(), deps=[("cc", P.cnt["cc"])])
    P.barrier()
    with nc.Block() as block:
        P.replay(block)
    ctx.close()


def kernel(**inputs):
    shared = build_shared_consts()
    nc = build_program(upto=99)
    in_maps = [build_core_inputs(c, inputs, shared) for c in range(8)]
    res = run_bass_kernel_spmd(nc, in_maps, core_ids=list(range(8)))
    outp = np.zeros((2, 16 * SL, D), dtype=np.float32)
    for core in range(8):
        b, j = divmod(core, 4)
        for s, g in enumerate(chunks_of(j)):
            outp[b, g * SL:(g + 1) * SL, :] = res.results[core]["out"][s]
    return outp
